# version 08
# baseline (speedup 1.0000x reference)
import numpy as np
import ml_dtypes
from contextlib import ExitStack

import concourse.bass as bass
import concourse.mybir as mybir
from concourse.bass_utils import run_bass_kernel_spmd
from concourse.hw_specs import get_activation_tables

F32 = mybir.dt.float32
BF16 = mybir.dt.bfloat16
AF = mybir.ActivationFunctionType
ALU = mybir.AluOpType
AX = mybir.AxisListType.X

B, S, D = 4, 2048, 2048
TPC = 1024
NTC = 8
NDC = 16
NE = 8
EPS = 1e-5
BIG = 1.0e30

ROUNDS = []
for r in range(16):
    ROUNDS.append(dict(kind=("K", "V", "Q", "WO")[r // 4], piece=r, n=r % 4))
for n in range(4):
    for e in range(NE):
        ROUNDS.append(dict(kind="EXP", piece=16 + n * 8 + e, n=n, e=e))
    ROUNDS.append(dict(kind="BIAS", piece=None, n=n))
NR = len(ROUNDS)
NPIECE = 48
_DEBUG = False
_OPT_PREPASS = True
_OPT_LADDER = True


def build_nc():
    nc = bass.Bass("TRN2", target_bir_lowering=False)

    x_h = nc.dram_tensor("x", [TPC, D], F32, kind="ExternalInput").ap()
    w1_h = nc.dram_tensor("w1", [D, 6144], F32, kind="ExternalInput").ap()
    wo_h = nc.dram_tensor("wo", [D, D], F32, kind="ExternalInput").ap()
    ew_h = nc.dram_tensor("ew", [NE, D, D], F32, kind="ExternalInput").ap()
    b1_h = nc.dram_tensor("b1", [1, 6144], F32, kind="ExternalInput").ap()
    bo_h = nc.dram_tensor("bo", [1, D], F32, kind="ExternalInput").ap()
    gw_h = nc.dram_tensor("gw", [D, 8], F32, kind="ExternalInput").ap()
    eb_h = nc.dram_tensor("eb", [NE, D], F32, kind="ExternalInput").ap()
    cos_h = nc.dram_tensor("cosT", [TPC, 512], BF16, kind="ExternalInput").ap()
    sin_h = nc.dram_tensor("sinT", [TPC, 512], BF16, kind="ExternalInput").ap()
    eye16_h = nc.dram_tensor("eye16", [128, 128], BF16, kind="ExternalInput").ap()
    eye32_h = nc.dram_tensor("eye32", [128, 128], F32, kind="ExternalInput").ap()
    ones_h = nc.dram_tensor("ones16", [1, 128], BF16, kind="ExternalInput").ap()
    konst_h = nc.dram_tensor("konst", [128, 2], F32, kind="ExternalInput").ap()
    out_h = nc.dram_tensor("out", [TPC, D], F32, kind="ExternalOutput").ap()

    cc_in = nc.dram_tensor("cc_in", [128, 16], F32)
    cc_out = nc.dram_tensor("cc_out", [128, 16], F32)
    dbg = {}
    if _DEBUG:
        for nm, shp, dt_ in (("d_x1", [128, 16384], BF16),
                             ("d_qt", [128, 16384], BF16),
                             ("d_h2t", [128, 16384], BF16),
                             ("d_acc", [128, 4096], F32),
                             ("d_g", [128, 64], F32),
                             ("d_acc2", [128, 16], F32),
                             ("d_stats", [128, 64], F32),
                             ("d_gt", [8, 1024], BF16),
                             ("d_eb", [8, 2048], BF16),
                             ("d_lf", [128, 64], F32)):
            dbg[nm] = nc.dram_tensor(nm, shp, dt_, kind="ExternalOutput").ap()

    def hbm_piece(p):
        if p < 12:
            src = w1_h[:, p * 512:(p + 1) * 512]
        elif p < 16:
            src = wo_h[:, (p - 12) * 512:(p - 11) * 512]
        else:
            k = p - 16
            n, e = k // 8, k % 8
            src = ew_h[e, :, n * 512:(n + 1) * 512]
        return src.rearrange("(c p) n -> p c n", p=128)

    with ExitStack() as ctx:
        al = ctx.enter_context
        buf_a = al(nc.sbuf_tensor("buf_a", [128, 16384], BF16))
        buf_b = al(nc.sbuf_tensor("buf_b", [128, 16384], BF16))
        buf_u = al(nc.sbuf_tensor("buf_u", [128, 16384], BF16))
        rs = [al(nc.sbuf_tensor(f"ring{i}", [128, 8192], BF16)) for i in range(3)]
        buf_f = al(nc.sbuf_tensor("buf_f", [128, 4096], F32))
        htmp = [al(nc.sbuf_tensor(f"htmp{i}", [128, 2048], BF16)) for i in range(2)]
        sqtr = al(nc.sbuf_tensor("sqtr", [128, 2048], BF16))
        st_f = al(nc.sbuf_tensor("st_f", [128, 64], F32))
        l_f = al(nc.sbuf_tensor("l_f", [128, 64], F32))
        e1_f = al(nc.sbuf_tensor("e1_f", [128, 64], F32))
        gs_f = al(nc.sbuf_tensor("gs_f", [128, 32], F32))
        g_f = al(nc.sbuf_tensor("g_f", [128, 64], F32))
        gbf = al(nc.sbuf_tensor("gbf", [128, 64], BF16))
        gT_sb = al(nc.sbuf_tensor("gT_sb", [8, 1024], BF16))
        l8_sb = al(nc.sbuf_tensor("l8_sb", [8, 128], F32))
        a_fm = al(nc.sbuf_tensor("a_fm", [128, 16], F32))
        a_cc = al(nc.sbuf_tensor("a_cc", [128, 16], F32))
        b1_sb = al(nc.sbuf_tensor("b1_sb", [1, 6144], BF16))
        bo_sb = al(nc.sbuf_tensor("bo_sb", [1, 2048], BF16))
        gw_sb = al(nc.sbuf_tensor("gw_sb", [128, 128], BF16))
        eb_sb = al(nc.sbuf_tensor("eb_sb", [8, 2048], BF16))
        eye16 = al(nc.sbuf_tensor("eye16s", [128, 128], BF16))
        eye32 = al(nc.sbuf_tensor("eye32s", [128, 128], F32))
        ones_sb = al(nc.sbuf_tensor("ones_sb", [1, 128], BF16))
        konst_sb = al(nc.sbuf_tensor("konst_sb", [128, 2], F32))
        ps_m = al(nc.psum_tensor("ps_m", [128, 2048], F32))
        ps_t = [al(nc.psum_tensor(f"ps_t{i}", [128, 1024], BF16)) for i in range(2)]
        ps_x = [al(nc.psum_tensor(f"ps_x{i}", [128, 512], F32)) for i in range(2)]
        sm = {k: al(nc.semaphore(f"sem_{k}"))
              for k in ("t", "v", "s", "o", "c", "yc", "y0", "y1",
                        "yt0", "yt1", "w0", "w1", "w2", "wa")}
        block = al(nc.Block())

        def h1T(dc, tc):
            return buf_a[:, dc * 1024 + tc * 128: dc * 1024 + (tc + 1) * 128]

        def x1v(tc, c0, c1):
            return buf_a[:, tc * 2048 + c0: tc * 2048 + c1]

        def kall(tc, c0, c1):
            return buf_b[:, tc * 2048 + c0: tc * 2048 + c1]

        def qT(dc, tc):
            return buf_b[:, dc * 1024 + tc * 128: dc * 1024 + (tc + 1) * 128]

        def qTblk(dc):
            return buf_b[:, dc * 1024:(dc + 1) * 1024]

        def h2T(dc, tc):
            return buf_u[:, dc * 1024 + tc * 128: dc * 1024 + (tc + 1) * 128]

        def tab_cos(j):
            return buf_u[:, j * 1024: j * 1024 + 512]

        def tab_sin(j):
            return buf_u[:, j * 1024 + 512:(j + 1) * 1024]

        def qtileA(tc):
            return buf_u[:, 2048 + tc * 512: 2048 + (tc + 1) * 512]

        def qslot(j):
            return buf_u[:, 6144 + j * 512: 6144 + (j + 1) * 512]

        def ropeout(j):
            return buf_u[:, 7168 + j * 1024: 7168 + (j + 1) * 1024]

        def vslot(j):
            return buf_u[:, 9216 + j * 512: 9216 + (j + 1) * 512]

        prodslot = buf_u[:, 11264:11776]

        def qtmp(j):
            return buf_u[:, 12288 + j * 512: 12288 + (j + 1) * 512]
        apart = buf_f[:, 0:2048]

        def xtile(j):
            return buf_f[:, 2048 + j * 1024: 2048 + (j + 1) * 1024]

        def accv(tc):
            return buf_f[:, tc * 512:(tc + 1) * 512]

        def stc(tc, k):
            return st_f[:, tc * 8 + k: tc * 8 + k + 1]

        def bank(g):
            return ps_m[:, (g % 4) * 512:((g % 4) + 1) * 512]

        q = {k: [] for k in ("t", "v", "s", "y", "g")}
        ct = {k: 0 for k in ("t", "v", "s", "o", "c", "yc", "y0", "y1",
                             "yt0", "yt1", "w0", "w1", "w2", "wa")}
        ENG = {"t": nc.tensor, "v": nc.vector, "s": nc.scalar,
               "y": nc.sync, "g": nc.gpsimd}

        def emit(e_, fn):
            q[e_].append(fn)

        wmax = {}

        def wait(e_, semk, val):
            wmax[semk] = max(wmax.get(semk, 0), val)
            emit(e_, lambda eng=ENG[e_], s_=sm[semk], v_=val: eng.wait_ge(s_, v_))

        def inc(semk, amt=1):
            ct[semk] += amt
            return ct[semk]

        Gseq = []
        mm_led = {}
        evac_led = {}
        tb_led = []
        tb_evac = []
        h1T_rdy = {}
        h1half = {}
        ln1 = {}
        ropeK_done = {}
        ropeQ_done = {}
        tabK = {}
        tabQ = {}
        rn_led = {}
        vprod_led = {}
        prod_cnt = [0]
        qslot_rel = {}
        q9tr = {}
        q9ev = {}
        qtr_ev = {}
        fold_led = {}
        red_done_led = {}
        x1ev = {}
        x2load = {}
        xh_led = {}
        a_tr_led = [0]
        h2b_led = {}
        h2T_rdy = {}
        logit_led = {}
        l8_led = {}
        ltr_led = {}
        lf_led = {}
        exp_led = {}
        gate_led = {}
        gT_led = {}
        out_led = {}
        w_led = {}
        a_dma_y = [None]
        acc_back_w = [None]
        piece_round = {}
        for r_, rd_ in enumerate(ROUNDS):
            if rd_["piece"] is not None:
                piece_round[rd_["piece"]] = r_

        def gk(r, tc):
            return r * 1000 + tc

        tabsets = get_activation_tables(nc.m.arch)
        set_id = list(tabsets).index("natural_log_exp_and_others")
        emit("s", lambda: nc.scalar.add_instruction(
            mybir.InstLoadActFuncSet(
                name=nc.get_next_instruction_name(),
                act_func_set_id=set_id, ins=[], outs=[])))

        nc.const_aps.register(F32, EPS, konst_sb[:, 0:1])
        nc.const_aps.register(F32, 1.0, konst_sb[:, 1:2])
        for dst, src in ((eye16[:, :], eye16_h), (eye32[:, :], eye32_h),
                         (ones_sb[:, :], ones_h), (konst_sb[:, :], konst_h)):
            inc("yc", 16)
            emit("y", lambda d=dst, s_=src:
                 nc.sync.dma_start(out=d, in_=s_).then_inc(sm["yc"], 16))
        const_y = ct["yc"]
        wait("s", "yc", const_y)
        for dst, src in (
                (b1_sb[:, :], b1_h), (bo_sb[:, :], bo_h),
                (gw_sb[:, :].rearrange("p (c n) -> p c n", n=8),
                 gw_h.rearrange("(c p) n -> p c n", p=128)),
                (eb_sb[:, :], eb_h)):
            inc("wa", 16)
            emit("g", lambda d=dst, s_=src:
                 nc.gpsimd.dma_start(out=d, in_=s_).then_inc(sm["wa"], 16))
        const_w = ct["wa"]

        emit("v", lambda: nc.vector.memset(apart, 0.0))

        def prepass_head(tc):
            for h in range(2):
                k = 2 * tc + h
                if k >= 2:
                    wait("y", "v", 2 * (tc - 1) + h + 1)
                    wait("y", "s", ln1[tc - 1])
                src = x_h[tc * 128:(tc + 1) * 128, h * 1024:(h + 1) * 1024]
                ys = f"y{k % 2}"
                inc(ys, 16)
                emit("y", lambda d=xtile(k % 2), s_=src, m_=sm[ys]:
                     nc.sync.dma_start(out=d, in_=s_).then_inc(m_, 16))
                wait("s", ys, ct[ys])
                xt = xtile(k % 2)
                emit("s", lambda o=sqtr[:, 0:1024], i=xt, a=stc(tc, 2 * h):
                     nc.scalar.activation(o, i, AF.Copy, accum_out=a))
                emit("s", lambda o=sqtr[:, 0:1024], i=xt, a=stc(tc, 2 * h + 1):
                     nc.scalar.activation(o, i, AF.Square, accum_out=a))
            M, R, S1, S2 = stc(tc, 4), stc(tc, 5), stc(tc, 6), stc(tc, 7)
            t0, t1 = gs_f[:, 0:1], gs_f[:, 1:2]
            emit("s", lambda o=t0, i=stc(tc, 0), a=stc(tc, 2):
                 nc.scalar.add(o, i, a))
            emit("s", lambda o=M, i=t0: nc.scalar.mul(o, i, 1.0 / 2048.0))
            emit("s", lambda o=t1, i=M: nc.scalar.activation(o, i, AF.Square))
            emit("s", lambda o=t1, i=t1: nc.scalar.mul(o, i, -1.0))
            emit("s", lambda o=t0, i=stc(tc, 1), a=stc(tc, 3):
                 nc.scalar.add(o, i, a))
            emit("s", lambda o=t0, i=t0: nc.scalar.mul(o, i, 1.0 / 2048.0))
            emit("s", lambda o=t0, i=t0, a=t1: nc.scalar.add(o, i, a))
            emit("s", lambda o=t0, i=t0: nc.scalar.add(o, i, EPS))
            emit("s", lambda o=t0, i=t0: nc.scalar.activation(o, i, AF.Ln))
            emit("s", lambda o=R, i=t0:
                 nc.scalar.activation(o, i, AF.Exp, scale=-0.5))
            emit("s", lambda o=S1, i=R: nc.scalar.add(o, i, 1.0))
            emit("s", lambda o=S2, i=M, a=R:
                 nc.scalar.mul(o, i, a).then_inc(sm["s"], 1))
            ln1[tc] = inc("s")

        def prepass_tail(tc):
            M, R = stc(tc, 4), stc(tc, 5)
            if tc >= 2:
                wait("v", "t", tb_led[2 * (tc - 2) + 1])
            wait("v", "s", ln1[tc])
            for h in range(2):
                xt = xtile((2 * tc + h) % 2)
                dst = htmp[tc % 2][:, h * 1024:(h + 1) * 1024]
                emit("v", lambda o=dst, i=xt, m=M, r_=R:
                     nc.vector.tensor_scalar(o, i, m, r_, ALU.subtract, ALU.mult)
                     .then_inc(sm["v"], 1))
                h1half[tc, h] = inc("v")
                assert h1half[tc, h] == 2 * tc + h + 1, "x-load pacing broke"

            for b_ in range(2):
                gb = len(tb_led)
                if gb == 0:
                    wait("t", "yc", const_y)
                wait("t", "v", h1half[tc, b_])
                if gb >= 2:
                    wait("t", "s", tb_evac[gb - 2])
                pt = ps_t[gb % 2]
                for j in range(8):
                    src = htmp[tc % 2][:, (b_ * 8 + j) * 128:(b_ * 8 + j + 1) * 128]
                    last = j == 7
                    emit("t", lambda o=pt[:, j * 128:(j + 1) * 128], i=src,
                         id_=eye16[:, :], l=last:
                         (nc.tensor.transpose(o, i, id_).then_inc(sm["t"], 1)
                          if l else nc.tensor.transpose(o, i, id_)))
                tb_led.append(inc("t"))
                wait("s", "t", tb_led[gb])
                dstv = buf_a[:, :].rearrange("p (c x) -> p c x", x=1024)[
                    :, b_ * 8:(b_ + 1) * 8, tc * 128:(tc + 1) * 128]
                srcv = pt[:, :].rearrange("p (c x) -> p c x", x=128)
                emit("s", lambda o=dstv, i=srcv:
                     nc.scalar.copy(o, i).then_inc(sm["s"], 1))
                tb_evac.append(inc("s"))
            h1T_rdy[tc] = tb_evac[-1]

        if _OPT_PREPASS:
            for _tc in range(4):
                prepass_head(_tc)
            for _tc in range(4, NTC):
                prepass_tail(_tc - 4)
                prepass_head(_tc)
        else:
            for _tc in range(NTC):
                prepass_head(_tc)
                prepass_tail(_tc)

        def emit_tab(tc, ledger, hold_sem=None):
            j = tc % 2
            if hold_sem is not None:
                wait("y", hold_sem[0], hold_sem[1])
            ys = f"yt{j}"
            for hsrc, dst in ((cos_h, tab_cos(j)), (sin_h, tab_sin(j))):
                src = hsrc[tc * 128:(tc + 1) * 128, :]
                inc(ys, 16)
                emit("y", lambda d=dst, s_=src, m_=sm[ys]:
                     nc.sync.dma_start(out=d, in_=s_).then_inc(m_, 16))
            ledger[tc] = (ys, ct[ys])

        emit_tab(0, tabK)
        emit_tab(1, tabK)

        gp_state = [0]

        def emit_collective_trigger():
            wait("g", "yc", a_dma_y[0])
            emit("g", lambda: nc.gpsimd.collective_compute(
                "AllReduce", ALU.add,
                replica_groups=[[0, 1], [2, 3], [4, 5], [6, 7]],
                ins=[cc_in.ap()], outs=[cc_out.ap()]).then_inc(sm["c"], 1))
            inc("c")

        def emit_collective_fetch():
            wait("g", "c", ct["c"])
            inc("wa", 16)
            emit("g", lambda: nc.gpsimd.dma_start(out=a_cc[:, :], in_=cc_out[:, :])
                 .then_inc(sm["wa"], 16))
            acc_back_w[0] = ct["wa"]

        def pump_pieces(upto):
            while gp_state[0] <= upto and gp_state[0] < NPIECE:
                p = gp_state[0]
                if p >= 3:
                    rprev = piece_round[p - 3]
                    wait("g", "t", mm_led[gk(rprev, NTC - 1)])
                slot = rs[p % 3]
                ws = f"w{p % 3}"
                dstv = slot[:, :].rearrange("p (c n) -> p c n", n=512)
                inc(ws, 16)
                emit("g", lambda d=dstv, s_=hbm_piece(p), m_=sm[ws]:
                     nc.gpsimd.dma_start(out=d, in_=s_).then_inc(m_, 16))
                w_led[p] = (ws, ct[ws])
                gp_state[0] += 1
                if gp_state[0] == 10:
                    emit_collective_trigger()
                if gp_state[0] == 13:
                    emit_collective_fetch()

        pump_pieces(2)

        def emit_group(r, tc):
            rd = ROUNDS[r]
            G = gk(r, tc)
            gidx = len(Gseq)
            Gseq.append(G)
            bk = bank(gidx)
            if r == 0 and tc == 0:
                wait("t", "wa", const_w)
                wait("t", "yc", const_y)
            if rd["piece"] is not None and tc == 0:
                wait("t", *w_led[rd["piece"]])
            if gidx >= 4:
                semk, cnt = evac_led[Gseq[gidx - 4]]
                wait("t", semk, cnt)
            if r == 0:
                wait("t", "s", h1T_rdy[tc])
            if rd["kind"] == "EXP" and rd["e"] == 0 and rd["n"] == 0:
                wait("t", "s", h2T_rdy[tc])
            if rd["kind"] == "BIAS":
                wait("t", "s", gT_led[tc])
                emit("t", lambda o=bk, l=gT_sb[0:8, tc * 128:(tc + 1) * 128],
                     rh=eb_sb[0:8, rd["n"] * 512:(rd["n"] + 1) * 512]:
                     nc.tensor.matmul(o, l, rh, start=True, stop=True)
                     .then_inc(sm["t"], 1))
                mm_led[G] = inc("t")
                return gidx
            slot = rs[rd["piece"] % 3]
            started = False
            if rd["kind"] in ("K", "V", "Q", "WO"):
                brow = (b1_sb[0:1, rd["piece"] * 512:(rd["piece"] + 1) * 512]
                        if rd["kind"] != "WO"
                        else bo_sb[0:1, rd["n"] * 512:(rd["n"] + 1) * 512])
                emit("t", lambda o=bk, l=ones_sb[0:1, :], rh=brow:
                     nc.tensor.matmul(o, l, rh, start=True, stop=False))
                started = True
            for dc in range(NDC):
                if rd["kind"] == "WO":
                    if r == 12 and tc == 0:
                        wait("t", "v", fold_led[dc])
                    lhs = qT(dc, tc)
                elif rd["kind"] == "EXP":
                    lhs = h2T(dc, tc)
                else:
                    lhs = h1T(dc, tc)
                rh = slot[:, dc * 512:(dc + 1) * 512]
                st_ = (not started) and dc == 0
                sp = dc == NDC - 1
                emit("t", lambda o=bk, l=lhs, rr=rh, a=st_, b_=sp:
                     (nc.tensor.matmul(o, l, rr, start=a, stop=b_)
                      .then_inc(sm["t"], 1) if b_ else
                      nc.tensor.matmul(o, l, rr, start=a, stop=b_)))
            mm_led[G] = inc("t")
            return gidx

        def evac_K(r, tc, gidx):
            G = gk(r, tc)
            bk = bank(gidx)
            wait("s", "t", mm_led[G])
            emit("s", lambda o=kall(tc, r * 512, (r + 1) * 512), i=bk:
                 nc.scalar.copy(o, i).then_inc(sm["s"], 1))
            evac_led[G] = ("s", inc("s"))
            if r == 1:
                wait("v", "s", evac_led[G][1])
                wait("v", *tabK[tc])
                j = tc % 2
                Ev, Ov = kall(tc, 0, 512), kall(tc, 512, 1024)
                tE, tO = qtmp(2 * j), qtmp(2 * j + 1)
                emit("v", lambda o=tE, a=Ev, b_=tab_sin(j): nc.vector.tensor_mul(o, a, b_))
                emit("v", lambda o=tO, a=Ov, b_=tab_sin(j): nc.vector.tensor_mul(o, a, b_))
                emit("v", lambda o=Ev, a=Ev, b_=tab_cos(j): nc.vector.tensor_mul(o, a, b_))
                emit("v", lambda o=Ov, a=Ov, b_=tab_cos(j): nc.vector.tensor_mul(o, a, b_))
                emit("v", lambda o=Ev, a=Ev, b_=tO: nc.vector.tensor_sub(o, a, b_))
                emit("v", lambda o=Ov, a=Ov, b_=tE:
                     nc.vector.tensor_add(o, a, b_).then_inc(sm["v"], 1))
                ropeK_done[tc] = inc("v")
                if tc + 2 < NTC:
                    emit_tab(tc + 2, tabK, ("v", ropeK_done[tc]))
            if r == 3:
                wait("s", "v", ropeK_done[tc])
                ssq, rn = stc(tc, 0), stc(tc, 1)
                emit("s", lambda o=sqtr[:, :], i=kall(tc, 0, 2048), a=ssq:
                     nc.scalar.activation(o, i, AF.Square, accum_out=a))
                emit("s", lambda o=ssq, i=ssq: nc.scalar.activation(o, i, AF.Ln))
                emit("s", lambda o=rn, i=ssq:
                     nc.scalar.activation(o, i, AF.Exp, scale=-0.5)
                     .then_inc(sm["s"], 1))
                rn_led[tc] = inc("s")

        def evac_V(r, tc, gidx):
            G = gk(r, tc)
            bk = bank(gidx)
            vj = vslot(gidx % 2)
            wait("s", "t", mm_led[G])
            if gidx >= 2 and Gseq[gidx - 2] in vprod_led:
                wait("s", "v", vprod_led[Gseq[gidx - 2]])
            emit("s", lambda o=vj, i=bk, sc=stc(tc, 1):
                 nc.scalar.mul(o, i, sc).then_inc(sm["s"], 1))
            evac_led[G] = ("s", inc("s"))
            c0 = (r - 4) * 512
            wait("v", "s", evac_led[G][1])
            emit("v", lambda o=prodslot, a=kall(tc, c0, c0 + 512), b_=vj:
                 nc.vector.tensor_mul(o, a, b_).then_inc(sm["v"], 1))
            pcnt = inc("v")
            wait("v", "v", pcnt)
            emit("v", lambda o=apart[:, c0:c0 + 512], a=apart[:, c0:c0 + 512],
                 b_=prodslot:
                 nc.vector.tensor_add(o, a, b_).then_inc(sm["v"], 1))
            vprod_led[G] = inc("v")
            prod_cnt[0] = vprod_led[G]

        def evac_Q(r, tc, gidx):
            G = gk(r, tc)
            bk = bank(gidx)
            wait("s", "t", mm_led[G])
            if r == 8 and tc == 0:
                wait("s", "v", prod_cnt[0])
            if r == 8:
                dst = qtileA(tc)
            else:
                if gidx >= 2:
                    prevG = Gseq[gidx - 2]
                    if prevG in qslot_rel:
                        semk, cnt = qslot_rel[prevG]
                        wait("s", semk, cnt)
                dst = qslot(gidx % 2)
            emit("s", lambda o=dst, i=bk: nc.scalar.copy(o, i).then_inc(sm["s"], 1))
            evac_led[G] = ("s", inc("s"))
            if r == 9:
                wait("v", "s", evac_led[G][1])
                wait("v", *tabQ[tc])
                if tc >= 2:
                    wait("v", "t", q9tr[tc - 2])
                j = tc % 2
                Ev, Ov = qtileA(tc), dst
                RO = ropeout(tc % 2)
                Eo, Oo = RO[:, 0:512], RO[:, 512:1024]
                sp1, sp2 = qtmp(2 * j), qtmp(2 * j + 1)
                emit("v", lambda o=sp1, a=Ev, b_=tab_sin(j): nc.vector.tensor_mul(o, a, b_))
                emit("v", lambda o=sp2, a=Ov, b_=tab_sin(j): nc.vector.tensor_mul(o, a, b_))
                emit("v", lambda o=Eo, a=Ev, b_=tab_cos(j): nc.vector.tensor_mul(o, a, b_))
                emit("v", lambda o=Oo, a=Ov, b_=tab_cos(j): nc.vector.tensor_mul(o, a, b_))
                emit("v", lambda o=Eo, a=Eo, b_=sp2: nc.vector.tensor_sub(o, a, b_))
                emit("v", lambda o=Oo, a=Oo, b_=sp1:
                     nc.vector.tensor_add(o, a, b_).then_inc(sm["v"], 1))
                ropeQ_done[tc] = inc("v")
                qslot_rel[G] = ("v", ropeQ_done[tc])
                if tc + 2 < NTC:
                    emit_tab(tc + 2, tabQ, ("v", ropeQ_done[tc]))

        def q_transposes(r, tc, gidx_of_group):
            gb = len(tb_led)
            pt = ps_t[gb % 2]
            if gb >= 2:
                wait("t", "s", tb_evac[gb - 2])
            if r == 9:
                wait("t", "v", ropeQ_done[tc])
                srcbase = ropeout(tc % 2)
                nchunks, dc0 = 8, 0
            else:
                G = gk(r, tc)
                wait("t", "s", evac_led[G][1])
                srcbase = qslot(gidx_of_group % 2)
                nchunks, dc0 = 4, 8 + (r - 10) * 4
            for j in range(nchunks):
                src = srcbase[:, j * 128:(j + 1) * 128]
                last = j == nchunks - 1
                emit("t", lambda o=pt[:, j * 128:(j + 1) * 128], i=src,
                     id_=eye16[:, :], l=last:
                     (nc.tensor.transpose(o, i, id_).then_inc(sm["t"], 1)
                      if l else nc.tensor.transpose(o, i, id_)))
            tb_led.append(inc("t"))
            if r == 9:
                q9tr[tc] = tb_led[-1]
            else:
                qslot_rel[gk(r, tc)] = ("t", tb_led[-1])
            wait("s", "t", tb_led[gb])
            dstv = buf_b[:, :].rearrange("p (c x) -> p c x", x=1024)[
                :, dc0:dc0 + nchunks, tc * 128:(tc + 1) * 128]
            srcv = pt[:, 0:nchunks * 128].rearrange("p (c x) -> p c x", x=128)
            emit("s", lambda o=dstv, i=srcv:
                 nc.scalar.copy(o, i).then_inc(sm["s"], 1))
            tb_evac.append(inc("s"))
            qtr_ev[r, tc] = tb_evac[-1]

        def evac_WO(r, tc, gidx):
            G = gk(r, tc)
            bk = bank(gidx)
            n = r - 12
            wait("v", "t", mm_led[G])
            xv = x1v(tc, n * 512, (n + 1) * 512)
            emit("v", lambda o=xv, a=bk, b_=xv:
                 nc.vector.tensor_add(o, a, b_).then_inc(sm["v"], 1))
            evac_led[G] = ("v", inc("v"))
            if r == 15:
                x1ev[tc] = evac_led[G][1]

        def evac_EXPBIAS(r, tc, gidx):
            G = gk(r, tc)
            rd = ROUNDS[r]
            bk = bank(gidx)
            n = rd["n"]
            acc = accv(tc)
            wait("v", "t", mm_led[G])
            if rd["kind"] == "BIAS":
                emit("v", lambda o=acc, a=bk, b_=acc:
                     nc.vector.tensor_add(o, a, b_).then_inc(sm["v"], 1))
                evac_led[G] = ("v", inc("v"))
                wait("y", "v", evac_led[G][1])
                dst = out_h[tc * 128:(tc + 1) * 128, n * 512:(n + 1) * 512]
                inc("o", 16)
                emit("y", lambda d=dst, s_=acc:
                     nc.sync.dma_start(out=d, in_=s_).then_inc(sm["o"], 16))
                out_led[n, tc] = ct["o"]
                return
            e = rd["e"]
            gsc = g_f[:, tc * 8 + e: tc * 8 + e + 1]
            if e == 0:
                wait("v", "v", gate_led[tc])
                if n > 0:
                    wait("v", "o", out_led[n - 1, tc])
                xv = x1v(tc, n * 512, (n + 1) * 512)
                emit("v", lambda o=acc, i=bk, sc=gsc, b_=xv:
                     nc.vector.scalar_tensor_tensor(o, i, sc, b_, ALU.mult,
                                                    ALU.add).then_inc(sm["v"], 1))
            else:
                emit("v", lambda o=acc, i=bk, sc=gsc, a=acc:
                     nc.vector.scalar_tensor_tensor(o, i, sc, a, ALU.mult,
                                                    ALU.add).then_inc(sm["v"], 1))
            evac_led[G] = ("v", inc("v"))

        def emit_a_reduce():
            for wv in range(4):
                px = ps_x[wv % 2]
                if wv >= 2:
                    wait("t", "v", red_done_led[wv - 2])
                for j in range(4):
                    c = wv * 4 + j
                    src = apart[:, c * 128:(c + 1) * 128]
                    last = j == 3
                    emit("t", lambda o=px[:, j * 128:(j + 1) * 128], i=src,
                         id_=eye32[:, :], l=last:
                         (nc.tensor.transpose(o, i, id_).then_inc(sm["t"], 1)
                          if l else nc.tensor.transpose(o, i, id_)))
                aw = inc("t")
                a_tr_led[0] = aw
                wait("v", "t", aw)
                for j in range(4):
                    c = wv * 4 + j
                    last = j == 3
                    emit("v", lambda o=a_fm[:, c:c + 1],
                         i=px[:, j * 128:(j + 1) * 128], l=last:
                         (nc.vector.reduce_sum(o, i, axis=AX).then_inc(sm["v"], 1)
                          if l else nc.vector.reduce_sum(o, i, axis=AX)))
                red_done_led[wv] = inc("v")
            wait("y", "v", red_done_led[3])
            inc("yc", 16)
            emit("y", lambda: nc.sync.dma_start(out=cc_in[:, :], in_=a_fm[:, :])
                 .then_inc(sm["yc"], 16))
            a_dma_y[0] = ct["yc"]

        def emit_p2a(tc):
            wait("s", "v", x1ev[tc])
            M2, R2 = stc(tc, 4), stc(tc, 5)
            sA, sB = stc(tc, 0), stc(tc, 1)
            t0, t1 = gs_f[:, 2:3], gs_f[:, 3:4]
            xv = x1v(tc, 0, 2048)
            emit("s", lambda o=sqtr[:, :], i=xv, a=sA:
                 nc.scalar.activation(o, i, AF.Copy, accum_out=a))
            emit("s", lambda o=sqtr[:, :], i=xv, a=sB:
                 nc.scalar.activation(o, i, AF.Square, accum_out=a))
            emit("s", lambda o=M2, i=sA: nc.scalar.mul(o, i, 1.0 / 2048.0))
            emit("s", lambda o=t1, i=M2: nc.scalar.activation(o, i, AF.Square))
            emit("s", lambda o=t1, i=t1: nc.scalar.mul(o, i, -1.0))
            emit("s", lambda o=t0, i=sB: nc.scalar.mul(o, i, 1.0 / 2048.0))
            emit("s", lambda o=t0, i=t0, a=t1: nc.scalar.add(o, i, a))
            emit("s", lambda o=t0, i=t0: nc.scalar.add(o, i, EPS))
            emit("s", lambda o=t0, i=t0: nc.scalar.activation(o, i, AF.Ln))
            emit("s", lambda o=R2, i=t0:
                 nc.scalar.activation(o, i, AF.Exp, scale=-0.5)
                 .then_inc(sm["s"], 1))
            ln2c = inc("s")
            if tc >= 2:
                wait("v", "t", h2b_led[tc - 2])
            else:
                wait("v", "t", tb_led[13 + 2 * tc])
            wait("v", "s", ln2c)
            emit("v", lambda o=htmp[tc % 2][:, :], i=x1v(tc, 0, 2048), m=M2, r_=R2:
                 nc.vector.tensor_scalar(o, i, m, r_, ALU.subtract, ALU.mult)
                 .then_inc(sm["v"], 1))
            h2ap = inc("v")
            for b_ in range(2):
                gb = len(tb_led)
                pt = ps_t[gb % 2]
                wait("t", "v", h2ap)
                if gb >= 2:
                    wait("t", "s", tb_evac[gb - 2])
                for j in range(8):
                    src = htmp[tc % 2][:, (b_ * 8 + j) * 128:(b_ * 8 + j + 1) * 128]
                    last = j == 7
                    emit("t", lambda o=pt[:, j * 128:(j + 1) * 128], i=src,
                         id_=eye16[:, :], l=last:
                         (nc.tensor.transpose(o, i, id_).then_inc(sm["t"], 1)
                          if l else nc.tensor.transpose(o, i, id_)))
                tb_led.append(inc("t"))
                wait("s", "t", tb_led[gb])
                dstv = buf_u[:, :].rearrange("p (c x) -> p c x", x=1024)[
                    :, b_ * 8:(b_ + 1) * 8, tc * 128:(tc + 1) * 128]
                srcv = pt[:, :].rearrange("p (c x) -> p c x", x=128)
                emit("s", lambda o=dstv, i=srcv:
                     nc.scalar.copy(o, i).then_inc(sm["s"], 1))
                tb_evac.append(inc("s"))
            h2b_led[tc] = tb_led[-1]
            h2T_rdy[tc] = tb_evac[-1]
            if tc == 0:
                wait("t", "v", red_done_led[3])
            else:
                wait("t", "s", l8_led[tc - 1])
            wait("t", "s", h2T_rdy[tc])
            for dc in range(NDC):
                st_, sp = dc == 0, dc == NDC - 1
                emit("t", lambda o=ps_x[0][0:8, 0:128],
                     l=gw_sb[:, dc * 8:(dc + 1) * 8],
                     rh=h2T(dc, tc), a=st_, b_=sp:
                     (nc.tensor.matmul(o, l, rh, start=a, stop=b_)
                      .then_inc(sm["t"], 1) if b_ else
                      nc.tensor.matmul(o, l, rh, start=a, stop=b_)))
            logit_led[tc] = inc("t")
            wait("s", "t", logit_led[tc])
            emit("s", lambda o=l8_sb[:, :], i=ps_x[0][0:8, 0:128]:
                 nc.scalar.copy(o, i).then_inc(sm["s"], 1))
            l8_led[tc] = inc("s")

        def emit_gates(tc):
            wait("t", "s", l8_led[tc])
            if tc == 0:
                wait("t", "v", red_done_led[3])
            else:
                wait("t", "s", lf_led[tc - 1])
            emit("t", lambda o=ps_x[1][:, 0:8], i=l8_sb[:, :], id_=eye32[0:8, 0:8]:
                 nc.tensor.transpose(o, i, id_).then_inc(sm["t"], 1))
            ltr_led[tc] = inc("t")
            wait("s", "t", ltr_led[tc])
            emit("s", lambda o=l_f[:, tc * 8:(tc + 1) * 8], i=ps_x[1][:, 0:8]:
                 nc.scalar.copy(o, i).then_inc(sm["s"], 1))
            lf_led[tc] = inc("s")
            lfv = l_f[:, tc * 8:(tc + 1) * 8]
            m1, m1n, m2 = gs_f[:, 4:5], gs_f[:, 5:6], gs_f[:, 6:7]
            eq = gs_f[:, 8:16]
            l2 = gs_f[:, 16:24]

            def vstep(fn):
                emit("v", lambda f=fn: f().then_inc(sm["v"], 1))
                c = inc("v")
                wait("v", "v", c)
                return c

            wait("v", "s", lf_led[tc])
            vstep(lambda o=m1, i=lfv: nc.vector.reduce_max(o, i, axis=AX))
            vstep(lambda o=m1n, i=lfv:
                  nc.vector.reduce_max(o, i, axis=AX, negate=True))
            vstep(lambda o=eq, i=lfv, sc=m1:
                  nc.vector.tensor_single_scalar(o, i, sc, ALU.is_ge))
            vstep(lambda o=l2, i=eq:
                  nc.vector.tensor_single_scalar(o, i, -BIG, ALU.mult))
            vstep(lambda o=l2, a=l2, b_=lfv: nc.vector.tensor_add(o, a, b_))
            p1 = vstep(lambda o=m2, i=l2: nc.vector.reduce_max(o, i, axis=AX))
            e1 = e1_f[:, tc * 8:(tc + 1) * 8]
            e2m = gs_f[:, 7:8]
            wait("s", "v", p1)
            emit("s", lambda o=e1, i=lfv, b_=m1n:
                 nc.scalar.activation(o, i, AF.Exp, bias=b_))
            emit("s", lambda o=e2m, i=m2, b_=m1n:
                 nc.scalar.activation(o, i, AF.Exp, bias=b_).then_inc(sm["s"], 1))
            exp_led[tc] = inc("s")
            ge2 = gs_f[:, 8:16]
            gu = gs_f[:, 16:24]
            den = gs_f[:, 4:5]
            rden = gs_f[:, 5:6]
            gv = g_f[:, tc * 8:(tc + 1) * 8]
            wait("v", "s", exp_led[tc])
            vstep(lambda o=ge2, i=lfv, sc=m2:
                  nc.vector.tensor_single_scalar(o, i, sc, ALU.is_ge))
            vstep(lambda o=gu, a=e1, b_=ge2: nc.vector.tensor_mul(o, a, b_))
            vstep(lambda o=den, i=e2m: nc.vector.tensor_scalar_add(o, i, 1.0))
            vstep(lambda o=rden, i=den: nc.vector.reciprocal(o, i))
            gate_led[tc] = vstep(lambda o=gv, i=gu, sc=rden:
                                 nc.vector.tensor_single_scalar(o, i, sc, ALU.mult))
            wait("t", "v", gate_led[tc])
            if tc >= 1:
                wait("t", "s", gT_led[tc - 1])
            emit("t", lambda o=ps_x[1][0:8, 128:256], i=g_f[:, tc * 8:(tc + 1) * 8],
                 id_=eye32[:, :]:
                 nc.tensor.transpose(o, i, id_).then_inc(sm["t"], 1))
            gttr = inc("t")
            wait("s", "t", gttr)
            emit("s", lambda o=gT_sb[0:8, tc * 128:(tc + 1) * 128],
                 i=ps_x[1][0:8, 128:256]:
                 nc.scalar.copy(o, i).then_inc(sm["s"], 1))
            gT_led[tc] = inc("s")

        for r, rd in enumerate(ROUNDS):
            if rd["piece"] is not None:
                pump_pieces(rd["piece"])
            if r == 16:
                if _OPT_LADDER:
                    emit_p2a(0)
                    emit_p2a(1)
                    emit_gates(0)
                else:
                    for _tc in range(NTC):
                        emit_p2a(_tc)
                        emit_gates(_tc)

            if r == 12:
                for dc in range(NDC):
                    if dc == 0:
                        wait("v", "wa", acc_back_w[0])
                        wait("v", "s", qtr_ev[11, NTC - 1])
                    emit("v", lambda o=qTblk(dc), i=qTblk(dc),
                         sc=a_cc[:, dc:dc + 1]:
                         nc.vector.tensor_single_scalar(o, i, sc, ALU.mult)
                         .then_inc(sm["v"], 1))
                    fold_led[dc] = inc("v")

            for tc in range(NTC):
                if _OPT_PREPASS and r == 0 and tc + 4 < NTC:
                    prepass_tail(tc + 4)
                if _OPT_LADDER and r == 16:
                    if tc + 2 < NTC:
                        emit_p2a(tc + 2)
                    if tc + 1 < NTC:
                        emit_gates(tc + 1)
                gidx = emit_group(r, tc)
                if rd["kind"] == "K":
                    evac_K(r, tc, gidx)
                elif rd["kind"] == "V":
                    evac_V(r, tc, gidx)
                elif rd["kind"] == "Q":
                    evac_Q(r, tc, gidx)
                    if r in (9, 10, 11) and tc >= 1:
                        q_transposes(r, tc - 1, gidx - 1)
                else:
                    if rd["kind"] == "WO":
                        evac_WO(r, tc, gidx)
                    else:
                        evac_EXPBIAS(r, tc, gidx)
            if r in (9, 10, 11):
                q_transposes(r, NTC - 1, len(Gseq) - 1)

            if r == 7:
                emit_a_reduce()
            if r == 8:
                emit_tab(0, tabQ, ("v", ropeK_done[6]))
                emit_tab(1, tabQ, ("v", ropeK_done[7]))
            if r == 11:
                wait("v", "t", mm_led[gk(11, NTC - 1)])
                for tc in range(NTC):
                    for h in range(2):
                        k = 2 * tc + h
                        if k >= 2:
                            wait("y", "v", xh_led[k - 2])
                        src = x_h[tc * 128:(tc + 1) * 128,
                                  h * 1024:(h + 1) * 1024]
                        ys = f"y{k % 2}"
                        inc(ys, 16)
                        emit("y", lambda d=xtile(k % 2), s_=src, m_=sm[ys]:
                             nc.sync.dma_start(out=d, in_=s_)
                             .then_inc(m_, 16))
                        x2load[tc, h] = (ys, ct[ys])
                        wait("v", *x2load[tc, h])
                        emit("v", lambda o=x1v(tc, h * 1024, (h + 1) * 1024),
                             i=xtile(k % 2), a=stc(tc, 6), b_=stc(tc, 7):
                             nc.vector.tensor_scalar(o, i, a, b_, ALU.mult,
                                                     ALU.subtract)
                             .then_inc(sm["v"], 1))
                        xh_led[k] = inc("v")

        if _DEBUG:
            wait("y", "v", ct["v"])
            wait("y", "s", ct["s"])
            wait("y", "t", ct["t"])
            for nm, src in (("d_x1", buf_a[:, :]), ("d_qt", buf_b[:, :]),
                            ("d_h2t", buf_u[:, :]), ("d_acc", buf_f[:, :]),
                            ("d_g", g_f[:, :]), ("d_acc2", a_cc[:, :]),
                            ("d_stats", st_f[:, :]), ("d_gt", gT_sb[:, :]),
                            ("d_eb", eb_sb[:, :]), ("d_lf", l_f[:, :])):
                inc("o", 16)
                emit("y", lambda d=dbg[nm], s_=src:
                     nc.sync.dma_start(out=d, in_=s_).then_inc(sm["o"], 16))
        emit("y", lambda v_=ct["o"]: nc.sync.wait_ge(sm["o"], v_))
        for _k, _v in wmax.items():
            assert _v <= ct[_k], f"wait on sem {_k} for {_v} but max count {ct[_k]}"


        @block.tensor
        def _(eng):
            for f in q["t"]:
                f()

        @block.vector
        def _(eng):
            for f in q["v"]:
                f()

        @block.scalar
        def _(eng):
            for f in q["s"]:
                f()

        @block.sync
        def _(eng):
            for f in q["y"]:
                f()

        @block.gpsimd
        def _(eng):
            for f in q["g"]:
                f()

    return nc


_PERM = np.concatenate([np.arange(0, 1024, 2), np.arange(1, 1024, 2),
                        np.arange(1024, 2048)])


def _host_stage(inputs):
    W_in = np.asarray(inputs["L0_attn_W_in"], dtype=np.float32)
    b_in = np.asarray(inputs["L0_attn_b_in"], dtype=np.float32)
    W_out = np.asarray(inputs["L0_attn_W_out"], dtype=np.float32)
    b_out = np.asarray(inputs["L0_attn_b_out"], dtype=np.float32)
    gate_w = np.asarray(inputs["L0_ffn_gate_w"], dtype=np.float32)
    ew = np.ascontiguousarray(np.asarray(inputs["L0_ffn_expert_w"],
                                         dtype=np.float32))
    eb = np.ascontiguousarray(np.asarray(inputs["L0_ffn_expert_b"],
                                         dtype=np.float32))

    Q_, K_, V_ = W_in[0:2048], W_in[2048:4096], W_in[4096:6144]
    bQ, bK, bV = b_in[0:2048], b_in[2048:4096], b_in[4096:6144]
    w1 = np.ascontiguousarray(
        np.concatenate([K_[_PERM], V_[_PERM], Q_[_PERM]], axis=0).T)
    b1 = np.ascontiguousarray(
        np.concatenate([bK[_PERM], bV[_PERM], bQ[_PERM]])[None, :])
    wo = np.ascontiguousarray(W_out.T[_PERM, :])
    bo = np.ascontiguousarray(b_out[None, :])
    gw = np.ascontiguousarray(gate_w.T)

    inv = 1.0 / (10000.0 ** (np.arange(0, 1024, 2, dtype=np.float64) / 1024.0))
    tabs = {}
    for half in range(2):
        pos = np.arange(half * 1024, (half + 1) * 1024, dtype=np.float64)
        ang = pos[:, None] * inv[None, :]
        tabs[half] = (np.cos(ang).astype(np.float32).astype(ml_dtypes.bfloat16),
                      np.sin(ang).astype(np.float32).astype(ml_dtypes.bfloat16))

    eye16 = np.eye(128, dtype=ml_dtypes.bfloat16)
    eye32 = np.eye(128, dtype=np.float32)
    ones16 = np.ones((1, 128), dtype=ml_dtypes.bfloat16)
    konst = np.tile(np.array([[EPS, 1.0]], dtype=np.float32), (128, 1))

    X = np.asarray(inputs["L0_attn_X"], dtype=np.float32)
    in_maps = []
    for c in range(8):
        b, half = c // 2, c % 2
        x = np.ascontiguousarray(X[b, half * 1024:(half + 1) * 1024, :])
        cosT, sinT = tabs[half]
        in_maps.append(dict(x=x, w1=w1, wo=wo, ew=ew, b1=b1, bo=bo, gw=gw,
                            eb=eb, cosT=cosT, sinT=sinT,
                            eye16=eye16, eye32=eye32, ones16=ones16,
                            konst=konst))
    return in_maps


def kernel(**inputs):
    in_maps = _host_stage(inputs)
    nc = build_nc()
    res = run_bass_kernel_spmd(nc, in_maps, core_ids=list(range(8)))
    out = np.empty((B, S, D), dtype=np.float32)
    for c in range(8):
        b, half = c // 2, c % 2
        out[b, half * 1024:(half + 1) * 1024, :] = res.results[c]["out"]
    return out


# version 09
# speedup vs baseline: 1.0825x; 1.0825x over previous
import numpy as np
import ml_dtypes
from contextlib import ExitStack

import concourse.bass as bass
import concourse.mybir as mybir
from concourse.bass_utils import run_bass_kernel_spmd
from concourse.hw_specs import get_activation_tables

F32 = mybir.dt.float32
BF16 = mybir.dt.bfloat16
AF = mybir.ActivationFunctionType
ALU = mybir.AluOpType
AX = mybir.AxisListType.X

B, S, D = 4, 2048, 2048
TPC = 1024
NTC = 8
NDC = 16
NE = 8
EPS = 1e-5
BIG = 1.0e30

ROUNDS = []
for r in range(16):
    ROUNDS.append(dict(kind=("K", "V", "Q", "WO")[r // 4], piece=r, n=r % 4))
for n in range(4):
    for e in range(NE):
        ROUNDS.append(dict(kind="EXP", piece=16 + n * 8 + e, n=n, e=e))
    ROUNDS.append(dict(kind="BIAS", piece=None, n=n))
NR = len(ROUNDS)
NPIECE = 48
_DEBUG = False
_OPT_PREPASS = True
_OPT_LADDER = True


def build_nc():
    nc = bass.Bass("TRN2", target_bir_lowering=False,
                   detect_race_conditions=False)

    x_h = nc.dram_tensor("x", [TPC, D], F32, kind="ExternalInput").ap()
    w1_h = nc.dram_tensor("w1", [D, 6144], F32, kind="ExternalInput").ap()
    wo_h = nc.dram_tensor("wo", [D, D], F32, kind="ExternalInput").ap()
    ew_h = nc.dram_tensor("ew", [NE, D, D], F32, kind="ExternalInput").ap()
    b1_h = nc.dram_tensor("b1", [1, 6144], F32, kind="ExternalInput").ap()
    bo_h = nc.dram_tensor("bo", [1, D], F32, kind="ExternalInput").ap()
    gw_h = nc.dram_tensor("gw", [D, 8], F32, kind="ExternalInput").ap()
    eb_h = nc.dram_tensor("eb", [NE, D], F32, kind="ExternalInput").ap()
    cos_h = nc.dram_tensor("cosT", [TPC, 512], BF16, kind="ExternalInput").ap()
    sin_h = nc.dram_tensor("sinT", [TPC, 512], BF16, kind="ExternalInput").ap()
    eye16_h = nc.dram_tensor("eye16", [128, 128], BF16, kind="ExternalInput").ap()
    eye32_h = nc.dram_tensor("eye32", [128, 128], F32, kind="ExternalInput").ap()
    ones_h = nc.dram_tensor("ones16", [1, 128], BF16, kind="ExternalInput").ap()
    konst_h = nc.dram_tensor("konst", [128, 2], F32, kind="ExternalInput").ap()
    out_h = nc.dram_tensor("out", [TPC, D], F32, kind="ExternalOutput").ap()

    cc_in = nc.dram_tensor("cc_in", [128, 16], F32)
    cc_out = nc.dram_tensor("cc_out", [128, 16], F32)
    dbg = {}
    if _DEBUG:
        for nm, shp, dt_ in (("d_x1", [128, 16384], BF16),
                             ("d_qt", [128, 16384], BF16),
                             ("d_h2t", [128, 16384], BF16),
                             ("d_acc", [128, 4096], F32),
                             ("d_g", [128, 64], F32),
                             ("d_acc2", [128, 16], F32),
                             ("d_stats", [128, 64], F32),
                             ("d_gt", [8, 1024], BF16),
                             ("d_eb", [8, 2048], BF16),
                             ("d_lf", [128, 64], F32)):
            dbg[nm] = nc.dram_tensor(nm, shp, dt_, kind="ExternalOutput").ap()

    def hbm_piece(p):
        if p < 12:
            src = w1_h[:, p * 512:(p + 1) * 512]
        elif p < 16:
            src = wo_h[:, (p - 12) * 512:(p - 11) * 512]
        else:
            k = p - 16
            n, e = k // 8, k % 8
            src = ew_h[e, :, n * 512:(n + 1) * 512]
        return src.rearrange("(c p) n -> p c n", p=128)

    with ExitStack() as ctx:
        al = ctx.enter_context
        buf_a = al(nc.sbuf_tensor("buf_a", [128, 16384], BF16))
        buf_b = al(nc.sbuf_tensor("buf_b", [128, 16384], BF16))
        buf_u = al(nc.sbuf_tensor("buf_u", [128, 16384], BF16))
        rs = [al(nc.sbuf_tensor(f"ring{i}", [128, 8192], BF16)) for i in range(3)]
        buf_f = al(nc.sbuf_tensor("buf_f", [128, 4096], F32))
        htmp = [al(nc.sbuf_tensor(f"htmp{i}", [128, 2048], BF16)) for i in range(2)]
        sqtr = al(nc.sbuf_tensor("sqtr", [128, 2048], BF16))
        st_f = al(nc.sbuf_tensor("st_f", [128, 64], F32))
        l_f = al(nc.sbuf_tensor("l_f", [128, 64], F32))
        e1_f = al(nc.sbuf_tensor("e1_f", [128, 64], F32))
        gs_f = al(nc.sbuf_tensor("gs_f", [128, 32], F32))
        g_f = al(nc.sbuf_tensor("g_f", [128, 64], F32))
        gbf = al(nc.sbuf_tensor("gbf", [128, 64], BF16))
        gT_sb = al(nc.sbuf_tensor("gT_sb", [8, 1024], BF16))
        l8_sb = al(nc.sbuf_tensor("l8_sb", [8, 128], F32))
        a_fm = al(nc.sbuf_tensor("a_fm", [128, 16], F32))
        a_cc = al(nc.sbuf_tensor("a_cc", [128, 16], F32))
        b1_sb = al(nc.sbuf_tensor("b1_sb", [1, 6144], BF16))
        bo_sb = al(nc.sbuf_tensor("bo_sb", [1, 2048], BF16))
        gw_sb = al(nc.sbuf_tensor("gw_sb", [128, 128], BF16))
        eb_sb = al(nc.sbuf_tensor("eb_sb", [8, 2048], BF16))
        eye16 = al(nc.sbuf_tensor("eye16s", [128, 128], BF16))
        eye32 = al(nc.sbuf_tensor("eye32s", [128, 128], F32))
        ones_sb = al(nc.sbuf_tensor("ones_sb", [1, 128], BF16))
        konst_sb = al(nc.sbuf_tensor("konst_sb", [128, 2], F32))
        ps_m = al(nc.psum_tensor("ps_m", [128, 2048], F32))
        ps_t = [al(nc.psum_tensor(f"ps_t{i}", [128, 1024], BF16)) for i in range(2)]
        ps_x = [al(nc.psum_tensor(f"ps_x{i}", [128, 512], F32)) for i in range(2)]
        sm = {k: al(nc.semaphore(f"sem_{k}"))
              for k in ("t", "v", "s", "o", "c", "yc", "y0", "y1",
                        "yt0", "yt1", "w0", "w1", "w2", "wa")}
        block = al(nc.Block())

        def h1T(dc, tc):
            return buf_a[:, dc * 1024 + tc * 128: dc * 1024 + (tc + 1) * 128]

        def x1v(tc, c0, c1):
            return buf_a[:, tc * 2048 + c0: tc * 2048 + c1]

        def kall(tc, c0, c1):
            return buf_b[:, tc * 2048 + c0: tc * 2048 + c1]

        def qT(dc, tc):
            return buf_b[:, dc * 1024 + tc * 128: dc * 1024 + (tc + 1) * 128]

        def qTblk(dc):
            return buf_b[:, dc * 1024:(dc + 1) * 1024]

        def h2T(dc, tc):
            return buf_u[:, dc * 1024 + tc * 128: dc * 1024 + (tc + 1) * 128]

        def tab_cos(j):
            return buf_u[:, j * 1024: j * 1024 + 512]

        def tab_sin(j):
            return buf_u[:, j * 1024 + 512:(j + 1) * 1024]

        def qtileA(tc):
            return buf_u[:, 2048 + tc * 512: 2048 + (tc + 1) * 512]

        def qslot(j):
            return buf_u[:, 6144 + j * 512: 6144 + (j + 1) * 512]

        def ropeout(j):
            return buf_u[:, 7168 + j * 1024: 7168 + (j + 1) * 1024]

        def vslot(j):
            return buf_u[:, 9216 + j * 512: 9216 + (j + 1) * 512]

        prodslot = buf_u[:, 11264:11776]

        def qtmp(j):
            return buf_u[:, 12288 + j * 512: 12288 + (j + 1) * 512]
        apart = buf_f[:, 0:2048]

        def xtile(j):
            return buf_f[:, 2048 + j * 1024: 2048 + (j + 1) * 1024]

        def accv(tc):
            return buf_f[:, tc * 512:(tc + 1) * 512]

        def stc(tc, k):
            return st_f[:, tc * 8 + k: tc * 8 + k + 1]

        def bank(g):
            return ps_m[:, (g % 4) * 512:((g % 4) + 1) * 512]

        q = {k: [] for k in ("t", "v", "s", "y", "g")}
        ct = {k: 0 for k in ("t", "v", "s", "o", "c", "yc", "y0", "y1",
                             "yt0", "yt1", "w0", "w1", "w2", "wa")}
        ENG = {"t": nc.tensor, "v": nc.vector, "s": nc.scalar,
               "y": nc.sync, "g": nc.gpsimd}

        def emit(e_, fn):
            q[e_].append(fn)

        wmax = {}

        def wait(e_, semk, val):
            wmax[semk] = max(wmax.get(semk, 0), val)
            emit(e_, lambda eng=ENG[e_], s_=sm[semk], v_=val: eng.wait_ge(s_, v_))

        def inc(semk, amt=1):
            ct[semk] += amt
            return ct[semk]

        Gseq = []
        mm_led = {}
        evac_led = {}
        tb_led = []
        tb_evac = []
        h1T_rdy = {}
        h1half = {}
        ln1 = {}
        ropeK_done = {}
        ropeQ_done = {}
        tabK = {}
        tabQ = {}
        rn_led = {}
        vprod_led = {}
        prod_cnt = [0]
        qslot_rel = {}
        q9tr = {}
        q9ev = {}
        qtr_ev = {}
        fold_led = {}
        red_done_led = {}
        x1ev = {}
        x2load = {}
        xh_led = {}
        a_tr_led = [0]
        h2b_led = {}
        h2T_rdy = {}
        logit_led = {}
        l8_led = {}
        ltr_led = {}
        lf_led = {}
        exp_led = {}
        gate_led = {}
        gT_led = {}
        out_led = {}
        w_led = {}
        a_dma_y = [None]
        acc_back_w = [None]
        piece_round = {}
        for r_, rd_ in enumerate(ROUNDS):
            if rd_["piece"] is not None:
                piece_round[rd_["piece"]] = r_

        def gk(r, tc):
            return r * 1000 + tc

        tabsets = get_activation_tables(nc.m.arch)
        set_id = list(tabsets).index("natural_log_exp_and_others")
        emit("s", lambda: nc.scalar.add_instruction(
            mybir.InstLoadActFuncSet(
                name=nc.get_next_instruction_name(),
                act_func_set_id=set_id, ins=[], outs=[])))

        nc.const_aps.register(F32, EPS, konst_sb[:, 0:1])
        nc.const_aps.register(F32, 1.0, konst_sb[:, 1:2])
        for dst, src in ((eye16[:, :], eye16_h), (eye32[:, :], eye32_h),
                         (ones_sb[:, :], ones_h), (konst_sb[:, :], konst_h)):
            inc("yc", 16)
            emit("y", lambda d=dst, s_=src:
                 nc.sync.dma_start(out=d, in_=s_).then_inc(sm["yc"], 16))
        const_y = ct["yc"]
        wait("s", "yc", const_y)
        for dst, src in (
                (b1_sb[:, :], b1_h), (bo_sb[:, :], bo_h),
                (gw_sb[:, :].rearrange("p (c n) -> p c n", n=8),
                 gw_h.rearrange("(c p) n -> p c n", p=128)),
                (eb_sb[:, :], eb_h)):
            inc("wa", 16)
            emit("g", lambda d=dst, s_=src:
                 nc.gpsimd.dma_start(out=d, in_=s_).then_inc(sm["wa"], 16))
        const_w = ct["wa"]

        emit("v", lambda: nc.vector.memset(apart, 0.0))

        def prepass_head(tc):
            for h in range(2):
                k = 2 * tc + h
                if k >= 2:
                    wait("y", "v", 2 * (tc - 1) + h + 1)
                    wait("y", "s", ln1[tc - 1])
                src = x_h[tc * 128:(tc + 1) * 128, h * 1024:(h + 1) * 1024]
                ys = f"y{k % 2}"
                inc(ys, 16)
                emit("y", lambda d=xtile(k % 2), s_=src, m_=sm[ys]:
                     nc.sync.dma_start(out=d, in_=s_).then_inc(m_, 16))
                wait("s", ys, ct[ys])
                xt = xtile(k % 2)
                emit("s", lambda o=sqtr[:, 0:1024], i=xt, a=stc(tc, 2 * h):
                     nc.scalar.activation(o, i, AF.Copy, accum_out=a))
                emit("s", lambda o=sqtr[:, 1024:2048], i=xt, a=stc(tc, 2 * h + 1):
                     nc.scalar.activation(o, i, AF.Square, accum_out=a))
            M, R, S1, S2 = stc(tc, 4), stc(tc, 5), stc(tc, 6), stc(tc, 7)
            t0, t1 = gs_f[:, 0:1], gs_f[:, 1:2]
            emit("s", lambda o=t0, i=stc(tc, 0), a=stc(tc, 2):
                 nc.scalar.add(o, i, a))
            emit("s", lambda o=M, i=t0: nc.scalar.mul(o, i, 1.0 / 2048.0))
            emit("s", lambda o=t1, i=M: nc.scalar.activation(o, i, AF.Square))
            emit("s", lambda o=t1, i=t1: nc.scalar.mul(o, i, -1.0))
            emit("s", lambda o=t0, i=stc(tc, 1), a=stc(tc, 3):
                 nc.scalar.add(o, i, a))
            emit("s", lambda o=t0, i=t0: nc.scalar.mul(o, i, 1.0 / 2048.0))
            emit("s", lambda o=t0, i=t0, a=t1: nc.scalar.add(o, i, a))
            emit("s", lambda o=t0, i=t0: nc.scalar.add(o, i, EPS))
            emit("s", lambda o=t0, i=t0: nc.scalar.activation(o, i, AF.Ln))
            emit("s", lambda o=R, i=t0:
                 nc.scalar.activation(o, i, AF.Exp, scale=-0.5))
            emit("s", lambda o=S1, i=R: nc.scalar.add(o, i, 1.0))
            emit("s", lambda o=S2, i=M, a=R:
                 nc.scalar.mul(o, i, a).then_inc(sm["s"], 1))
            ln1[tc] = inc("s")

        def prepass_tail(tc):
            M, R = stc(tc, 4), stc(tc, 5)
            if tc >= 2:
                wait("v", "t", tb_led[2 * (tc - 2) + 1])
            wait("v", "s", ln1[tc])
            for h in range(2):
                xt = xtile((2 * tc + h) % 2)
                dst = htmp[tc % 2][:, h * 1024:(h + 1) * 1024]
                emit("v", lambda o=dst, i=xt, m=M, r_=R:
                     nc.vector.tensor_scalar(o, i, m, r_, ALU.subtract, ALU.mult)
                     .then_inc(sm["v"], 1))
                h1half[tc, h] = inc("v")
                assert h1half[tc, h] == 2 * tc + h + 1, "x-load pacing broke"

            for b_ in range(2):
                gb = len(tb_led)
                if gb == 0:
                    wait("t", "yc", const_y)
                wait("t", "v", h1half[tc, b_])
                if gb >= 2:
                    wait("t", "s", tb_evac[gb - 2])
                pt = ps_t[gb % 2]
                for j in range(8):
                    src = htmp[tc % 2][:, (b_ * 8 + j) * 128:(b_ * 8 + j + 1) * 128]
                    last = j == 7
                    emit("t", lambda o=pt[:, j * 128:(j + 1) * 128], i=src,
                         id_=eye16[:, :], l=last:
                         (nc.tensor.transpose(o, i, id_).then_inc(sm["t"], 1)
                          if l else nc.tensor.transpose(o, i, id_)))
                tb_led.append(inc("t"))
                wait("s", "t", tb_led[gb])
                dstv = buf_a[:, :].rearrange("p (c x) -> p c x", x=1024)[
                    :, b_ * 8:(b_ + 1) * 8, tc * 128:(tc + 1) * 128]
                srcv = pt[:, :].rearrange("p (c x) -> p c x", x=128)
                emit("s", lambda o=dstv, i=srcv:
                     nc.scalar.copy(o, i).then_inc(sm["s"], 1))
                tb_evac.append(inc("s"))
            h1T_rdy[tc] = tb_evac[-1]

        if _OPT_PREPASS:
            for _tc in range(4):
                prepass_head(_tc)
            for _tc in range(4, NTC):
                prepass_tail(_tc - 4)
                prepass_head(_tc)
        else:
            for _tc in range(NTC):
                prepass_head(_tc)
                prepass_tail(_tc)

        def emit_tab(tc, ledger, hold_sem=None):
            j = tc % 2
            if hold_sem is not None:
                wait("y", hold_sem[0], hold_sem[1])
            ys = f"yt{j}"
            for hsrc, dst in ((cos_h, tab_cos(j)), (sin_h, tab_sin(j))):
                src = hsrc[tc * 128:(tc + 1) * 128, :]
                inc(ys, 16)
                emit("y", lambda d=dst, s_=src, m_=sm[ys]:
                     nc.sync.dma_start(out=d, in_=s_).then_inc(m_, 16))
            ledger[tc] = (ys, ct[ys])

        emit_tab(0, tabK)
        emit_tab(1, tabK)

        gp_state = [0]

        def emit_collective_trigger():
            wait("g", "yc", a_dma_y[0])
            emit("g", lambda: nc.gpsimd.collective_compute(
                "AllReduce", ALU.add,
                replica_groups=[[0, 1], [2, 3], [4, 5], [6, 7]],
                ins=[cc_in.ap()], outs=[cc_out.ap()]).then_inc(sm["c"], 1))
            inc("c")

        def emit_collective_fetch():
            wait("g", "c", ct["c"])
            inc("wa", 16)
            emit("g", lambda: nc.gpsimd.dma_start(out=a_cc[:, :], in_=cc_out[:, :])
                 .then_inc(sm["wa"], 16))
            acc_back_w[0] = ct["wa"]

        def pump_pieces(upto):
            while gp_state[0] <= upto and gp_state[0] < NPIECE:
                p = gp_state[0]
                if p >= 3:
                    rprev = piece_round[p - 3]
                    wait("g", "t", mm_led[gk(rprev, NTC - 1)])
                slot = rs[p % 3]
                ws = f"w{p % 3}"
                dstv = slot[:, :].rearrange("p (c n) -> p c n", n=512)
                inc(ws, 16)
                emit("g", lambda d=dstv, s_=hbm_piece(p), m_=sm[ws]:
                     nc.gpsimd.dma_start(out=d, in_=s_).then_inc(m_, 16))
                w_led[p] = (ws, ct[ws])
                gp_state[0] += 1
                if gp_state[0] == 10:
                    emit_collective_trigger()
                if gp_state[0] == 13:
                    emit_collective_fetch()

        pump_pieces(2)

        def emit_group(r, tc):
            rd = ROUNDS[r]
            G = gk(r, tc)
            gidx = len(Gseq)
            Gseq.append(G)
            bk = bank(gidx)
            if r == 0 and tc == 0:
                wait("t", "wa", const_w)
                wait("t", "yc", const_y)
            if rd["piece"] is not None and tc == 0:
                wait("t", *w_led[rd["piece"]])
            if gidx >= 4:
                semk, cnt = evac_led[Gseq[gidx - 4]]
                wait("t", semk, cnt)
            if r == 0:
                wait("t", "s", h1T_rdy[tc])
            if rd["kind"] == "EXP" and rd["e"] == 0 and rd["n"] == 0:
                wait("t", "s", h2T_rdy[tc])
            if rd["kind"] == "BIAS":
                wait("t", "s", gT_led[tc])
                emit("t", lambda o=bk, l=gT_sb[0:8, tc * 128:(tc + 1) * 128],
                     rh=eb_sb[0:8, rd["n"] * 512:(rd["n"] + 1) * 512]:
                     nc.tensor.matmul(o, l, rh, start=True, stop=True)
                     .then_inc(sm["t"], 1))
                mm_led[G] = inc("t")
                return gidx
            slot = rs[rd["piece"] % 3]
            started = False
            if rd["kind"] in ("K", "V", "Q", "WO"):
                brow = (b1_sb[0:1, rd["piece"] * 512:(rd["piece"] + 1) * 512]
                        if rd["kind"] != "WO"
                        else bo_sb[0:1, rd["n"] * 512:(rd["n"] + 1) * 512])
                emit("t", lambda o=bk, l=ones_sb[0:1, :], rh=brow:
                     nc.tensor.matmul(o, l, rh, start=True, stop=False))
                started = True
            for dc in range(NDC):
                if rd["kind"] == "WO":
                    if r == 12 and tc == 0:
                        wait("t", "v", fold_led[dc])
                    lhs = qT(dc, tc)
                elif rd["kind"] == "EXP":
                    lhs = h2T(dc, tc)
                else:
                    lhs = h1T(dc, tc)
                rh = slot[:, dc * 512:(dc + 1) * 512]
                st_ = (not started) and dc == 0
                sp = dc == NDC - 1
                emit("t", lambda o=bk, l=lhs, rr=rh, a=st_, b_=sp:
                     (nc.tensor.matmul(o, l, rr, start=a, stop=b_)
                      .then_inc(sm["t"], 1) if b_ else
                      nc.tensor.matmul(o, l, rr, start=a, stop=b_)))
            mm_led[G] = inc("t")
            return gidx

        def evac_K(r, tc, gidx):
            G = gk(r, tc)
            bk = bank(gidx)
            wait("s", "t", mm_led[G])
            emit("s", lambda o=kall(tc, r * 512, (r + 1) * 512), i=bk:
                 nc.scalar.copy(o, i).then_inc(sm["s"], 1))
            evac_led[G] = ("s", inc("s"))
            if r == 1:
                wait("v", "s", evac_led[G][1])
                wait("v", *tabK[tc])
                j = tc % 2
                Ev, Ov = kall(tc, 0, 512), kall(tc, 512, 1024)
                tE, tO = qtmp(2 * j), qtmp(2 * j + 1)
                emit("v", lambda o=tE, a=Ev, b_=tab_sin(j): nc.vector.tensor_mul(o, a, b_))
                emit("v", lambda o=tO, a=Ov, b_=tab_sin(j): nc.vector.tensor_mul(o, a, b_))
                emit("v", lambda o=Ev, a=Ev, b_=tab_cos(j): nc.vector.tensor_mul(o, a, b_))
                emit("v", lambda o=Ov, a=Ov, b_=tab_cos(j): nc.vector.tensor_mul(o, a, b_))
                emit("v", lambda o=Ev, a=Ev, b_=tO: nc.vector.tensor_sub(o, a, b_))
                emit("v", lambda o=Ov, a=Ov, b_=tE:
                     nc.vector.tensor_add(o, a, b_).then_inc(sm["v"], 1))
                ropeK_done[tc] = inc("v")
                if tc + 2 < NTC:
                    emit_tab(tc + 2, tabK, ("v", ropeK_done[tc]))
            if r == 3:
                wait("s", "v", ropeK_done[tc])
                ssq, rn = stc(tc, 0), stc(tc, 1)
                emit("s", lambda o=sqtr[:, :], i=kall(tc, 0, 2048), a=ssq:
                     nc.scalar.activation(o, i, AF.Square, accum_out=a))
                emit("s", lambda o=ssq, i=ssq: nc.scalar.activation(o, i, AF.Ln))
                emit("s", lambda o=rn, i=ssq:
                     nc.scalar.activation(o, i, AF.Exp, scale=-0.5)
                     .then_inc(sm["s"], 1))
                rn_led[tc] = inc("s")

        def evac_V(r, tc, gidx):
            G = gk(r, tc)
            bk = bank(gidx)
            vj = vslot(gidx % 2)
            wait("s", "t", mm_led[G])
            if gidx >= 2 and Gseq[gidx - 2] in vprod_led:
                wait("s", "v", vprod_led[Gseq[gidx - 2]])
            emit("s", lambda o=vj, i=bk, sc=stc(tc, 1):
                 nc.scalar.mul(o, i, sc).then_inc(sm["s"], 1))
            evac_led[G] = ("s", inc("s"))
            c0 = (r - 4) * 512
            wait("v", "s", evac_led[G][1])
            emit("v", lambda o=prodslot, a=kall(tc, c0, c0 + 512), b_=vj:
                 nc.vector.tensor_mul(o, a, b_).then_inc(sm["v"], 1))
            pcnt = inc("v")
            wait("v", "v", pcnt)
            emit("v", lambda o=apart[:, c0:c0 + 512], a=apart[:, c0:c0 + 512],
                 b_=prodslot:
                 nc.vector.tensor_add(o, a, b_).then_inc(sm["v"], 1))
            vprod_led[G] = inc("v")
            prod_cnt[0] = vprod_led[G]

        def evac_Q(r, tc, gidx):
            G = gk(r, tc)
            bk = bank(gidx)
            wait("s", "t", mm_led[G])
            if r == 8 and tc == 0:
                wait("s", "v", prod_cnt[0])
            if r == 8:
                dst = qtileA(tc)
            else:
                if gidx >= 2:
                    prevG = Gseq[gidx - 2]
                    if prevG in qslot_rel:
                        semk, cnt = qslot_rel[prevG]
                        wait("s", semk, cnt)
                dst = qslot(gidx % 2)
            emit("s", lambda o=dst, i=bk: nc.scalar.copy(o, i).then_inc(sm["s"], 1))
            evac_led[G] = ("s", inc("s"))
            if r == 9:
                wait("v", "s", evac_led[G][1])
                wait("v", *tabQ[tc])
                if tc >= 2:
                    wait("v", "t", q9tr[tc - 2])
                j = tc % 2
                Ev, Ov = qtileA(tc), dst
                RO = ropeout(tc % 2)
                Eo, Oo = RO[:, 0:512], RO[:, 512:1024]
                sp1, sp2 = qtmp(2 * j), qtmp(2 * j + 1)
                emit("v", lambda o=sp1, a=Ev, b_=tab_sin(j): nc.vector.tensor_mul(o, a, b_))
                emit("v", lambda o=sp2, a=Ov, b_=tab_sin(j): nc.vector.tensor_mul(o, a, b_))
                emit("v", lambda o=Eo, a=Ev, b_=tab_cos(j): nc.vector.tensor_mul(o, a, b_))
                emit("v", lambda o=Oo, a=Ov, b_=tab_cos(j): nc.vector.tensor_mul(o, a, b_))
                emit("v", lambda o=Eo, a=Eo, b_=sp2: nc.vector.tensor_sub(o, a, b_))
                emit("v", lambda o=Oo, a=Oo, b_=sp1:
                     nc.vector.tensor_add(o, a, b_).then_inc(sm["v"], 1))
                ropeQ_done[tc] = inc("v")
                qslot_rel[G] = ("v", ropeQ_done[tc])
                if tc + 2 < NTC:
                    emit_tab(tc + 2, tabQ, ("v", ropeQ_done[tc]))

        def q_transposes(r, tc, gidx_of_group):
            gb = len(tb_led)
            pt = ps_t[gb % 2]
            if gb >= 2:
                wait("t", "s", tb_evac[gb - 2])
            if r == 9:
                wait("t", "v", ropeQ_done[tc])
                srcbase = ropeout(tc % 2)
                nchunks, dc0 = 8, 0
            else:
                G = gk(r, tc)
                wait("t", "s", evac_led[G][1])
                srcbase = qslot(gidx_of_group % 2)
                nchunks, dc0 = 4, 8 + (r - 10) * 4
            for j in range(nchunks):
                src = srcbase[:, j * 128:(j + 1) * 128]
                last = j == nchunks - 1
                emit("t", lambda o=pt[:, j * 128:(j + 1) * 128], i=src,
                     id_=eye16[:, :], l=last:
                     (nc.tensor.transpose(o, i, id_).then_inc(sm["t"], 1)
                      if l else nc.tensor.transpose(o, i, id_)))
            tb_led.append(inc("t"))
            if r == 9:
                q9tr[tc] = tb_led[-1]
            else:
                qslot_rel[gk(r, tc)] = ("t", tb_led[-1])
            wait("s", "t", tb_led[gb])
            dstv = buf_b[:, :].rearrange("p (c x) -> p c x", x=1024)[
                :, dc0:dc0 + nchunks, tc * 128:(tc + 1) * 128]
            srcv = pt[:, 0:nchunks * 128].rearrange("p (c x) -> p c x", x=128)
            emit("s", lambda o=dstv, i=srcv:
                 nc.scalar.copy(o, i).then_inc(sm["s"], 1))
            tb_evac.append(inc("s"))
            qtr_ev[r, tc] = tb_evac[-1]

        def evac_WO(r, tc, gidx):
            G = gk(r, tc)
            bk = bank(gidx)
            n = r - 12
            wait("v", "t", mm_led[G])
            xv = x1v(tc, n * 512, (n + 1) * 512)
            emit("v", lambda o=xv, a=bk, b_=xv:
                 nc.vector.tensor_add(o, a, b_).then_inc(sm["v"], 1))
            evac_led[G] = ("v", inc("v"))
            if r == 15:
                x1ev[tc] = evac_led[G][1]

        def evac_EXPBIAS(r, tc, gidx):
            G = gk(r, tc)
            rd = ROUNDS[r]
            bk = bank(gidx)
            n = rd["n"]
            acc = accv(tc)
            wait("v", "t", mm_led[G])
            if rd["kind"] == "BIAS":
                emit("v", lambda o=acc, a=bk, b_=acc:
                     nc.vector.tensor_add(o, a, b_).then_inc(sm["v"], 1))
                evac_led[G] = ("v", inc("v"))
                wait("y", "v", evac_led[G][1])
                dst = out_h[tc * 128:(tc + 1) * 128, n * 512:(n + 1) * 512]
                inc("o", 16)
                emit("y", lambda d=dst, s_=acc:
                     nc.sync.dma_start(out=d, in_=s_).then_inc(sm["o"], 16))
                out_led[n, tc] = ct["o"]
                return
            e = rd["e"]
            gsc = g_f[:, tc * 8 + e: tc * 8 + e + 1]
            if e == 0:
                wait("v", "v", gate_led[tc])
                if n > 0:
                    wait("v", "o", out_led[n - 1, tc])
                xv = x1v(tc, n * 512, (n + 1) * 512)
                emit("v", lambda o=acc, i=bk, sc=gsc, b_=xv:
                     nc.vector.scalar_tensor_tensor(o, i, sc, b_, ALU.mult,
                                                    ALU.add).then_inc(sm["v"], 1))
            else:
                emit("v", lambda o=acc, i=bk, sc=gsc, a=acc:
                     nc.vector.scalar_tensor_tensor(o, i, sc, a, ALU.mult,
                                                    ALU.add).then_inc(sm["v"], 1))
            evac_led[G] = ("v", inc("v"))

        def emit_a_reduce():
            for wv in range(4):
                px = ps_x[wv % 2]
                if wv >= 2:
                    wait("t", "v", red_done_led[wv - 2])
                for j in range(4):
                    c = wv * 4 + j
                    src = apart[:, c * 128:(c + 1) * 128]
                    last = j == 3
                    emit("t", lambda o=px[:, j * 128:(j + 1) * 128], i=src,
                         id_=eye32[:, :], l=last:
                         (nc.tensor.transpose(o, i, id_).then_inc(sm["t"], 1)
                          if l else nc.tensor.transpose(o, i, id_)))
                aw = inc("t")
                a_tr_led[0] = aw
                wait("v", "t", aw)
                for j in range(4):
                    c = wv * 4 + j
                    last = j == 3
                    emit("v", lambda o=a_fm[:, c:c + 1],
                         i=px[:, j * 128:(j + 1) * 128], l=last:
                         (nc.vector.reduce_sum(o, i, axis=AX).then_inc(sm["v"], 1)
                          if l else nc.vector.reduce_sum(o, i, axis=AX)))
                red_done_led[wv] = inc("v")
            wait("y", "v", red_done_led[3])
            inc("yc", 16)
            emit("y", lambda: nc.sync.dma_start(out=cc_in[:, :], in_=a_fm[:, :])
                 .then_inc(sm["yc"], 16))
            a_dma_y[0] = ct["yc"]

        def emit_p2a(tc):
            wait("s", "v", x1ev[tc])
            M2, R2 = stc(tc, 4), stc(tc, 5)
            sA, sB = stc(tc, 0), stc(tc, 1)
            t0, t1 = gs_f[:, 2:3], gs_f[:, 3:4]
            xv = x1v(tc, 0, 2048)
            emit("s", lambda o=sqtr[:, :], i=xv, a=sA:
                 nc.scalar.activation(o, i, AF.Copy, accum_out=a))
            emit("s", lambda o=sqtr[:, :], i=xv, a=sB:
                 nc.scalar.activation(o, i, AF.Square, accum_out=a))
            emit("s", lambda o=M2, i=sA: nc.scalar.mul(o, i, 1.0 / 2048.0))
            emit("s", lambda o=t1, i=M2: nc.scalar.activation(o, i, AF.Square))
            emit("s", lambda o=t1, i=t1: nc.scalar.mul(o, i, -1.0))
            emit("s", lambda o=t0, i=sB: nc.scalar.mul(o, i, 1.0 / 2048.0))
            emit("s", lambda o=t0, i=t0, a=t1: nc.scalar.add(o, i, a))
            emit("s", lambda o=t0, i=t0: nc.scalar.add(o, i, EPS))
            emit("s", lambda o=t0, i=t0: nc.scalar.activation(o, i, AF.Ln))
            emit("s", lambda o=R2, i=t0:
                 nc.scalar.activation(o, i, AF.Exp, scale=-0.5)
                 .then_inc(sm["s"], 1))
            ln2c = inc("s")
            if tc >= 2:
                wait("v", "t", h2b_led[tc - 2])
            else:
                wait("v", "t", tb_led[13 + 2 * tc])
            wait("v", "s", ln2c)
            emit("v", lambda o=htmp[tc % 2][:, :], i=x1v(tc, 0, 2048), m=M2, r_=R2:
                 nc.vector.tensor_scalar(o, i, m, r_, ALU.subtract, ALU.mult)
                 .then_inc(sm["v"], 1))
            h2ap = inc("v")
            for b_ in range(2):
                gb = len(tb_led)
                pt = ps_t[gb % 2]
                wait("t", "v", h2ap)
                if gb >= 2:
                    wait("t", "s", tb_evac[gb - 2])
                for j in range(8):
                    src = htmp[tc % 2][:, (b_ * 8 + j) * 128:(b_ * 8 + j + 1) * 128]
                    last = j == 7
                    emit("t", lambda o=pt[:, j * 128:(j + 1) * 128], i=src,
                         id_=eye16[:, :], l=last:
                         (nc.tensor.transpose(o, i, id_).then_inc(sm["t"], 1)
                          if l else nc.tensor.transpose(o, i, id_)))
                tb_led.append(inc("t"))
                wait("s", "t", tb_led[gb])
                dstv = buf_u[:, :].rearrange("p (c x) -> p c x", x=1024)[
                    :, b_ * 8:(b_ + 1) * 8, tc * 128:(tc + 1) * 128]
                srcv = pt[:, :].rearrange("p (c x) -> p c x", x=128)
                emit("s", lambda o=dstv, i=srcv:
                     nc.scalar.copy(o, i).then_inc(sm["s"], 1))
                tb_evac.append(inc("s"))
            h2b_led[tc] = tb_led[-1]
            h2T_rdy[tc] = tb_evac[-1]
            if tc == 0:
                wait("t", "v", red_done_led[3])
            else:
                wait("t", "s", l8_led[tc - 1])
            wait("t", "s", h2T_rdy[tc])
            for dc in range(NDC):
                st_, sp = dc == 0, dc == NDC - 1
                emit("t", lambda o=ps_x[0][0:8, 0:128],
                     l=gw_sb[:, dc * 8:(dc + 1) * 8],
                     rh=h2T(dc, tc), a=st_, b_=sp:
                     (nc.tensor.matmul(o, l, rh, start=a, stop=b_)
                      .then_inc(sm["t"], 1) if b_ else
                      nc.tensor.matmul(o, l, rh, start=a, stop=b_)))
            logit_led[tc] = inc("t")
            wait("s", "t", logit_led[tc])
            emit("s", lambda o=l8_sb[:, :], i=ps_x[0][0:8, 0:128]:
                 nc.scalar.copy(o, i).then_inc(sm["s"], 1))
            l8_led[tc] = inc("s")

        def emit_gates(tc):
            wait("t", "s", l8_led[tc])
            if tc == 0:
                wait("t", "v", red_done_led[3])
            else:
                wait("t", "s", lf_led[tc - 1])
            emit("t", lambda o=ps_x[1][:, 0:8], i=l8_sb[:, :], id_=eye32[0:8, 0:8]:
                 nc.tensor.transpose(o, i, id_).then_inc(sm["t"], 1))
            ltr_led[tc] = inc("t")
            wait("s", "t", ltr_led[tc])
            emit("s", lambda o=l_f[:, tc * 8:(tc + 1) * 8], i=ps_x[1][:, 0:8]:
                 nc.scalar.copy(o, i).then_inc(sm["s"], 1))
            lf_led[tc] = inc("s")
            lfv = l_f[:, tc * 8:(tc + 1) * 8]
            m1, m1n, m2 = gs_f[:, 4:5], gs_f[:, 5:6], gs_f[:, 6:7]
            eq = gs_f[:, 8:16]
            l2 = gs_f[:, 16:24]

            def vstep(fn):
                emit("v", lambda f=fn: f().then_inc(sm["v"], 1))
                c = inc("v")
                wait("v", "v", c)
                return c

            wait("v", "s", lf_led[tc])
            vstep(lambda o=m1, i=lfv: nc.vector.reduce_max(o, i, axis=AX))
            vstep(lambda o=m1n, i=lfv:
                  nc.vector.reduce_max(o, i, axis=AX, negate=True))
            vstep(lambda o=eq, i=lfv, sc=m1:
                  nc.vector.tensor_single_scalar(o, i, sc, ALU.is_ge))
            vstep(lambda o=l2, i=eq:
                  nc.vector.tensor_single_scalar(o, i, -BIG, ALU.mult))
            vstep(lambda o=l2, a=l2, b_=lfv: nc.vector.tensor_add(o, a, b_))
            p1 = vstep(lambda o=m2, i=l2: nc.vector.reduce_max(o, i, axis=AX))
            e1 = e1_f[:, tc * 8:(tc + 1) * 8]
            e2m = gs_f[:, 7:8]
            wait("s", "v", p1)
            emit("s", lambda o=e1, i=lfv, b_=m1n:
                 nc.scalar.activation(o, i, AF.Exp, bias=b_))
            emit("s", lambda o=e2m, i=m2, b_=m1n:
                 nc.scalar.activation(o, i, AF.Exp, bias=b_).then_inc(sm["s"], 1))
            exp_led[tc] = inc("s")
            ge2 = gs_f[:, 8:16]
            gu = gs_f[:, 16:24]
            den = gs_f[:, 4:5]
            rden = gs_f[:, 5:6]
            gv = g_f[:, tc * 8:(tc + 1) * 8]
            wait("v", "s", exp_led[tc])
            vstep(lambda o=ge2, i=lfv, sc=m2:
                  nc.vector.tensor_single_scalar(o, i, sc, ALU.is_ge))
            vstep(lambda o=gu, a=e1, b_=ge2: nc.vector.tensor_mul(o, a, b_))
            vstep(lambda o=den, i=e2m: nc.vector.tensor_scalar_add(o, i, 1.0))
            vstep(lambda o=rden, i=den: nc.vector.reciprocal(o, i))
            gate_led[tc] = vstep(lambda o=gv, i=gu, sc=rden:
                                 nc.vector.tensor_single_scalar(o, i, sc, ALU.mult))
            wait("t", "v", gate_led[tc])
            if tc >= 1:
                wait("t", "s", gT_led[tc - 1])
            emit("t", lambda o=ps_x[1][0:8, 128:256], i=g_f[:, tc * 8:(tc + 1) * 8],
                 id_=eye32[:, :]:
                 nc.tensor.transpose(o, i, id_).then_inc(sm["t"], 1))
            gttr = inc("t")
            wait("s", "t", gttr)
            emit("s", lambda o=gT_sb[0:8, tc * 128:(tc + 1) * 128],
                 i=ps_x[1][0:8, 128:256]:
                 nc.scalar.copy(o, i).then_inc(sm["s"], 1))
            gT_led[tc] = inc("s")

        for r, rd in enumerate(ROUNDS):
            if rd["piece"] is not None:
                pump_pieces(rd["piece"])
            if r == 16:
                if _OPT_LADDER:
                    emit_p2a(0)
                    emit_p2a(1)
                    emit_gates(0)
                else:
                    for _tc in range(NTC):
                        emit_p2a(_tc)
                        emit_gates(_tc)

            if r == 12:
                for dc in range(NDC):
                    if dc == 0:
                        wait("v", "wa", acc_back_w[0])
                        wait("v", "s", qtr_ev[11, NTC - 1])
                    emit("v", lambda o=qTblk(dc), i=qTblk(dc),
                         sc=a_cc[:, dc:dc + 1]:
                         nc.vector.tensor_single_scalar(o, i, sc, ALU.mult)
                         .then_inc(sm["v"], 1))
                    fold_led[dc] = inc("v")

            for tc in range(NTC):
                if _OPT_PREPASS and r == 0 and tc + 4 < NTC:
                    prepass_tail(tc + 4)
                if _OPT_LADDER and r == 16:
                    if tc + 2 < NTC:
                        emit_p2a(tc + 2)
                    if tc + 1 < NTC:
                        emit_gates(tc + 1)
                gidx = emit_group(r, tc)
                if rd["kind"] == "K":
                    evac_K(r, tc, gidx)
                elif rd["kind"] == "V":
                    evac_V(r, tc, gidx)
                elif rd["kind"] == "Q":
                    evac_Q(r, tc, gidx)
                    if r in (9, 10, 11) and tc >= 1:
                        q_transposes(r, tc - 1, gidx - 1)
                else:
                    if rd["kind"] == "WO":
                        evac_WO(r, tc, gidx)
                    else:
                        evac_EXPBIAS(r, tc, gidx)
            if r in (9, 10, 11):
                q_transposes(r, NTC - 1, len(Gseq) - 1)

            if r == 7:
                emit_a_reduce()
            if r == 8:
                emit_tab(0, tabQ, ("v", ropeK_done[6]))
                emit_tab(1, tabQ, ("v", ropeK_done[7]))
            if r == 11:
                wait("v", "t", mm_led[gk(11, NTC - 1)])
                for tc in range(NTC):
                    for h in range(2):
                        k = 2 * tc + h
                        if k >= 2:
                            wait("y", "v", xh_led[k - 2])
                        src = x_h[tc * 128:(tc + 1) * 128,
                                  h * 1024:(h + 1) * 1024]
                        ys = f"y{k % 2}"
                        inc(ys, 16)
                        emit("y", lambda d=xtile(k % 2), s_=src, m_=sm[ys]:
                             nc.sync.dma_start(out=d, in_=s_)
                             .then_inc(m_, 16))
                        x2load[tc, h] = (ys, ct[ys])
                        wait("v", *x2load[tc, h])
                        emit("v", lambda o=x1v(tc, h * 1024, (h + 1) * 1024),
                             i=xtile(k % 2), a=stc(tc, 6), b_=stc(tc, 7):
                             nc.vector.tensor_scalar(o, i, a, b_, ALU.mult,
                                                     ALU.subtract)
                             .then_inc(sm["v"], 1))
                        xh_led[k] = inc("v")

        if _DEBUG:
            wait("y", "v", ct["v"])
            wait("y", "s", ct["s"])
            wait("y", "t", ct["t"])
            for nm, src in (("d_x1", buf_a[:, :]), ("d_qt", buf_b[:, :]),
                            ("d_h2t", buf_u[:, :]), ("d_acc", buf_f[:, :]),
                            ("d_g", g_f[:, :]), ("d_acc2", a_cc[:, :]),
                            ("d_stats", st_f[:, :]), ("d_gt", gT_sb[:, :]),
                            ("d_eb", eb_sb[:, :]), ("d_lf", l_f[:, :])):
                inc("o", 16)
                emit("y", lambda d=dbg[nm], s_=src:
                     nc.sync.dma_start(out=d, in_=s_).then_inc(sm["o"], 16))
        emit("y", lambda v_=ct["o"]: nc.sync.wait_ge(sm["o"], v_))
        for _k, _v in wmax.items():
            assert _v <= ct[_k], f"wait on sem {_k} for {_v} but max count {ct[_k]}"


        @block.tensor
        def _(eng):
            for f in q["t"]:
                f()

        @block.vector
        def _(eng):
            for f in q["v"]:
                f()

        @block.scalar
        def _(eng):
            for f in q["s"]:
                f()

        @block.sync
        def _(eng):
            for f in q["y"]:
                f()

        @block.gpsimd
        def _(eng):
            for f in q["g"]:
                f()

    return nc


_PERM = np.concatenate([np.arange(0, 1024, 2), np.arange(1, 1024, 2),
                        np.arange(1024, 2048)])


def _host_stage(inputs):
    W_in = np.asarray(inputs["L0_attn_W_in"], dtype=np.float32)
    b_in = np.asarray(inputs["L0_attn_b_in"], dtype=np.float32)
    W_out = np.asarray(inputs["L0_attn_W_out"], dtype=np.float32)
    b_out = np.asarray(inputs["L0_attn_b_out"], dtype=np.float32)
    gate_w = np.asarray(inputs["L0_ffn_gate_w"], dtype=np.float32)
    ew = np.ascontiguousarray(np.asarray(inputs["L0_ffn_expert_w"],
                                         dtype=np.float32))
    eb = np.ascontiguousarray(np.asarray(inputs["L0_ffn_expert_b"],
                                         dtype=np.float32))

    Q_, K_, V_ = W_in[0:2048], W_in[2048:4096], W_in[4096:6144]
    bQ, bK, bV = b_in[0:2048], b_in[2048:4096], b_in[4096:6144]
    w1 = np.ascontiguousarray(
        np.concatenate([K_[_PERM], V_[_PERM], Q_[_PERM]], axis=0).T)
    b1 = np.ascontiguousarray(
        np.concatenate([bK[_PERM], bV[_PERM], bQ[_PERM]])[None, :])
    wo = np.ascontiguousarray(W_out.T[_PERM, :])
    bo = np.ascontiguousarray(b_out[None, :])
    gw = np.ascontiguousarray(gate_w.T)

    inv = 1.0 / (10000.0 ** (np.arange(0, 1024, 2, dtype=np.float64) / 1024.0))
    tabs = {}
    for half in range(2):
        pos = np.arange(half * 1024, (half + 1) * 1024, dtype=np.float64)
        ang = pos[:, None] * inv[None, :]
        tabs[half] = (np.cos(ang).astype(np.float32).astype(ml_dtypes.bfloat16),
                      np.sin(ang).astype(np.float32).astype(ml_dtypes.bfloat16))

    eye16 = np.eye(128, dtype=ml_dtypes.bfloat16)
    eye32 = np.eye(128, dtype=np.float32)
    ones16 = np.ones((1, 128), dtype=ml_dtypes.bfloat16)
    konst = np.tile(np.array([[EPS, 1.0]], dtype=np.float32), (128, 1))

    X = np.asarray(inputs["L0_attn_X"], dtype=np.float32)
    in_maps = []
    for c in range(8):
        b, half = c // 2, c % 2
        x = np.ascontiguousarray(X[b, half * 1024:(half + 1) * 1024, :])
        cosT, sinT = tabs[half]
        in_maps.append(dict(x=x, w1=w1, wo=wo, ew=ew, b1=b1, bo=bo, gw=gw,
                            eb=eb, cosT=cosT, sinT=sinT,
                            eye16=eye16, eye32=eye32, ones16=ones16,
                            konst=konst))
    return in_maps


def kernel(**inputs):
    in_maps = _host_stage(inputs)
    nc = build_nc()
    res = run_bass_kernel_spmd(nc, in_maps, core_ids=list(range(8)))
    out = np.empty((B, S, D), dtype=np.float32)
    for c in range(8):
        b, half = c // 2, c % 2
        out[b, half * 1024:(half + 1) * 1024, :] = res.results[c]["out"]
    return out


# version 10
# speedup vs baseline: 1.1183x; 1.0331x over previous
import numpy as np
import ml_dtypes
from contextlib import ExitStack

import concourse.bass as bass
import concourse.mybir as mybir
from concourse.bass_utils import run_bass_kernel_spmd
from concourse.hw_specs import get_activation_tables

F32 = mybir.dt.float32
BF16 = mybir.dt.bfloat16
AF = mybir.ActivationFunctionType
ALU = mybir.AluOpType
AX = mybir.AxisListType.X

B, S, D = 4, 2048, 2048
TPC = 1024
NTC = 8
NDC = 16
NE = 8
EPS = 1e-5
BIG = 1.0e30

ROUNDS = []
for r in range(16):
    ROUNDS.append(dict(kind=("K", "V", "Q", "WO")[r // 4], piece=r, n=r % 4))
for n in range(4):
    for e in range(NE):
        ROUNDS.append(dict(kind="EXP", piece=16 + n * 8 + e, n=n, e=e))
    ROUNDS.append(dict(kind="BIAS", piece=None, n=n))
NR = len(ROUNDS)
NPIECE = 48
_DEBUG = False
_OPT_PREPASS = True
_OPT_LADDER = True


def build_nc():
    nc = bass.Bass("TRN2", target_bir_lowering=False,
                   detect_race_conditions=False)

    x_h = nc.dram_tensor("x", [TPC, D], F32, kind="ExternalInput").ap()
    w1_h = nc.dram_tensor("w1", [D, 6144], F32, kind="ExternalInput").ap()
    wo_h = nc.dram_tensor("wo", [D, D], F32, kind="ExternalInput").ap()
    ew_h = nc.dram_tensor("ew", [NE, D, D], F32, kind="ExternalInput").ap()
    b1_h = nc.dram_tensor("b1", [1, 6144], F32, kind="ExternalInput").ap()
    bo_h = nc.dram_tensor("bo", [1, D], F32, kind="ExternalInput").ap()
    gw_h = nc.dram_tensor("gw", [D, 8], F32, kind="ExternalInput").ap()
    eb_h = nc.dram_tensor("eb", [NE, D], F32, kind="ExternalInput").ap()
    cos_h = nc.dram_tensor("cosT", [TPC, 512], BF16, kind="ExternalInput").ap()
    sin_h = nc.dram_tensor("sinT", [TPC, 512], BF16, kind="ExternalInput").ap()
    eye16_h = nc.dram_tensor("eye16", [128, 128], BF16, kind="ExternalInput").ap()
    eye32_h = nc.dram_tensor("eye32", [128, 128], F32, kind="ExternalInput").ap()
    ones_h = nc.dram_tensor("ones16", [1, 128], BF16, kind="ExternalInput").ap()
    konst_h = nc.dram_tensor("konst", [128, 2], F32, kind="ExternalInput").ap()
    out_h = nc.dram_tensor("out", [TPC, D], F32, kind="ExternalOutput").ap()

    cc_in = nc.dram_tensor("cc_in", [128, 16], F32)
    cc_out = nc.dram_tensor("cc_out", [128, 16], F32)
    dbg = {}
    if _DEBUG:
        for nm, shp, dt_ in (("d_x1", [128, 16384], BF16),
                             ("d_qt", [128, 16384], BF16),
                             ("d_h2t", [128, 16384], BF16),
                             ("d_acc", [128, 4096], F32),
                             ("d_g", [128, 64], F32),
                             ("d_acc2", [128, 16], F32),
                             ("d_stats", [128, 64], F32),
                             ("d_gt", [8, 1024], BF16),
                             ("d_eb", [8, 2048], BF16),
                             ("d_lf", [128, 64], F32)):
            dbg[nm] = nc.dram_tensor(nm, shp, dt_, kind="ExternalOutput").ap()

    def hbm_piece(p):
        if p < 12:
            src = w1_h[:, p * 512:(p + 1) * 512]
        elif p < 16:
            src = wo_h[:, (p - 12) * 512:(p - 11) * 512]
        else:
            k = p - 16
            n, e = k // 8, k % 8
            src = ew_h[e, :, n * 512:(n + 1) * 512]
        return src.rearrange("(c p) n -> p c n", p=128)

    with ExitStack() as ctx:
        al = ctx.enter_context
        buf_a = al(nc.sbuf_tensor("buf_a", [128, 16384], BF16))
        buf_b = al(nc.sbuf_tensor("buf_b", [128, 16384], BF16))
        buf_u = al(nc.sbuf_tensor("buf_u", [128, 16384], BF16))
        rs = [al(nc.sbuf_tensor(f"ring{i}", [128, 8192], BF16)) for i in range(3)]
        buf_f = al(nc.sbuf_tensor("buf_f", [128, 4096], F32))
        htmp = [al(nc.sbuf_tensor(f"htmp{i}", [128, 2048], BF16)) for i in range(2)]
        sqtr = al(nc.sbuf_tensor("sqtr", [128, 2048], BF16))
        st_f = al(nc.sbuf_tensor("st_f", [128, 64], F32))
        l_f = al(nc.sbuf_tensor("l_f", [128, 64], F32))
        e1_f = al(nc.sbuf_tensor("e1_f", [128, 64], F32))
        gs_f = al(nc.sbuf_tensor("gs_f", [128, 32], F32))
        g_f = al(nc.sbuf_tensor("g_f", [128, 64], F32))
        gbf = al(nc.sbuf_tensor("gbf", [128, 64], BF16))
        gT_sb = al(nc.sbuf_tensor("gT_sb", [8, 1024], BF16))
        l8_sb = al(nc.sbuf_tensor("l8_sb", [8, 128], F32))
        a_fm = al(nc.sbuf_tensor("a_fm", [128, 16], F32))
        a_cc = al(nc.sbuf_tensor("a_cc", [128, 16], F32))
        b1_sb = al(nc.sbuf_tensor("b1_sb", [1, 6144], BF16))
        bo_sb = al(nc.sbuf_tensor("bo_sb", [1, 2048], BF16))
        gw_sb = al(nc.sbuf_tensor("gw_sb", [128, 128], BF16))
        eb_sb = al(nc.sbuf_tensor("eb_sb", [8, 2048], BF16))
        eye16 = al(nc.sbuf_tensor("eye16s", [128, 128], BF16))
        eye32 = al(nc.sbuf_tensor("eye32s", [128, 128], F32))
        ones_sb = al(nc.sbuf_tensor("ones_sb", [1, 128], BF16))
        konst_sb = al(nc.sbuf_tensor("konst_sb", [128, 2], F32))
        ps_m = al(nc.psum_tensor("ps_m", [128, 2048], F32))
        ps_t = [al(nc.psum_tensor(f"ps_t{i}", [128, 1024], BF16)) for i in range(2)]
        ps_x = [al(nc.psum_tensor(f"ps_x{i}", [128, 512], F32)) for i in range(2)]
        sm = {k: al(nc.semaphore(f"sem_{k}"))
              for k in ("t", "v", "s", "o", "c", "yc", "y0", "y1", "y2",
                        "y3", "yt0", "yt1", "w0", "w1", "w2", "wa")}
        block = al(nc.Block())

        def h1T(dc, tc):
            return buf_a[:, dc * 1024 + tc * 128: dc * 1024 + (tc + 1) * 128]

        def x1v(tc, c0, c1):
            return buf_a[:, tc * 2048 + c0: tc * 2048 + c1]

        def kall(tc, c0, c1):
            return buf_b[:, tc * 2048 + c0: tc * 2048 + c1]

        def qT(dc, tc):
            return buf_b[:, dc * 1024 + tc * 128: dc * 1024 + (tc + 1) * 128]

        def qTblk(dc):
            return buf_b[:, dc * 1024:(dc + 1) * 1024]

        def h2T(dc, tc):
            return buf_u[:, dc * 1024 + tc * 128: dc * 1024 + (tc + 1) * 128]

        def tab_cos(j):
            return buf_u[:, j * 1024: j * 1024 + 512]

        def tab_sin(j):
            return buf_u[:, j * 1024 + 512:(j + 1) * 1024]

        def qtileA(tc):
            return buf_u[:, 2048 + tc * 512: 2048 + (tc + 1) * 512]

        def qslot(j):
            return buf_u[:, 6144 + j * 512: 6144 + (j + 1) * 512]

        def ropeout(j):
            return buf_u[:, 7168 + j * 1024: 7168 + (j + 1) * 1024]

        def vslot(j):
            return buf_u[:, 9216 + j * 512: 9216 + (j + 1) * 512]

        prodslot = buf_u[:, 11264:11776]

        def qtmp(j):
            return buf_u[:, 12288 + j * 512: 12288 + (j + 1) * 512]
        apart = buf_f[:, 0:2048]

        def xtile(j):
            base = 2048 + j * 1024 if j < 2 else (j - 2) * 1024
            return buf_f[:, base: base + 1024]

        def accv(tc):
            return buf_f[:, tc * 512:(tc + 1) * 512]

        def stc(tc, k):
            return st_f[:, tc * 8 + k: tc * 8 + k + 1]

        def bank(g):
            return ps_m[:, (g % 4) * 512:((g % 4) + 1) * 512]

        q = {k: [] for k in ("t", "v", "s", "y", "g")}
        ct = {k: 0 for k in ("t", "v", "s", "o", "c", "yc", "y0", "y1",
                             "y2", "y3", "yt0", "yt1", "w0", "w1", "w2",
                             "wa")}
        ENG = {"t": nc.tensor, "v": nc.vector, "s": nc.scalar,
               "y": nc.sync, "g": nc.gpsimd}

        def emit(e_, fn):
            q[e_].append(fn)

        wmax = {}

        def wait(e_, semk, val):
            wmax[semk] = max(wmax.get(semk, 0), val)
            emit(e_, lambda eng=ENG[e_], s_=sm[semk], v_=val: eng.wait_ge(s_, v_))

        def inc(semk, amt=1):
            ct[semk] += amt
            return ct[semk]

        Gseq = []
        mm_led = {}
        evac_led = {}
        tb_led = []
        tb_evac = []
        h1T_rdy = {}
        h1half = {}
        ln1 = {}
        ropeK_done = {}
        ropeQ_done = {}
        tabK = {}
        tabQ = {}
        rn_led = {}
        vprod_led = {}
        prod_cnt = [0]
        qslot_rel = {}
        q9tr = {}
        q9ev = {}
        qtr_ev = {}
        fold_led = {}
        red_done_led = {}
        x1ev = {}
        x2load = {}
        xh_led = {}
        a_tr_led = [0]
        h2b_led = {}
        h2T_rdy = {}
        logit_led = {}
        l8_led = {}
        ltr_led = {}
        lf_led = {}
        exp_led = {}
        gate_led = {}
        gT_led = {}
        out_led = {}
        w_led = {}
        a_dma_y = [None]
        acc_back_w = [None]
        piece_round = {}
        for r_, rd_ in enumerate(ROUNDS):
            if rd_["piece"] is not None:
                piece_round[rd_["piece"]] = r_

        def gk(r, tc):
            return r * 1000 + tc

        tabsets = get_activation_tables(nc.m.arch)
        set_id = list(tabsets).index("natural_log_exp_and_others")
        emit("s", lambda: nc.scalar.add_instruction(
            mybir.InstLoadActFuncSet(
                name=nc.get_next_instruction_name(),
                act_func_set_id=set_id, ins=[], outs=[])))

        nc.const_aps.register(F32, EPS, konst_sb[:, 0:1])
        nc.const_aps.register(F32, 1.0, konst_sb[:, 1:2])
        for dst, src in ((eye16[:, :], eye16_h), (eye32[:, :], eye32_h),
                         (ones_sb[:, :], ones_h), (konst_sb[:, :], konst_h)):
            inc("yc", 16)
            emit("y", lambda d=dst, s_=src:
                 nc.sync.dma_start(out=d, in_=s_).then_inc(sm["yc"], 16))
        const_y = ct["yc"]
        wait("s", "yc", const_y)
        for dst, src in (
                (b1_sb[:, :], b1_h), (bo_sb[:, :], bo_h),
                (gw_sb[:, :].rearrange("p (c n) -> p c n", n=8),
                 gw_h.rearrange("(c p) n -> p c n", p=128)),
                (eb_sb[:, :], eb_h)):
            inc("wa", 16)
            emit("g", lambda d=dst, s_=src:
                 nc.gpsimd.dma_start(out=d, in_=s_).then_inc(sm["wa"], 16))
        const_w = ct["wa"]

        def prepass_head(tc):
            for h in range(2):
                k = 2 * tc + h
                if k >= 4:
                    wait("y", "v", 2 * (tc - 2) + h + 1)
                    wait("y", "s", ln1[tc - 2])
                src = x_h[tc * 128:(tc + 1) * 128, h * 1024:(h + 1) * 1024]
                ys = f"y{k % 4}"
                inc(ys, 16)
                emit("y", lambda d=xtile(k % 4), s_=src, m_=sm[ys]:
                     nc.sync.dma_start(out=d, in_=s_).then_inc(m_, 16))
                wait("s", ys, ct[ys])
                xt = xtile(k % 4)
                emit("s", lambda o=sqtr[:, 0:1024], i=xt, a=stc(tc, 2 * h):
                     nc.scalar.activation(o, i, AF.Copy, accum_out=a))
                emit("s", lambda o=sqtr[:, 1024:2048], i=xt, a=stc(tc, 2 * h + 1):
                     nc.scalar.activation(o, i, AF.Square, accum_out=a))
            M, R, S1, S2 = stc(tc, 4), stc(tc, 5), stc(tc, 6), stc(tc, 7)
            t0, t1 = gs_f[:, 0:1], gs_f[:, 1:2]
            emit("s", lambda o=t0, i=stc(tc, 0), a=stc(tc, 2):
                 nc.scalar.add(o, i, a))
            emit("s", lambda o=M, i=t0: nc.scalar.mul(o, i, 1.0 / 2048.0))
            emit("s", lambda o=t1, i=M: nc.scalar.activation(o, i, AF.Square))
            emit("s", lambda o=t1, i=t1: nc.scalar.mul(o, i, -1.0))
            emit("s", lambda o=t0, i=stc(tc, 1), a=stc(tc, 3):
                 nc.scalar.add(o, i, a))
            emit("s", lambda o=t0, i=t0: nc.scalar.mul(o, i, 1.0 / 2048.0))
            emit("s", lambda o=t0, i=t0, a=t1: nc.scalar.add(o, i, a))
            emit("s", lambda o=t0, i=t0: nc.scalar.add(o, i, EPS))
            emit("s", lambda o=t0, i=t0: nc.scalar.activation(o, i, AF.Ln))
            emit("s", lambda o=R, i=t0:
                 nc.scalar.activation(o, i, AF.Exp, scale=-0.5))
            emit("s", lambda o=S1, i=R: nc.scalar.add(o, i, 1.0))
            emit("s", lambda o=S2, i=M, a=R:
                 nc.scalar.mul(o, i, a).then_inc(sm["s"], 1))
            ln1[tc] = inc("s")

        def prepass_tail(tc):
            M, R = stc(tc, 4), stc(tc, 5)
            if tc >= 2:
                wait("v", "t", tb_led[2 * (tc - 2) + 1])
            wait("v", "s", ln1[tc])
            for h in range(2):
                xt = xtile((2 * tc + h) % 4)
                dst = htmp[tc % 2][:, h * 1024:(h + 1) * 1024]
                emit("v", lambda o=dst, i=xt, m=M, r_=R:
                     nc.vector.tensor_scalar(o, i, m, r_, ALU.subtract, ALU.mult)
                     .then_inc(sm["v"], 1))
                h1half[tc, h] = inc("v")
                assert h1half[tc, h] == 2 * tc + h + 1, "x-load pacing broke"
            if tc == NTC - 1:
                emit("v", lambda: nc.vector.memset(apart, 0.0))

            for b_ in range(2):
                gb = len(tb_led)
                if gb == 0:
                    wait("t", "yc", const_y)
                wait("t", "v", h1half[tc, b_])
                if gb >= 2:
                    wait("t", "s", tb_evac[gb - 2])
                pt = ps_t[gb % 2]
                for j in range(8):
                    src = htmp[tc % 2][:, (b_ * 8 + j) * 128:(b_ * 8 + j + 1) * 128]
                    last = j == 7
                    emit("t", lambda o=pt[:, j * 128:(j + 1) * 128], i=src,
                         id_=eye16[:, :], l=last:
                         (nc.tensor.transpose(o, i, id_).then_inc(sm["t"], 1)
                          if l else nc.tensor.transpose(o, i, id_)))
                tb_led.append(inc("t"))
                wait("s", "t", tb_led[gb])
                dstv = buf_a[:, :].rearrange("p (c x) -> p c x", x=1024)[
                    :, b_ * 8:(b_ + 1) * 8, tc * 128:(tc + 1) * 128]
                srcv = pt[:, :].rearrange("p (c x) -> p c x", x=128)
                emit("s", lambda o=dstv, i=srcv:
                     nc.scalar.copy(o, i).then_inc(sm["s"], 1))
                tb_evac.append(inc("s"))
            h1T_rdy[tc] = tb_evac[-1]

        if _OPT_PREPASS:
            for _tc in range(4):
                prepass_head(_tc)
            for _tc in range(4, NTC):
                prepass_tail(_tc - 4)
                prepass_head(_tc)
        else:
            for _tc in range(NTC):
                prepass_head(_tc)
                prepass_tail(_tc)

        def emit_tab(tc, ledger, hold_sem=None):
            j = tc % 2
            if hold_sem is not None:
                wait("y", hold_sem[0], hold_sem[1])
            ys = f"yt{j}"
            for hsrc, dst in ((cos_h, tab_cos(j)), (sin_h, tab_sin(j))):
                src = hsrc[tc * 128:(tc + 1) * 128, :]
                inc(ys, 16)
                emit("y", lambda d=dst, s_=src, m_=sm[ys]:
                     nc.sync.dma_start(out=d, in_=s_).then_inc(m_, 16))
            ledger[tc] = (ys, ct[ys])

        emit_tab(0, tabK)
        emit_tab(1, tabK)

        gp_state = [0]

        def emit_collective_trigger():
            wait("g", "yc", a_dma_y[0])
            emit("g", lambda: nc.gpsimd.collective_compute(
                "AllReduce", ALU.add,
                replica_groups=[[0, 1], [2, 3], [4, 5], [6, 7]],
                ins=[cc_in.ap()], outs=[cc_out.ap()]).then_inc(sm["c"], 1))
            inc("c")

        def emit_collective_fetch():
            wait("g", "c", ct["c"])
            inc("wa", 16)
            emit("g", lambda: nc.gpsimd.dma_start(out=a_cc[:, :], in_=cc_out[:, :])
                 .then_inc(sm["wa"], 16))
            acc_back_w[0] = ct["wa"]

        def pump_pieces(upto):
            while gp_state[0] <= upto and gp_state[0] < NPIECE:
                p = gp_state[0]
                if p >= 3:
                    rprev = piece_round[p - 3]
                    wait("g", "t", mm_led[gk(rprev, NTC - 1)])
                slot = rs[p % 3]
                ws = f"w{p % 3}"
                dstv = slot[:, :].rearrange("p (c n) -> p c n", n=512)
                inc(ws, 16)
                emit("g", lambda d=dstv, s_=hbm_piece(p), m_=sm[ws]:
                     nc.gpsimd.dma_start(out=d, in_=s_).then_inc(m_, 16))
                w_led[p] = (ws, ct[ws])
                gp_state[0] += 1
                if gp_state[0] == 10:
                    emit_collective_trigger()
                if gp_state[0] == 13:
                    emit_collective_fetch()

        pump_pieces(2)

        def emit_group(r, tc):
            rd = ROUNDS[r]
            G = gk(r, tc)
            gidx = len(Gseq)
            Gseq.append(G)
            bk = bank(gidx)
            if r == 0 and tc == 0:
                wait("t", "wa", const_w)
                wait("t", "yc", const_y)
            if rd["piece"] is not None and tc == 0:
                wait("t", *w_led[rd["piece"]])
            if gidx >= 4:
                semk, cnt = evac_led[Gseq[gidx - 4]]
                wait("t", semk, cnt)
            if r == 0:
                wait("t", "s", h1T_rdy[tc])
            if rd["kind"] == "EXP" and rd["e"] == 0 and rd["n"] == 0:
                wait("t", "s", h2T_rdy[tc])
            if rd["kind"] == "BIAS":
                wait("t", "s", gT_led[tc])
                emit("t", lambda o=bk, l=gT_sb[0:8, tc * 128:(tc + 1) * 128],
                     rh=eb_sb[0:8, rd["n"] * 512:(rd["n"] + 1) * 512]:
                     nc.tensor.matmul(o, l, rh, start=True, stop=True)
                     .then_inc(sm["t"], 1))
                mm_led[G] = inc("t")
                return gidx
            slot = rs[rd["piece"] % 3]
            started = False
            if rd["kind"] in ("K", "V", "Q", "WO"):
                brow = (b1_sb[0:1, rd["piece"] * 512:(rd["piece"] + 1) * 512]
                        if rd["kind"] != "WO"
                        else bo_sb[0:1, rd["n"] * 512:(rd["n"] + 1) * 512])
                emit("t", lambda o=bk, l=ones_sb[0:1, :], rh=brow:
                     nc.tensor.matmul(o, l, rh, start=True, stop=False))
                started = True
            for dc in range(NDC):
                if rd["kind"] == "WO":
                    if r == 12 and tc == 0:
                        wait("t", "v", fold_led[dc])
                    lhs = qT(dc, tc)
                elif rd["kind"] == "EXP":
                    lhs = h2T(dc, tc)
                else:
                    lhs = h1T(dc, tc)
                rh = slot[:, dc * 512:(dc + 1) * 512]
                st_ = (not started) and dc == 0
                sp = dc == NDC - 1
                emit("t", lambda o=bk, l=lhs, rr=rh, a=st_, b_=sp:
                     (nc.tensor.matmul(o, l, rr, start=a, stop=b_)
                      .then_inc(sm["t"], 1) if b_ else
                      nc.tensor.matmul(o, l, rr, start=a, stop=b_)))
            mm_led[G] = inc("t")
            return gidx

        def evac_K(r, tc, gidx):
            G = gk(r, tc)
            bk = bank(gidx)
            wait("s", "t", mm_led[G])
            emit("s", lambda o=kall(tc, r * 512, (r + 1) * 512), i=bk:
                 nc.scalar.copy(o, i).then_inc(sm["s"], 1))
            evac_led[G] = ("s", inc("s"))
            if r == 1:
                wait("v", "s", evac_led[G][1])
                wait("v", *tabK[tc])
                j = tc % 2
                Ev, Ov = kall(tc, 0, 512), kall(tc, 512, 1024)
                tE, tO = qtmp(2 * j), qtmp(2 * j + 1)
                emit("v", lambda o=tE, a=Ev, b_=tab_sin(j): nc.vector.tensor_mul(o, a, b_))
                emit("v", lambda o=tO, a=Ov, b_=tab_sin(j): nc.vector.tensor_mul(o, a, b_))
                emit("v", lambda o=Ev, a=Ev, b_=tab_cos(j): nc.vector.tensor_mul(o, a, b_))
                emit("v", lambda o=Ov, a=Ov, b_=tab_cos(j): nc.vector.tensor_mul(o, a, b_))
                emit("v", lambda o=Ev, a=Ev, b_=tO: nc.vector.tensor_sub(o, a, b_))
                emit("v", lambda o=Ov, a=Ov, b_=tE:
                     nc.vector.tensor_add(o, a, b_).then_inc(sm["v"], 1))
                ropeK_done[tc] = inc("v")
                if tc + 2 < NTC:
                    emit_tab(tc + 2, tabK, ("v", ropeK_done[tc]))
            if r == 3:
                wait("s", "v", ropeK_done[tc])
                ssq, rn = stc(tc, 0), stc(tc, 1)
                emit("s", lambda o=sqtr[:, :], i=kall(tc, 0, 2048), a=ssq:
                     nc.scalar.activation(o, i, AF.Square, accum_out=a))
                emit("s", lambda o=ssq, i=ssq: nc.scalar.activation(o, i, AF.Ln))
                emit("s", lambda o=rn, i=ssq:
                     nc.scalar.activation(o, i, AF.Exp, scale=-0.5)
                     .then_inc(sm["s"], 1))
                rn_led[tc] = inc("s")

        def evac_V(r, tc, gidx):
            G = gk(r, tc)
            bk = bank(gidx)
            vj = vslot(gidx % 2)
            wait("s", "t", mm_led[G])
            if gidx >= 2 and Gseq[gidx - 2] in vprod_led:
                wait("s", "v", vprod_led[Gseq[gidx - 2]])
            emit("s", lambda o=vj, i=bk, sc=stc(tc, 1):
                 nc.scalar.mul(o, i, sc).then_inc(sm["s"], 1))
            evac_led[G] = ("s", inc("s"))
            c0 = (r - 4) * 512
            wait("v", "s", evac_led[G][1])
            emit("v", lambda o=prodslot, a=kall(tc, c0, c0 + 512), b_=vj:
                 nc.vector.tensor_mul(o, a, b_).then_inc(sm["v"], 1))
            pcnt = inc("v")
            wait("v", "v", pcnt)
            emit("v", lambda o=apart[:, c0:c0 + 512], a=apart[:, c0:c0 + 512],
                 b_=prodslot:
                 nc.vector.tensor_add(o, a, b_).then_inc(sm["v"], 1))
            vprod_led[G] = inc("v")
            prod_cnt[0] = vprod_led[G]

        def evac_Q(r, tc, gidx):
            G = gk(r, tc)
            bk = bank(gidx)
            wait("s", "t", mm_led[G])
            if r == 8 and tc == 0:
                wait("s", "v", prod_cnt[0])
            if r == 8:
                dst = qtileA(tc)
            else:
                if gidx >= 2:
                    prevG = Gseq[gidx - 2]
                    if prevG in qslot_rel:
                        semk, cnt = qslot_rel[prevG]
                        wait("s", semk, cnt)
                dst = qslot(gidx % 2)
            emit("s", lambda o=dst, i=bk: nc.scalar.copy(o, i).then_inc(sm["s"], 1))
            evac_led[G] = ("s", inc("s"))
            if r == 9:
                wait("v", "s", evac_led[G][1])
                wait("v", *tabQ[tc])
                if tc >= 2:
                    wait("v", "t", q9tr[tc - 2])
                j = tc % 2
                Ev, Ov = qtileA(tc), dst
                RO = ropeout(tc % 2)
                Eo, Oo = RO[:, 0:512], RO[:, 512:1024]
                sp1, sp2 = qtmp(2 * j), qtmp(2 * j + 1)
                emit("v", lambda o=sp1, a=Ev, b_=tab_sin(j): nc.vector.tensor_mul(o, a, b_))
                emit("v", lambda o=sp2, a=Ov, b_=tab_sin(j): nc.vector.tensor_mul(o, a, b_))
                emit("v", lambda o=Eo, a=Ev, b_=tab_cos(j): nc.vector.tensor_mul(o, a, b_))
                emit("v", lambda o=Oo, a=Ov, b_=tab_cos(j): nc.vector.tensor_mul(o, a, b_))
                emit("v", lambda o=Eo, a=Eo, b_=sp2: nc.vector.tensor_sub(o, a, b_))
                emit("v", lambda o=Oo, a=Oo, b_=sp1:
                     nc.vector.tensor_add(o, a, b_).then_inc(sm["v"], 1))
                ropeQ_done[tc] = inc("v")
                qslot_rel[G] = ("v", ropeQ_done[tc])
                if tc + 2 < NTC:
                    emit_tab(tc + 2, tabQ, ("v", ropeQ_done[tc]))

        def q_transposes(r, tc, gidx_of_group):
            gb = len(tb_led)
            pt = ps_t[gb % 2]
            if gb >= 2:
                wait("t", "s", tb_evac[gb - 2])
            if r == 9:
                wait("t", "v", ropeQ_done[tc])
                srcbase = ropeout(tc % 2)
                nchunks, dc0 = 8, 0
            else:
                G = gk(r, tc)
                wait("t", "s", evac_led[G][1])
                srcbase = qslot(gidx_of_group % 2)
                nchunks, dc0 = 4, 8 + (r - 10) * 4
            for j in range(nchunks):
                src = srcbase[:, j * 128:(j + 1) * 128]
                last = j == nchunks - 1
                emit("t", lambda o=pt[:, j * 128:(j + 1) * 128], i=src,
                     id_=eye16[:, :], l=last:
                     (nc.tensor.transpose(o, i, id_).then_inc(sm["t"], 1)
                      if l else nc.tensor.transpose(o, i, id_)))
            tb_led.append(inc("t"))
            if r == 9:
                q9tr[tc] = tb_led[-1]
            else:
                qslot_rel[gk(r, tc)] = ("t", tb_led[-1])
            wait("s", "t", tb_led[gb])
            dstv = buf_b[:, :].rearrange("p (c x) -> p c x", x=1024)[
                :, dc0:dc0 + nchunks, tc * 128:(tc + 1) * 128]
            srcv = pt[:, 0:nchunks * 128].rearrange("p (c x) -> p c x", x=128)
            emit("s", lambda o=dstv, i=srcv:
                 nc.scalar.copy(o, i).then_inc(sm["s"], 1))
            tb_evac.append(inc("s"))
            qtr_ev[r, tc] = tb_evac[-1]

        def evac_WO(r, tc, gidx):
            G = gk(r, tc)
            bk = bank(gidx)
            n = r - 12
            wait("v", "t", mm_led[G])
            xv = x1v(tc, n * 512, (n + 1) * 512)
            emit("v", lambda o=xv, a=bk, b_=xv:
                 nc.vector.tensor_add(o, a, b_).then_inc(sm["v"], 1))
            evac_led[G] = ("v", inc("v"))
            if r == 15:
                x1ev[tc] = evac_led[G][1]

        def evac_EXPBIAS(r, tc, gidx):
            G = gk(r, tc)
            rd = ROUNDS[r]
            bk = bank(gidx)
            n = rd["n"]
            acc = accv(tc)
            wait("v", "t", mm_led[G])
            if rd["kind"] == "BIAS":
                emit("v", lambda o=acc, a=bk, b_=acc:
                     nc.vector.tensor_add(o, a, b_).then_inc(sm["v"], 1))
                evac_led[G] = ("v", inc("v"))
                wait("y", "v", evac_led[G][1])
                dst = out_h[tc * 128:(tc + 1) * 128, n * 512:(n + 1) * 512]
                inc("o", 16)
                emit("y", lambda d=dst, s_=acc:
                     nc.sync.dma_start(out=d, in_=s_).then_inc(sm["o"], 16))
                out_led[n, tc] = ct["o"]
                return
            e = rd["e"]
            gsc = g_f[:, tc * 8 + e: tc * 8 + e + 1]
            if e == 0:
                wait("v", "v", gate_led[tc])
                if n > 0:
                    wait("v", "o", out_led[n - 1, tc])
                xv = x1v(tc, n * 512, (n + 1) * 512)
                emit("v", lambda o=acc, i=bk, sc=gsc, b_=xv:
                     nc.vector.scalar_tensor_tensor(o, i, sc, b_, ALU.mult,
                                                    ALU.add).then_inc(sm["v"], 1))
            else:
                emit("v", lambda o=acc, i=bk, sc=gsc, a=acc:
                     nc.vector.scalar_tensor_tensor(o, i, sc, a, ALU.mult,
                                                    ALU.add).then_inc(sm["v"], 1))
            evac_led[G] = ("v", inc("v"))

        def emit_a_reduce():
            for wv in range(4):
                px = ps_x[wv % 2]
                if wv >= 2:
                    wait("t", "v", red_done_led[wv - 2])
                for j in range(4):
                    c = wv * 4 + j
                    src = apart[:, c * 128:(c + 1) * 128]
                    last = j == 3
                    emit("t", lambda o=px[:, j * 128:(j + 1) * 128], i=src,
                         id_=eye32[:, :], l=last:
                         (nc.tensor.transpose(o, i, id_).then_inc(sm["t"], 1)
                          if l else nc.tensor.transpose(o, i, id_)))
                aw = inc("t")
                a_tr_led[0] = aw
                wait("v", "t", aw)
                for j in range(4):
                    c = wv * 4 + j
                    last = j == 3
                    emit("v", lambda o=a_fm[:, c:c + 1],
                         i=px[:, j * 128:(j + 1) * 128], l=last:
                         (nc.vector.reduce_sum(o, i, axis=AX).then_inc(sm["v"], 1)
                          if l else nc.vector.reduce_sum(o, i, axis=AX)))
                red_done_led[wv] = inc("v")
            wait("y", "v", red_done_led[3])
            inc("yc", 16)
            emit("y", lambda: nc.sync.dma_start(out=cc_in[:, :], in_=a_fm[:, :])
                 .then_inc(sm["yc"], 16))
            a_dma_y[0] = ct["yc"]

        def emit_p2a(tc):
            wait("s", "v", x1ev[tc])
            M2, R2 = stc(tc, 4), stc(tc, 5)
            sA, sB = stc(tc, 0), stc(tc, 1)
            t0, t1 = gs_f[:, 2:3], gs_f[:, 3:4]
            xv = x1v(tc, 0, 2048)
            emit("s", lambda o=sqtr[:, :], i=xv, a=sA:
                 nc.scalar.activation(o, i, AF.Copy, accum_out=a))
            emit("s", lambda o=sqtr[:, :], i=xv, a=sB:
                 nc.scalar.activation(o, i, AF.Square, accum_out=a))
            emit("s", lambda o=M2, i=sA: nc.scalar.mul(o, i, 1.0 / 2048.0))
            emit("s", lambda o=t1, i=M2: nc.scalar.activation(o, i, AF.Square))
            emit("s", lambda o=t1, i=t1: nc.scalar.mul(o, i, -1.0))
            emit("s", lambda o=t0, i=sB: nc.scalar.mul(o, i, 1.0 / 2048.0))
            emit("s", lambda o=t0, i=t0, a=t1: nc.scalar.add(o, i, a))
            emit("s", lambda o=t0, i=t0: nc.scalar.add(o, i, EPS))
            emit("s", lambda o=t0, i=t0: nc.scalar.activation(o, i, AF.Ln))
            emit("s", lambda o=R2, i=t0:
                 nc.scalar.activation(o, i, AF.Exp, scale=-0.5)
                 .then_inc(sm["s"], 1))
            ln2c = inc("s")
            if tc >= 2:
                wait("v", "t", h2b_led[tc - 2])
            else:
                wait("v", "t", tb_led[13 + 2 * tc])
            wait("v", "s", ln2c)
            emit("v", lambda o=htmp[tc % 2][:, :], i=x1v(tc, 0, 2048), m=M2, r_=R2:
                 nc.vector.tensor_scalar(o, i, m, r_, ALU.subtract, ALU.mult)
                 .then_inc(sm["v"], 1))
            h2ap = inc("v")
            for b_ in range(2):
                gb = len(tb_led)
                pt = ps_t[gb % 2]
                wait("t", "v", h2ap)
                if gb >= 2:
                    wait("t", "s", tb_evac[gb - 2])
                for j in range(8):
                    src = htmp[tc % 2][:, (b_ * 8 + j) * 128:(b_ * 8 + j + 1) * 128]
                    last = j == 7
                    emit("t", lambda o=pt[:, j * 128:(j + 1) * 128], i=src,
                         id_=eye16[:, :], l=last:
                         (nc.tensor.transpose(o, i, id_).then_inc(sm["t"], 1)
                          if l else nc.tensor.transpose(o, i, id_)))
                tb_led.append(inc("t"))
                wait("s", "t", tb_led[gb])
                dstv = buf_u[:, :].rearrange("p (c x) -> p c x", x=1024)[
                    :, b_ * 8:(b_ + 1) * 8, tc * 128:(tc + 1) * 128]
                srcv = pt[:, :].rearrange("p (c x) -> p c x", x=128)
                emit("s", lambda o=dstv, i=srcv:
                     nc.scalar.copy(o, i).then_inc(sm["s"], 1))
                tb_evac.append(inc("s"))
            h2b_led[tc] = tb_led[-1]
            h2T_rdy[tc] = tb_evac[-1]
            if tc == 0:
                wait("t", "v", red_done_led[3])
            else:
                wait("t", "s", l8_led[tc - 1])
            wait("t", "s", h2T_rdy[tc])
            for dc in range(NDC):
                st_, sp = dc == 0, dc == NDC - 1
                emit("t", lambda o=ps_x[0][0:8, 0:128],
                     l=gw_sb[:, dc * 8:(dc + 1) * 8],
                     rh=h2T(dc, tc), a=st_, b_=sp:
                     (nc.tensor.matmul(o, l, rh, start=a, stop=b_)
                      .then_inc(sm["t"], 1) if b_ else
                      nc.tensor.matmul(o, l, rh, start=a, stop=b_)))
            logit_led[tc] = inc("t")
            wait("s", "t", logit_led[tc])
            emit("s", lambda o=l8_sb[:, :], i=ps_x[0][0:8, 0:128]:
                 nc.scalar.copy(o, i).then_inc(sm["s"], 1))
            l8_led[tc] = inc("s")

        def emit_gates(tc):
            wait("t", "s", l8_led[tc])
            if tc == 0:
                wait("t", "v", red_done_led[3])
            else:
                wait("t", "s", lf_led[tc - 1])
            emit("t", lambda o=ps_x[1][:, 0:8], i=l8_sb[:, :], id_=eye32[0:8, 0:8]:
                 nc.tensor.transpose(o, i, id_).then_inc(sm["t"], 1))
            ltr_led[tc] = inc("t")
            wait("s", "t", ltr_led[tc])
            emit("s", lambda o=l_f[:, tc * 8:(tc + 1) * 8], i=ps_x[1][:, 0:8]:
                 nc.scalar.copy(o, i).then_inc(sm["s"], 1))
            lf_led[tc] = inc("s")
            lfv = l_f[:, tc * 8:(tc + 1) * 8]
            m1, m1n, m2 = gs_f[:, 4:5], gs_f[:, 5:6], gs_f[:, 6:7]
            eq = gs_f[:, 8:16]
            l2 = gs_f[:, 16:24]

            def vstep(fn):
                emit("v", lambda f=fn: f().then_inc(sm["v"], 1))
                c = inc("v")
                wait("v", "v", c)
                return c

            wait("v", "s", lf_led[tc])
            vstep(lambda o=m1, i=lfv: nc.vector.reduce_max(o, i, axis=AX))
            vstep(lambda o=m1n, i=lfv:
                  nc.vector.reduce_max(o, i, axis=AX, negate=True))
            vstep(lambda o=eq, i=lfv, sc=m1:
                  nc.vector.tensor_single_scalar(o, i, sc, ALU.is_ge))
            vstep(lambda o=l2, i=eq:
                  nc.vector.tensor_single_scalar(o, i, -BIG, ALU.mult))
            vstep(lambda o=l2, a=l2, b_=lfv: nc.vector.tensor_add(o, a, b_))
            p1 = vstep(lambda o=m2, i=l2: nc.vector.reduce_max(o, i, axis=AX))
            e1 = e1_f[:, tc * 8:(tc + 1) * 8]
            e2m = gs_f[:, 7:8]
            wait("s", "v", p1)
            emit("s", lambda o=e1, i=lfv, b_=m1n:
                 nc.scalar.activation(o, i, AF.Exp, bias=b_))
            emit("s", lambda o=e2m, i=m2, b_=m1n:
                 nc.scalar.activation(o, i, AF.Exp, bias=b_).then_inc(sm["s"], 1))
            exp_led[tc] = inc("s")
            ge2 = gs_f[:, 8:16]
            gu = gs_f[:, 16:24]
            den = gs_f[:, 4:5]
            rden = gs_f[:, 5:6]
            gv = g_f[:, tc * 8:(tc + 1) * 8]
            wait("v", "s", exp_led[tc])
            vstep(lambda o=ge2, i=lfv, sc=m2:
                  nc.vector.tensor_single_scalar(o, i, sc, ALU.is_ge))
            vstep(lambda o=gu, a=e1, b_=ge2: nc.vector.tensor_mul(o, a, b_))
            vstep(lambda o=den, i=e2m: nc.vector.tensor_scalar_add(o, i, 1.0))
            vstep(lambda o=rden, i=den: nc.vector.reciprocal(o, i))
            gate_led[tc] = vstep(lambda o=gv, i=gu, sc=rden:
                                 nc.vector.tensor_single_scalar(o, i, sc, ALU.mult))
            wait("t", "v", gate_led[tc])
            if tc >= 1:
                wait("t", "s", gT_led[tc - 1])
            emit("t", lambda o=ps_x[1][0:8, 128:256], i=g_f[:, tc * 8:(tc + 1) * 8],
                 id_=eye32[:, :]:
                 nc.tensor.transpose(o, i, id_).then_inc(sm["t"], 1))
            gttr = inc("t")
            wait("s", "t", gttr)
            emit("s", lambda o=gT_sb[0:8, tc * 128:(tc + 1) * 128],
                 i=ps_x[1][0:8, 128:256]:
                 nc.scalar.copy(o, i).then_inc(sm["s"], 1))
            gT_led[tc] = inc("s")

        for r, rd in enumerate(ROUNDS):
            if rd["piece"] is not None:
                pump_pieces(rd["piece"])
            if r == 16:
                if _OPT_LADDER:
                    emit_p2a(0)
                    emit_p2a(1)
                    emit_gates(0)
                else:
                    for _tc in range(NTC):
                        emit_p2a(_tc)
                        emit_gates(_tc)

            if r == 12:
                for dc in range(NDC):
                    if dc == 0:
                        wait("v", "wa", acc_back_w[0])
                        wait("v", "s", qtr_ev[11, NTC - 1])
                    emit("v", lambda o=qTblk(dc), i=qTblk(dc),
                         sc=a_cc[:, dc:dc + 1]:
                         nc.vector.tensor_single_scalar(o, i, sc, ALU.mult)
                         .then_inc(sm["v"], 1))
                    fold_led[dc] = inc("v")

            for tc in range(NTC):
                if _OPT_PREPASS and r == 0 and tc + 4 < NTC:
                    prepass_tail(tc + 4)
                if _OPT_LADDER and r == 16:
                    if tc + 2 < NTC:
                        emit_p2a(tc + 2)
                    if tc + 1 < NTC:
                        emit_gates(tc + 1)
                gidx = emit_group(r, tc)
                if rd["kind"] == "K":
                    evac_K(r, tc, gidx)
                elif rd["kind"] == "V":
                    evac_V(r, tc, gidx)
                elif rd["kind"] == "Q":
                    evac_Q(r, tc, gidx)
                    if r in (9, 10, 11) and tc >= 1:
                        q_transposes(r, tc - 1, gidx - 1)
                else:
                    if rd["kind"] == "WO":
                        evac_WO(r, tc, gidx)
                    else:
                        evac_EXPBIAS(r, tc, gidx)
            if r in (9, 10, 11):
                q_transposes(r, NTC - 1, len(Gseq) - 1)

            if r == 7:
                emit_a_reduce()
            if r == 8:
                emit_tab(0, tabQ, ("v", ropeK_done[6]))
                emit_tab(1, tabQ, ("v", ropeK_done[7]))
            if r == 11:
                wait("v", "t", mm_led[gk(11, NTC - 1)])
                for tc in range(NTC):
                    for h in range(2):
                        k = 2 * tc + h
                        if k >= 2:
                            wait("y", "v", xh_led[k - 2])
                        src = x_h[tc * 128:(tc + 1) * 128,
                                  h * 1024:(h + 1) * 1024]
                        ys = f"y{k % 2}"
                        inc(ys, 16)
                        emit("y", lambda d=xtile(k % 2), s_=src, m_=sm[ys]:
                             nc.sync.dma_start(out=d, in_=s_)
                             .then_inc(m_, 16))
                        x2load[tc, h] = (ys, ct[ys])
                        wait("v", *x2load[tc, h])
                        emit("v", lambda o=x1v(tc, h * 1024, (h + 1) * 1024),
                             i=xtile(k % 2), a=stc(tc, 6), b_=stc(tc, 7):
                             nc.vector.tensor_scalar(o, i, a, b_, ALU.mult,
                                                     ALU.subtract)
                             .then_inc(sm["v"], 1))
                        xh_led[k] = inc("v")

        if _DEBUG:
            wait("y", "v", ct["v"])
            wait("y", "s", ct["s"])
            wait("y", "t", ct["t"])
            for nm, src in (("d_x1", buf_a[:, :]), ("d_qt", buf_b[:, :]),
                            ("d_h2t", buf_u[:, :]), ("d_acc", buf_f[:, :]),
                            ("d_g", g_f[:, :]), ("d_acc2", a_cc[:, :]),
                            ("d_stats", st_f[:, :]), ("d_gt", gT_sb[:, :]),
                            ("d_eb", eb_sb[:, :]), ("d_lf", l_f[:, :])):
                inc("o", 16)
                emit("y", lambda d=dbg[nm], s_=src:
                     nc.sync.dma_start(out=d, in_=s_).then_inc(sm["o"], 16))
        emit("y", lambda v_=ct["o"]: nc.sync.wait_ge(sm["o"], v_))
        for _k, _v in wmax.items():
            assert _v <= ct[_k], f"wait on sem {_k} for {_v} but max count {ct[_k]}"


        @block.tensor
        def _(eng):
            for f in q["t"]:
                f()

        @block.vector
        def _(eng):
            for f in q["v"]:
                f()

        @block.scalar
        def _(eng):
            for f in q["s"]:
                f()

        @block.sync
        def _(eng):
            for f in q["y"]:
                f()

        @block.gpsimd
        def _(eng):
            for f in q["g"]:
                f()

    return nc


_PERM = np.concatenate([np.arange(0, 1024, 2), np.arange(1, 1024, 2),
                        np.arange(1024, 2048)])


def _host_stage(inputs):
    W_in = np.asarray(inputs["L0_attn_W_in"], dtype=np.float32)
    b_in = np.asarray(inputs["L0_attn_b_in"], dtype=np.float32)
    W_out = np.asarray(inputs["L0_attn_W_out"], dtype=np.float32)
    b_out = np.asarray(inputs["L0_attn_b_out"], dtype=np.float32)
    gate_w = np.asarray(inputs["L0_ffn_gate_w"], dtype=np.float32)
    ew = np.ascontiguousarray(np.asarray(inputs["L0_ffn_expert_w"],
                                         dtype=np.float32))
    eb = np.ascontiguousarray(np.asarray(inputs["L0_ffn_expert_b"],
                                         dtype=np.float32))

    Q_, K_, V_ = W_in[0:2048], W_in[2048:4096], W_in[4096:6144]
    bQ, bK, bV = b_in[0:2048], b_in[2048:4096], b_in[4096:6144]
    w1 = np.ascontiguousarray(
        np.concatenate([K_[_PERM], V_[_PERM], Q_[_PERM]], axis=0).T)
    b1 = np.ascontiguousarray(
        np.concatenate([bK[_PERM], bV[_PERM], bQ[_PERM]])[None, :])
    wo = np.ascontiguousarray(W_out.T[_PERM, :])
    bo = np.ascontiguousarray(b_out[None, :])
    gw = np.ascontiguousarray(gate_w.T)

    inv = 1.0 / (10000.0 ** (np.arange(0, 1024, 2, dtype=np.float64) / 1024.0))
    tabs = {}
    for half in range(2):
        pos = np.arange(half * 1024, (half + 1) * 1024, dtype=np.float64)
        ang = pos[:, None] * inv[None, :]
        tabs[half] = (np.cos(ang).astype(np.float32).astype(ml_dtypes.bfloat16),
                      np.sin(ang).astype(np.float32).astype(ml_dtypes.bfloat16))

    eye16 = np.eye(128, dtype=ml_dtypes.bfloat16)
    eye32 = np.eye(128, dtype=np.float32)
    ones16 = np.ones((1, 128), dtype=ml_dtypes.bfloat16)
    konst = np.tile(np.array([[EPS, 1.0]], dtype=np.float32), (128, 1))

    X = np.asarray(inputs["L0_attn_X"], dtype=np.float32)
    in_maps = []
    for c in range(8):
        b, half = c // 2, c % 2
        x = np.ascontiguousarray(X[b, half * 1024:(half + 1) * 1024, :])
        cosT, sinT = tabs[half]
        in_maps.append(dict(x=x, w1=w1, wo=wo, ew=ew, b1=b1, bo=bo, gw=gw,
                            eb=eb, cosT=cosT, sinT=sinT,
                            eye16=eye16, eye32=eye32, ones16=ones16,
                            konst=konst))
    return in_maps


def kernel(**inputs):
    in_maps = _host_stage(inputs)
    nc = build_nc()
    res = run_bass_kernel_spmd(nc, in_maps, core_ids=list(range(8)))
    out = np.empty((B, S, D), dtype=np.float32)
    for c in range(8):
        b, half = c // 2, c % 2
        out[b, half * 1024:(half + 1) * 1024, :] = res.results[c]["out"]
    return out


# version 11
# speedup vs baseline: 1.1257x; 1.0066x over previous
import numpy as np
import ml_dtypes
from contextlib import ExitStack

import concourse.bass as bass
import concourse.mybir as mybir
from concourse.bass_utils import run_bass_kernel_spmd
from concourse.hw_specs import get_activation_tables

F32 = mybir.dt.float32
BF16 = mybir.dt.bfloat16
AF = mybir.ActivationFunctionType
ALU = mybir.AluOpType
AX = mybir.AxisListType.X

B, S, D = 4, 2048, 2048
TPC = 1024
NTC = 8
NDC = 16
NE = 8
EPS = 1e-5
BIG = 1.0e30

ROUNDS = []
for r in range(16):
    ROUNDS.append(dict(kind=("K", "V", "Q", "WO")[r // 4], piece=r, n=r % 4))
for n in range(4):
    for e in range(NE):
        ROUNDS.append(dict(kind="EXP", piece=16 + n * 8 + e, n=n, e=e))
    ROUNDS.append(dict(kind="BIAS", piece=None, n=n))
NR = len(ROUNDS)
NPIECE = 48
_DEBUG = False
_OPT_PREPASS = True
_OPT_LADDER = True


def build_nc():
    nc = bass.Bass("TRN2", target_bir_lowering=False,
                   detect_race_conditions=False)

    x_h = nc.dram_tensor("x", [TPC, D], F32, kind="ExternalInput").ap()
    w1_h = nc.dram_tensor("w1", [D, 6144], F32, kind="ExternalInput").ap()
    wo_h = nc.dram_tensor("wo", [D, D], F32, kind="ExternalInput").ap()
    ew_h = nc.dram_tensor("ew", [NE, D, D], F32, kind="ExternalInput").ap()
    b1_h = nc.dram_tensor("b1", [1, 6144], F32, kind="ExternalInput").ap()
    bo_h = nc.dram_tensor("bo", [1, D], F32, kind="ExternalInput").ap()
    gw_h = nc.dram_tensor("gw", [D, 8], F32, kind="ExternalInput").ap()
    eb_h = nc.dram_tensor("eb", [NE, D], F32, kind="ExternalInput").ap()
    cos_h = nc.dram_tensor("cosT", [TPC, 512], BF16, kind="ExternalInput").ap()
    sin_h = nc.dram_tensor("sinT", [TPC, 512], BF16, kind="ExternalInput").ap()
    eye16_h = nc.dram_tensor("eye16", [128, 128], BF16, kind="ExternalInput").ap()
    eye32_h = nc.dram_tensor("eye32", [128, 128], F32, kind="ExternalInput").ap()
    ones_h = nc.dram_tensor("ones16", [1, 128], BF16, kind="ExternalInput").ap()
    konst_h = nc.dram_tensor("konst", [128, 2], F32, kind="ExternalInput").ap()
    out_h = nc.dram_tensor("out", [TPC, D], F32, kind="ExternalOutput").ap()

    cc_in = nc.dram_tensor("cc_in", [128, 16], F32)
    cc_out = nc.dram_tensor("cc_out", [128, 16], F32)
    dbg = {}
    if _DEBUG:
        for nm, shp, dt_ in (("d_x1", [128, 16384], BF16),
                             ("d_qt", [128, 16384], BF16),
                             ("d_h2t", [128, 16384], BF16),
                             ("d_acc", [128, 4096], F32),
                             ("d_g", [128, 64], F32),
                             ("d_acc2", [128, 16], F32),
                             ("d_stats", [128, 64], F32),
                             ("d_gt", [8, 1024], BF16),
                             ("d_eb", [8, 2048], BF16),
                             ("d_lf", [128, 64], F32)):
            dbg[nm] = nc.dram_tensor(nm, shp, dt_, kind="ExternalOutput").ap()

    def hbm_piece(p):
        if p < 12:
            src = w1_h[:, p * 512:(p + 1) * 512]
        elif p < 16:
            src = wo_h[:, (p - 12) * 512:(p - 11) * 512]
        else:
            k = p - 16
            n, e = k // 8, k % 8
            src = ew_h[e, :, n * 512:(n + 1) * 512]
        return src.rearrange("(c p) n -> p c n", p=128)

    with ExitStack() as ctx:
        al = ctx.enter_context
        buf_a = al(nc.sbuf_tensor("buf_a", [128, 16384], BF16))
        buf_b = al(nc.sbuf_tensor("buf_b", [128, 16384], BF16))
        buf_u = al(nc.sbuf_tensor("buf_u", [128, 16384], BF16))
        rs = [al(nc.sbuf_tensor(f"ring{i}", [128, 8192], BF16)) for i in range(3)]
        buf_f = al(nc.sbuf_tensor("buf_f", [128, 4096], F32))
        htmp = [al(nc.sbuf_tensor(f"htmp{i}", [128, 2048], BF16)) for i in range(2)]
        sqtr = al(nc.sbuf_tensor("sqtr", [128, 2048], BF16))
        st_f = al(nc.sbuf_tensor("st_f", [128, 64], F32))
        l_f = al(nc.sbuf_tensor("l_f", [128, 64], F32))
        e1_f = al(nc.sbuf_tensor("e1_f", [128, 64], F32))
        gs_f = al(nc.sbuf_tensor("gs_f", [128, 32], F32))
        g_f = al(nc.sbuf_tensor("g_f", [128, 64], F32))
        gbf = al(nc.sbuf_tensor("gbf", [128, 64], BF16))
        gT_sb = al(nc.sbuf_tensor("gT_sb", [8, 1024], BF16))
        l8_sb = al(nc.sbuf_tensor("l8_sb", [8, 128], F32))
        a_fm = al(nc.sbuf_tensor("a_fm", [128, 16], F32))
        a_cc = al(nc.sbuf_tensor("a_cc", [128, 16], F32))
        b1_sb = al(nc.sbuf_tensor("b1_sb", [1, 6144], BF16))
        bo_sb = al(nc.sbuf_tensor("bo_sb", [1, 2048], BF16))
        gw_sb = al(nc.sbuf_tensor("gw_sb", [128, 128], BF16))
        eb_sb = al(nc.sbuf_tensor("eb_sb", [8, 2048], BF16))
        eye16 = al(nc.sbuf_tensor("eye16s", [128, 128], BF16))
        eye32 = al(nc.sbuf_tensor("eye32s", [128, 128], F32))
        ones_sb = al(nc.sbuf_tensor("ones_sb", [1, 128], BF16))
        konst_sb = al(nc.sbuf_tensor("konst_sb", [128, 2], F32))
        ps_m = al(nc.psum_tensor("ps_m", [128, 2048], F32))
        ps_t = [al(nc.psum_tensor(f"ps_t{i}", [128, 1024], BF16)) for i in range(2)]
        ps_x = [al(nc.psum_tensor(f"ps_x{i}", [128, 512], F32)) for i in range(2)]
        sm = {k: al(nc.semaphore(f"sem_{k}"))
              for k in ("t", "v", "s", "o", "c", "yc", "y0", "y1", "y2",
                        "y3", "yt0", "yt1", "w0", "w1", "w2", "wa")}
        block = al(nc.Block())

        def h1T(dc, tc):
            return buf_a[:, dc * 1024 + tc * 128: dc * 1024 + (tc + 1) * 128]

        def x1v(tc, c0, c1):
            return buf_a[:, tc * 2048 + c0: tc * 2048 + c1]

        def kall(tc, c0, c1):
            return buf_b[:, tc * 2048 + c0: tc * 2048 + c1]

        def qT(dc, tc):
            return buf_b[:, dc * 1024 + tc * 128: dc * 1024 + (tc + 1) * 128]

        def qTblk(dc):
            return buf_b[:, dc * 1024:(dc + 1) * 1024]

        def h2T(dc, tc):
            return buf_u[:, dc * 1024 + tc * 128: dc * 1024 + (tc + 1) * 128]

        def tab_cos(j):
            return buf_u[:, j * 1024: j * 1024 + 512]

        def tab_sin(j):
            return buf_u[:, j * 1024 + 512:(j + 1) * 1024]

        def qtileA(tc):
            return buf_u[:, 2048 + tc * 512: 2048 + (tc + 1) * 512]

        def qslot(j):
            return buf_u[:, 6144 + j * 512: 6144 + (j + 1) * 512]

        def ropeout(j):
            return buf_u[:, 7168 + j * 1024: 7168 + (j + 1) * 1024]

        def vslot(j):
            return buf_u[:, 9216 + j * 512: 9216 + (j + 1) * 512]

        prodslot = buf_u[:, 11264:11776]

        def qtmp(j):
            return buf_u[:, 12288 + j * 512: 12288 + (j + 1) * 512]
        apart = buf_f[:, 0:2048]

        def xtile(j):
            base = 2048 + j * 1024 if j < 2 else (j - 2) * 1024
            return buf_f[:, base: base + 1024]

        def accv(tc):
            return buf_f[:, tc * 512:(tc + 1) * 512]

        def stc(tc, k):
            return st_f[:, tc * 8 + k: tc * 8 + k + 1]

        def bank(g):
            return ps_m[:, (g % 4) * 512:((g % 4) + 1) * 512]

        q = {k: [] for k in ("t", "v", "s", "y", "g")}
        ct = {k: 0 for k in ("t", "v", "s", "o", "c", "yc", "y0", "y1",
                             "y2", "y3", "yt0", "yt1", "w0", "w1", "w2",
                             "wa")}
        ENG = {"t": nc.tensor, "v": nc.vector, "s": nc.scalar,
               "y": nc.sync, "g": nc.gpsimd}

        def emit(e_, fn):
            q[e_].append(fn)

        wmax = {}

        def wait(e_, semk, val):
            wmax[semk] = max(wmax.get(semk, 0), val)
            emit(e_, lambda eng=ENG[e_], s_=sm[semk], v_=val: eng.wait_ge(s_, v_))

        def inc(semk, amt=1):
            ct[semk] += amt
            return ct[semk]

        Gseq = []
        mm_led = {}
        evac_led = {}
        tb_led = []
        tb_evac = []
        h1T_rdy = {}
        h1half = {}
        ln1 = {}
        ropeK_done = {}
        ropeQ_done = {}
        tabK = {}
        tabQ = {}
        rn_led = {}
        vprod_led = {}
        prod_cnt = [0]
        qslot_rel = {}
        q9tr = {}
        q9ev = {}
        qtr_ev = {}
        fold_led = {}
        red_done_led = {}
        x1ev = {}
        x2load = {}
        xh_led = {}
        a_tr = [0]
        a_tr_led = [0]
        h2b_led = {}
        h2T_rdy = {}
        logit_led = {}
        l8_led = {}
        ltr_led = {}
        lf_led = {}
        exp_led = {}
        gate_led = {}
        gT_led = {}
        out_led = {}
        w_led = {}
        a_dma_y = [None]
        acc_back_w = [None]
        piece_round = {}
        for r_, rd_ in enumerate(ROUNDS):
            if rd_["piece"] is not None:
                piece_round[rd_["piece"]] = r_

        def gk(r, tc):
            return r * 1000 + tc

        tabsets = get_activation_tables(nc.m.arch)
        set_id = list(tabsets).index("natural_log_exp_and_others")
        emit("s", lambda: nc.scalar.add_instruction(
            mybir.InstLoadActFuncSet(
                name=nc.get_next_instruction_name(),
                act_func_set_id=set_id, ins=[], outs=[])))

        nc.const_aps.register(F32, EPS, konst_sb[:, 0:1])
        nc.const_aps.register(F32, 1.0, konst_sb[:, 1:2])
        for dst, src in ((eye16[:, :], eye16_h), (eye32[:, :], eye32_h),
                         (ones_sb[:, :], ones_h), (konst_sb[:, :], konst_h)):
            inc("yc", 16)
            emit("y", lambda d=dst, s_=src:
                 nc.sync.dma_start(out=d, in_=s_).then_inc(sm["yc"], 16))
        const_y = ct["yc"]
        wait("s", "yc", const_y)
        for dst, src in (
                (b1_sb[:, :], b1_h), (bo_sb[:, :], bo_h),
                (gw_sb[:, :].rearrange("p (c n) -> p c n", n=8),
                 gw_h.rearrange("(c p) n -> p c n", p=128)),
                (eb_sb[:, :], eb_h)):
            inc("wa", 16)
            emit("g", lambda d=dst, s_=src:
                 nc.gpsimd.dma_start(out=d, in_=s_).then_inc(sm["wa"], 16))
        const_w = ct["wa"]

        def prepass_head(tc):
            for h in range(2):
                k = 2 * tc + h
                if k >= 4:
                    wait("y", "v", 2 * (tc - 2) + h + 1)
                    wait("y", "s", ln1[tc - 2])
                src = x_h[tc * 128:(tc + 1) * 128, h * 1024:(h + 1) * 1024]
                ys = f"y{k % 4}"
                inc(ys, 16)
                emit("y", lambda d=xtile(k % 4), s_=src, m_=sm[ys]:
                     nc.sync.dma_start(out=d, in_=s_).then_inc(m_, 16))
                wait("s", ys, ct[ys])
                xt = xtile(k % 4)
                emit("s", lambda o=sqtr[:, 0:1024], i=xt, a=stc(tc, 2 * h):
                     nc.scalar.activation(o, i, AF.Copy, accum_out=a))
                emit("s", lambda o=sqtr[:, 1024:2048], i=xt, a=stc(tc, 2 * h + 1):
                     nc.scalar.activation(o, i, AF.Square, accum_out=a))
            M, R, S1, S2 = stc(tc, 4), stc(tc, 5), stc(tc, 6), stc(tc, 7)
            t0, t1 = gs_f[:, 0:1], gs_f[:, 1:2]
            emit("s", lambda o=t0, i=stc(tc, 0), a=stc(tc, 2):
                 nc.scalar.add(o, i, a))
            emit("s", lambda o=M, i=t0: nc.scalar.mul(o, i, 1.0 / 2048.0))
            emit("s", lambda o=t1, i=M: nc.scalar.activation(o, i, AF.Square))
            emit("s", lambda o=t1, i=t1: nc.scalar.mul(o, i, -1.0))
            emit("s", lambda o=t0, i=stc(tc, 1), a=stc(tc, 3):
                 nc.scalar.add(o, i, a))
            emit("s", lambda o=t0, i=t0: nc.scalar.mul(o, i, 1.0 / 2048.0))
            emit("s", lambda o=t0, i=t0, a=t1: nc.scalar.add(o, i, a))
            emit("s", lambda o=t0, i=t0: nc.scalar.add(o, i, EPS))
            emit("s", lambda o=t0, i=t0: nc.scalar.activation(o, i, AF.Ln))
            emit("s", lambda o=R, i=t0:
                 nc.scalar.activation(o, i, AF.Exp, scale=-0.5))
            emit("s", lambda o=S1, i=R: nc.scalar.add(o, i, 1.0))
            emit("s", lambda o=S2, i=M, a=R:
                 nc.scalar.mul(o, i, a).then_inc(sm["s"], 1))
            ln1[tc] = inc("s")

        def prepass_tail(tc):
            M, R = stc(tc, 4), stc(tc, 5)
            if tc >= 2:
                wait("v", "t", tb_led[2 * (tc - 2) + 1])
            wait("v", "s", ln1[tc])
            for h in range(2):
                xt = xtile((2 * tc + h) % 4)
                dst = htmp[tc % 2][:, h * 1024:(h + 1) * 1024]
                emit("v", lambda o=dst, i=xt, m=M, r_=R:
                     nc.vector.tensor_scalar(o, i, m, r_, ALU.subtract, ALU.mult)
                     .then_inc(sm["v"], 1))
                h1half[tc, h] = inc("v")
                assert h1half[tc, h] == 2 * tc + h + 1, "x-load pacing broke"
            if tc == NTC - 1:
                emit("v", lambda: nc.vector.memset(apart, 0.0))

            for b_ in range(2):
                gb = len(tb_led)
                if gb == 0:
                    wait("t", "yc", const_y)
                wait("t", "v", h1half[tc, b_])
                if gb >= 2:
                    wait("t", "s", tb_evac[gb - 2])
                pt = ps_t[gb % 2]
                for j in range(8):
                    src = htmp[tc % 2][:, (b_ * 8 + j) * 128:(b_ * 8 + j + 1) * 128]
                    last = j == 7
                    emit("t", lambda o=pt[:, j * 128:(j + 1) * 128], i=src,
                         id_=eye16[:, :], l=last:
                         (nc.tensor.transpose(o, i, id_).then_inc(sm["t"], 1)
                          if l else nc.tensor.transpose(o, i, id_)))
                tb_led.append(inc("t"))
                wait("s", "t", tb_led[gb])
                dstv = buf_a[:, :].rearrange("p (c x) -> p c x", x=1024)[
                    :, b_ * 8:(b_ + 1) * 8, tc * 128:(tc + 1) * 128]
                srcv = pt[:, :].rearrange("p (c x) -> p c x", x=128)
                emit("s", lambda o=dstv, i=srcv:
                     nc.scalar.copy(o, i).then_inc(sm["s"], 1))
                tb_evac.append(inc("s"))
            h1T_rdy[tc] = tb_evac[-1]

        if _OPT_PREPASS:
            for _tc in range(4):
                prepass_head(_tc)
            for _tc in range(4, NTC):
                prepass_tail(_tc - 4)
                prepass_head(_tc)
        else:
            for _tc in range(NTC):
                prepass_head(_tc)
                prepass_tail(_tc)

        def emit_tab(tc, ledger, hold_sem=None):
            j = tc % 2
            if hold_sem is not None:
                wait("y", hold_sem[0], hold_sem[1])
            ys = f"yt{j}"
            for hsrc, dst in ((cos_h, tab_cos(j)), (sin_h, tab_sin(j))):
                src = hsrc[tc * 128:(tc + 1) * 128, :]
                inc(ys, 16)
                emit("y", lambda d=dst, s_=src, m_=sm[ys]:
                     nc.sync.dma_start(out=d, in_=s_).then_inc(m_, 16))
            ledger[tc] = (ys, ct[ys])

        emit_tab(0, tabK)
        emit_tab(1, tabK)

        gp_state = [0]

        def emit_collective_trigger():
            wait("g", "yc", a_dma_y[0])
            emit("g", lambda: nc.gpsimd.collective_compute(
                "AllReduce", ALU.add,
                replica_groups=[[0, 1], [2, 3], [4, 5], [6, 7]],
                ins=[cc_in.ap()], outs=[cc_out.ap()]).then_inc(sm["c"], 1))
            inc("c")

        def emit_collective_fetch():
            wait("g", "c", ct["c"])
            inc("wa", 16)
            emit("g", lambda: nc.gpsimd.dma_start(out=a_cc[:, :], in_=cc_out[:, :])
                 .then_inc(sm["wa"], 16))
            acc_back_w[0] = ct["wa"]

        def pump_pieces(upto):
            while gp_state[0] <= upto and gp_state[0] < NPIECE:
                p = gp_state[0]
                if p >= 3:
                    rprev = piece_round[p - 3]
                    wait("g", "t", mm_led[gk(rprev, NTC - 1)])
                slot = rs[p % 3]
                ws = f"w{p % 3}"
                dstv = slot[:, :].rearrange("p (c n) -> p c n", n=512)
                inc(ws, 16)
                emit("g", lambda d=dstv, s_=hbm_piece(p), m_=sm[ws]:
                     nc.gpsimd.dma_start(out=d, in_=s_).then_inc(m_, 16))
                w_led[p] = (ws, ct[ws])
                gp_state[0] += 1
                if gp_state[0] == 10:
                    emit_collective_trigger()
                if gp_state[0] == 13:
                    emit_collective_fetch()

        pump_pieces(2)

        def emit_group(r, tc):
            rd = ROUNDS[r]
            G = gk(r, tc)
            gidx = len(Gseq)
            Gseq.append(G)
            bk = bank(gidx)
            if r == 0 and tc == 0:
                wait("t", "wa", const_w)
                wait("t", "yc", const_y)
            if rd["piece"] is not None and tc == 0:
                wait("t", *w_led[rd["piece"]])
            if gidx >= 4:
                semk, cnt = evac_led[Gseq[gidx - 4]]
                wait("t", semk, cnt)
            if r == 0:
                wait("t", "s", h1T_rdy[tc])
            if rd["kind"] == "EXP" and rd["e"] == 0 and rd["n"] == 0:
                wait("t", "s", h2T_rdy[tc])
            if rd["kind"] == "BIAS":
                wait("t", "s", gT_led[tc])
                emit("t", lambda o=bk, l=gT_sb[0:8, tc * 128:(tc + 1) * 128],
                     rh=eb_sb[0:8, rd["n"] * 512:(rd["n"] + 1) * 512]:
                     nc.tensor.matmul(o, l, rh, start=True, stop=True)
                     .then_inc(sm["t"], 1))
                mm_led[G] = inc("t")
                return gidx
            slot = rs[rd["piece"] % 3]
            started = False
            if rd["kind"] in ("K", "V", "Q", "WO"):
                brow = (b1_sb[0:1, rd["piece"] * 512:(rd["piece"] + 1) * 512]
                        if rd["kind"] != "WO"
                        else bo_sb[0:1, rd["n"] * 512:(rd["n"] + 1) * 512])
                emit("t", lambda o=bk, l=ones_sb[0:1, :], rh=brow:
                     nc.tensor.matmul(o, l, rh, start=True, stop=False))
                started = True
            for dc in range(NDC):
                if rd["kind"] == "WO":
                    if r == 12 and tc == 0:
                        wait("t", "v", fold_led[dc])
                    lhs = qT(dc, tc)
                elif rd["kind"] == "EXP":
                    lhs = h2T(dc, tc)
                else:
                    lhs = h1T(dc, tc)
                rh = slot[:, dc * 512:(dc + 1) * 512]
                st_ = (not started) and dc == 0
                sp = dc == NDC - 1
                emit("t", lambda o=bk, l=lhs, rr=rh, a=st_, b_=sp:
                     (nc.tensor.matmul(o, l, rr, start=a, stop=b_)
                      .then_inc(sm["t"], 1) if b_ else
                      nc.tensor.matmul(o, l, rr, start=a, stop=b_)))
            mm_led[G] = inc("t")
            return gidx

        def evac_K(r, tc, gidx):
            G = gk(r, tc)
            bk = bank(gidx)
            wait("s", "t", mm_led[G])
            emit("s", lambda o=kall(tc, r * 512, (r + 1) * 512), i=bk:
                 nc.scalar.copy(o, i).then_inc(sm["s"], 1))
            evac_led[G] = ("s", inc("s"))
            if r == 1:
                wait("v", "s", evac_led[G][1])
                wait("v", *tabK[tc])
                j = tc % 2
                Ev, Ov = kall(tc, 0, 512), kall(tc, 512, 1024)
                tE, tO = qtmp(2 * j), qtmp(2 * j + 1)
                emit("v", lambda o=tE, a=Ev, b_=tab_sin(j): nc.vector.tensor_mul(o, a, b_))
                emit("v", lambda o=tO, a=Ov, b_=tab_sin(j): nc.vector.tensor_mul(o, a, b_))
                emit("v", lambda o=Ev, a=Ev, b_=tab_cos(j): nc.vector.tensor_mul(o, a, b_))
                emit("v", lambda o=Ov, a=Ov, b_=tab_cos(j): nc.vector.tensor_mul(o, a, b_))
                emit("v", lambda o=Ev, a=Ev, b_=tO: nc.vector.tensor_sub(o, a, b_))
                emit("v", lambda o=Ov, a=Ov, b_=tE:
                     nc.vector.tensor_add(o, a, b_).then_inc(sm["v"], 1))
                ropeK_done[tc] = inc("v")
                if tc + 2 < NTC:
                    emit_tab(tc + 2, tabK, ("v", ropeK_done[tc]))
            if r == 3:
                wait("s", "v", ropeK_done[tc])
                ssq, rn = stc(tc, 0), stc(tc, 1)
                emit("s", lambda o=sqtr[:, :], i=kall(tc, 0, 2048), a=ssq:
                     nc.scalar.activation(o, i, AF.Square, accum_out=a))
                emit("s", lambda o=ssq, i=ssq: nc.scalar.activation(o, i, AF.Ln))
                emit("s", lambda o=rn, i=ssq:
                     nc.scalar.activation(o, i, AF.Exp, scale=-0.5)
                     .then_inc(sm["s"], 1))
                rn_led[tc] = inc("s")

        def evac_V(r, tc, gidx):
            G = gk(r, tc)
            bk = bank(gidx)
            vj = vslot(gidx % 2)
            wait("s", "t", mm_led[G])
            if gidx >= 2 and Gseq[gidx - 2] in vprod_led:
                wait("s", "v", vprod_led[Gseq[gidx - 2]])
            emit("s", lambda o=vj, i=bk, sc=stc(tc, 1):
                 nc.scalar.mul(o, i, sc).then_inc(sm["s"], 1))
            evac_led[G] = ("s", inc("s"))
            c0 = (r - 4) * 512
            wait("v", "s", evac_led[G][1])
            emit("v", lambda o=prodslot, a=kall(tc, c0, c0 + 512), b_=vj:
                 nc.vector.tensor_mul(o, a, b_).then_inc(sm["v"], 1))
            pcnt = inc("v")
            wait("v", "v", pcnt)
            emit("v", lambda o=apart[:, c0:c0 + 512], a=apart[:, c0:c0 + 512],
                 b_=prodslot:
                 nc.vector.tensor_add(o, a, b_).then_inc(sm["v"], 1))
            vprod_led[G] = inc("v")
            prod_cnt[0] = vprod_led[G]

        def evac_Q(r, tc, gidx):
            G = gk(r, tc)
            bk = bank(gidx)
            wait("s", "t", mm_led[G])
            if r == 8 and tc == 0:
                wait("s", "v", prod_cnt[0])
            if r == 8:
                dst = qtileA(tc)
            else:
                if gidx >= 2:
                    prevG = Gseq[gidx - 2]
                    if prevG in qslot_rel:
                        semk, cnt = qslot_rel[prevG]
                        wait("s", semk, cnt)
                dst = qslot(gidx % 2)
            emit("s", lambda o=dst, i=bk: nc.scalar.copy(o, i).then_inc(sm["s"], 1))
            evac_led[G] = ("s", inc("s"))
            if r == 9:
                wait("v", "s", evac_led[G][1])
                wait("v", *tabQ[tc])
                if tc >= 2:
                    wait("v", "t", q9tr[tc - 2])
                j = tc % 2
                Ev, Ov = qtileA(tc), dst
                RO = ropeout(tc % 2)
                Eo, Oo = RO[:, 0:512], RO[:, 512:1024]
                sp1, sp2 = qtmp(2 * j), qtmp(2 * j + 1)
                emit("v", lambda o=sp1, a=Ev, b_=tab_sin(j): nc.vector.tensor_mul(o, a, b_))
                emit("v", lambda o=sp2, a=Ov, b_=tab_sin(j): nc.vector.tensor_mul(o, a, b_))
                emit("v", lambda o=Eo, a=Ev, b_=tab_cos(j): nc.vector.tensor_mul(o, a, b_))
                emit("v", lambda o=Oo, a=Ov, b_=tab_cos(j): nc.vector.tensor_mul(o, a, b_))
                emit("v", lambda o=Eo, a=Eo, b_=sp2: nc.vector.tensor_sub(o, a, b_))
                emit("v", lambda o=Oo, a=Oo, b_=sp1:
                     nc.vector.tensor_add(o, a, b_).then_inc(sm["v"], 1))
                ropeQ_done[tc] = inc("v")
                qslot_rel[G] = ("v", ropeQ_done[tc])
                if tc + 2 < NTC:
                    emit_tab(tc + 2, tabQ, ("v", ropeQ_done[tc]))

        def q_transposes(r, tc, gidx_of_group):
            gb = len(tb_led)
            pt = ps_t[gb % 2]
            if gb >= 2:
                wait("t", "s", tb_evac[gb - 2])
            if r == 9:
                wait("t", "v", ropeQ_done[tc])
                srcbase = ropeout(tc % 2)
                nchunks, dc0 = 8, 0
            else:
                G = gk(r, tc)
                wait("t", "s", evac_led[G][1])
                srcbase = qslot(gidx_of_group % 2)
                nchunks, dc0 = 4, 8 + (r - 10) * 4
            for j in range(nchunks):
                src = srcbase[:, j * 128:(j + 1) * 128]
                last = j == nchunks - 1
                emit("t", lambda o=pt[:, j * 128:(j + 1) * 128], i=src,
                     id_=eye16[:, :], l=last:
                     (nc.tensor.transpose(o, i, id_).then_inc(sm["t"], 1)
                      if l else nc.tensor.transpose(o, i, id_)))
            tb_led.append(inc("t"))
            if r == 9:
                q9tr[tc] = tb_led[-1]
            else:
                qslot_rel[gk(r, tc)] = ("t", tb_led[-1])
            wait("s", "t", tb_led[gb])
            dstv = buf_b[:, :].rearrange("p (c x) -> p c x", x=1024)[
                :, dc0:dc0 + nchunks, tc * 128:(tc + 1) * 128]
            srcv = pt[:, 0:nchunks * 128].rearrange("p (c x) -> p c x", x=128)
            emit("s", lambda o=dstv, i=srcv:
                 nc.scalar.copy(o, i).then_inc(sm["s"], 1))
            tb_evac.append(inc("s"))
            qtr_ev[r, tc] = tb_evac[-1]

        def evac_WO(r, tc, gidx):
            G = gk(r, tc)
            bk = bank(gidx)
            n = r - 12
            wait("v", "t", mm_led[G])
            xv = x1v(tc, n * 512, (n + 1) * 512)
            emit("v", lambda o=xv, a=bk, b_=xv:
                 nc.vector.tensor_add(o, a, b_).then_inc(sm["v"], 1))
            evac_led[G] = ("v", inc("v"))
            if r == 15:
                x1ev[tc] = evac_led[G][1]

        def evac_EXPBIAS(r, tc, gidx):
            G = gk(r, tc)
            rd = ROUNDS[r]
            bk = bank(gidx)
            n = rd["n"]
            acc = accv(tc)
            wait("v", "t", mm_led[G])
            if rd["kind"] == "BIAS":
                emit("v", lambda o=acc, a=bk, b_=acc:
                     nc.vector.tensor_add(o, a, b_).then_inc(sm["v"], 1))
                evac_led[G] = ("v", inc("v"))
                wait("y", "v", evac_led[G][1])
                dst = out_h[tc * 128:(tc + 1) * 128, n * 512:(n + 1) * 512]
                inc("o", 16)
                emit("y", lambda d=dst, s_=acc:
                     nc.sync.dma_start(out=d, in_=s_).then_inc(sm["o"], 16))
                out_led[n, tc] = ct["o"]
                return
            e = rd["e"]
            gsc = g_f[:, tc * 8 + e: tc * 8 + e + 1]
            if e == 0:
                wait("v", "v", gate_led[tc])
                if n > 0:
                    wait("v", "o", out_led[n - 1, tc])
                xv = x1v(tc, n * 512, (n + 1) * 512)
                emit("v", lambda o=acc, i=bk, sc=gsc, b_=xv:
                     nc.vector.scalar_tensor_tensor(o, i, sc, b_, ALU.mult,
                                                    ALU.add).then_inc(sm["v"], 1))
            else:
                emit("v", lambda o=acc, i=bk, sc=gsc, a=acc:
                     nc.vector.scalar_tensor_tensor(o, i, sc, a, ALU.mult,
                                                    ALU.add).then_inc(sm["v"], 1))
            evac_led[G] = ("v", inc("v"))

        def emit_a_reduce():
            for wv in range(4):
                px = ps_x[wv % 2]
                if wv >= 2:
                    wait("t", "v", red_done_led[wv - 2])
                for j in range(4):
                    c = wv * 4 + j
                    src = apart[:, c * 128:(c + 1) * 128]
                    last = j == 3
                    emit("t", lambda o=px[:, j * 128:(j + 1) * 128], i=src,
                         id_=eye32[:, :], l=last:
                         (nc.tensor.transpose(o, i, id_).then_inc(sm["t"], 1)
                          if l else nc.tensor.transpose(o, i, id_)))
                aw = inc("t")
                a_tr_led[0] = aw
                wait("v", "t", aw)
                for j in range(4):
                    c = wv * 4 + j
                    last = j == 3
                    emit("v", lambda o=a_fm[:, c:c + 1],
                         i=px[:, j * 128:(j + 1) * 128], l=last:
                         (nc.vector.reduce_sum(o, i, axis=AX).then_inc(sm["v"], 1)
                          if l else nc.vector.reduce_sum(o, i, axis=AX)))
                red_done_led[wv] = inc("v")
            wait("y", "v", red_done_led[3])
            inc("yc", 16)
            emit("y", lambda: nc.sync.dma_start(out=cc_in[:, :], in_=a_fm[:, :])
                 .then_inc(sm["yc"], 16))
            a_dma_y[0] = ct["yc"]

        def emit_p2a(tc):
            wait("s", "v", x1ev[tc])
            M2, R2 = stc(tc, 4), stc(tc, 5)
            sA, sB = stc(tc, 0), stc(tc, 1)
            t0, t1 = gs_f[:, 2:3], gs_f[:, 3:4]
            xv = x1v(tc, 0, 2048)
            emit("s", lambda o=sqtr[:, :], i=xv, a=sA:
                 nc.scalar.activation(o, i, AF.Copy, accum_out=a))
            emit("s", lambda o=sqtr[:, :], i=xv, a=sB:
                 nc.scalar.activation(o, i, AF.Square, accum_out=a))
            emit("s", lambda o=M2, i=sA: nc.scalar.mul(o, i, 1.0 / 2048.0))
            emit("s", lambda o=t1, i=M2: nc.scalar.activation(o, i, AF.Square))
            emit("s", lambda o=t1, i=t1: nc.scalar.mul(o, i, -1.0))
            emit("s", lambda o=t0, i=sB: nc.scalar.mul(o, i, 1.0 / 2048.0))
            emit("s", lambda o=t0, i=t0, a=t1: nc.scalar.add(o, i, a))
            emit("s", lambda o=t0, i=t0: nc.scalar.add(o, i, EPS))
            emit("s", lambda o=t0, i=t0: nc.scalar.activation(o, i, AF.Ln))
            emit("s", lambda o=R2, i=t0:
                 nc.scalar.activation(o, i, AF.Exp, scale=-0.5)
                 .then_inc(sm["s"], 1))
            ln2c = inc("s")
            if tc >= 2:
                wait("v", "t", h2b_led[tc - 2])
            else:
                wait("v", "t", tb_led[13 + 2 * tc])
            wait("v", "s", ln2c)
            emit("v", lambda o=htmp[tc % 2][:, :], i=x1v(tc, 0, 2048), m=M2, r_=R2:
                 nc.vector.tensor_scalar(o, i, m, r_, ALU.subtract, ALU.mult)
                 .then_inc(sm["v"], 1))
            h2ap = inc("v")
            for b_ in range(2):
                gb = len(tb_led)
                pt = ps_t[gb % 2]
                wait("t", "v", h2ap)
                if gb >= 2:
                    wait("t", "s", tb_evac[gb - 2])
                for j in range(8):
                    src = htmp[tc % 2][:, (b_ * 8 + j) * 128:(b_ * 8 + j + 1) * 128]
                    last = j == 7
                    emit("t", lambda o=pt[:, j * 128:(j + 1) * 128], i=src,
                         id_=eye16[:, :], l=last:
                         (nc.tensor.transpose(o, i, id_).then_inc(sm["t"], 1)
                          if l else nc.tensor.transpose(o, i, id_)))
                tb_led.append(inc("t"))
                wait("s", "t", tb_led[gb])
                dstv = buf_u[:, :].rearrange("p (c x) -> p c x", x=1024)[
                    :, b_ * 8:(b_ + 1) * 8, tc * 128:(tc + 1) * 128]
                srcv = pt[:, :].rearrange("p (c x) -> p c x", x=128)
                emit("s", lambda o=dstv, i=srcv:
                     nc.scalar.copy(o, i).then_inc(sm["s"], 1))
                tb_evac.append(inc("s"))
            h2b_led[tc] = tb_led[-1]
            h2T_rdy[tc] = tb_evac[-1]
            if tc == 0:
                wait("t", "v", red_done_led[3])
            else:
                wait("t", "s", l8_led[tc - 1])
            wait("t", "s", h2T_rdy[tc])
            for dc in range(NDC):
                st_, sp = dc == 0, dc == NDC - 1
                emit("t", lambda o=ps_x[0][0:8, 0:128],
                     l=gw_sb[:, dc * 8:(dc + 1) * 8],
                     rh=h2T(dc, tc), a=st_, b_=sp:
                     (nc.tensor.matmul(o, l, rh, start=a, stop=b_)
                      .then_inc(sm["t"], 1) if b_ else
                      nc.tensor.matmul(o, l, rh, start=a, stop=b_)))
            logit_led[tc] = inc("t")
            wait("s", "t", logit_led[tc])
            emit("s", lambda o=l8_sb[:, :], i=ps_x[0][0:8, 0:128]:
                 nc.scalar.copy(o, i).then_inc(sm["s"], 1))
            l8_led[tc] = inc("s")

        def emit_gates(tc):
            wait("t", "s", l8_led[tc])
            if tc == 0:
                wait("t", "v", red_done_led[3])
            else:
                wait("t", "s", lf_led[tc - 1])
            emit("t", lambda o=ps_x[1][:, 0:8], i=l8_sb[:, :], id_=eye32[0:8, 0:8]:
                 nc.tensor.transpose(o, i, id_).then_inc(sm["t"], 1))
            ltr_led[tc] = inc("t")
            wait("s", "t", ltr_led[tc])
            emit("s", lambda o=l_f[:, tc * 8:(tc + 1) * 8], i=ps_x[1][:, 0:8]:
                 nc.scalar.copy(o, i).then_inc(sm["s"], 1))
            lf_led[tc] = inc("s")
            lfv = l_f[:, tc * 8:(tc + 1) * 8]
            m1, m1n, m2 = gs_f[:, 4:5], gs_f[:, 5:6], gs_f[:, 6:7]
            eq = gs_f[:, 8:16]
            l2 = gs_f[:, 16:24]

            def vstep(fn):
                emit("v", lambda f=fn: f().then_inc(sm["v"], 1))
                c = inc("v")
                wait("v", "v", c)
                return c

            wait("v", "s", lf_led[tc])
            vstep(lambda o=m1, i=lfv: nc.vector.reduce_max(o, i, axis=AX))
            vstep(lambda o=m1n, i=lfv:
                  nc.vector.reduce_max(o, i, axis=AX, negate=True))
            vstep(lambda o=eq, i=lfv, sc=m1:
                  nc.vector.tensor_single_scalar(o, i, sc, ALU.is_ge))
            vstep(lambda o=l2, i=eq:
                  nc.vector.tensor_single_scalar(o, i, -BIG, ALU.mult))
            vstep(lambda o=l2, a=l2, b_=lfv: nc.vector.tensor_add(o, a, b_))
            p1 = vstep(lambda o=m2, i=l2: nc.vector.reduce_max(o, i, axis=AX))
            e1 = e1_f[:, tc * 8:(tc + 1) * 8]
            e2m = gs_f[:, 7:8]
            wait("s", "v", p1)
            emit("s", lambda o=e1, i=lfv, b_=m1n:
                 nc.scalar.activation(o, i, AF.Exp, bias=b_))
            emit("s", lambda o=e2m, i=m2, b_=m1n:
                 nc.scalar.activation(o, i, AF.Exp, bias=b_).then_inc(sm["s"], 1))
            exp_led[tc] = inc("s")
            ge2 = gs_f[:, 8:16]
            gu = gs_f[:, 16:24]
            den = gs_f[:, 4:5]
            rden = gs_f[:, 5:6]
            gv = g_f[:, tc * 8:(tc + 1) * 8]
            wait("v", "s", exp_led[tc])
            vstep(lambda o=ge2, i=lfv, sc=m2:
                  nc.vector.tensor_single_scalar(o, i, sc, ALU.is_ge))
            vstep(lambda o=gu, a=e1, b_=ge2: nc.vector.tensor_mul(o, a, b_))
            vstep(lambda o=den, i=e2m: nc.vector.tensor_scalar_add(o, i, 1.0))
            vstep(lambda o=rden, i=den: nc.vector.reciprocal(o, i))
            gate_led[tc] = vstep(lambda o=gv, i=gu, sc=rden:
                                 nc.vector.tensor_single_scalar(o, i, sc, ALU.mult))
            wait("t", "v", gate_led[tc])
            if tc >= 1:
                wait("t", "s", gT_led[tc - 1])
            emit("t", lambda o=ps_x[1][0:8, 128:256], i=g_f[:, tc * 8:(tc + 1) * 8],
                 id_=eye32[:, :]:
                 nc.tensor.transpose(o, i, id_).then_inc(sm["t"], 1))
            gttr = inc("t")
            wait("s", "t", gttr)
            emit("s", lambda o=gT_sb[0:8, tc * 128:(tc + 1) * 128],
                 i=ps_x[1][0:8, 128:256]:
                 nc.scalar.copy(o, i).then_inc(sm["s"], 1))
            gT_led[tc] = inc("s")

        for r, rd in enumerate(ROUNDS):
            if rd["piece"] is not None:
                pump_pieces(rd["piece"])
            if r == 16:
                if _OPT_LADDER:
                    emit_p2a(0)
                    emit_p2a(1)
                    emit_gates(0)
                else:
                    for _tc in range(NTC):
                        emit_p2a(_tc)
                        emit_gates(_tc)

            if r == 12:
                for dc in range(NDC):
                    if dc == 0:
                        wait("v", "wa", acc_back_w[0])
                        wait("v", "s", qtr_ev[11, NTC - 1])
                    emit("v", lambda o=qTblk(dc), i=qTblk(dc),
                         sc=a_cc[:, dc:dc + 1]:
                         nc.vector.tensor_single_scalar(o, i, sc, ALU.mult)
                         .then_inc(sm["v"], 1))
                    fold_led[dc] = inc("v")

            for tc in range(NTC):
                if _OPT_PREPASS and r == 0 and tc + 4 < NTC:
                    prepass_tail(tc + 4)
                if _OPT_LADDER and r == 16:
                    if tc + 2 < NTC:
                        emit_p2a(tc + 2)
                    if tc + 1 < NTC:
                        emit_gates(tc + 1)
                gidx = emit_group(r, tc)
                if rd["kind"] == "K":
                    evac_K(r, tc, gidx)
                elif rd["kind"] == "V":
                    evac_V(r, tc, gidx)
                elif rd["kind"] == "Q":
                    evac_Q(r, tc, gidx)
                    if r in (9, 10, 11) and tc >= 1:
                        q_transposes(r, tc - 1, gidx - 1)
                else:
                    if rd["kind"] == "WO":
                        evac_WO(r, tc, gidx)
                    else:
                        evac_EXPBIAS(r, tc, gidx)
            if r in (9, 10, 11):
                q_transposes(r, NTC - 1, len(Gseq) - 1)

            if r == 7:
                emit_a_reduce()
            if r == 8:
                emit_tab(0, tabQ, ("v", ropeK_done[6]))
                emit_tab(1, tabQ, ("v", ropeK_done[7]))
            if r == 11:
                wait("v", "t", mm_led[gk(11, NTC - 1)])
                for tc in range(NTC):
                    for h in range(2):
                        k = 2 * tc + h
                        if k >= 4:
                            wait("y", "v", xh_led[k - 4])
                        elif k >= 2:
                            wait("y", "t", a_tr[0])
                        else:
                            wait("y", "v", h1half[6, h])
                        src = x_h[tc * 128:(tc + 1) * 128,
                                  h * 1024:(h + 1) * 1024]
                        ys = f"y{k % 4}"
                        inc(ys, 16)
                        emit("y", lambda d=xtile(k % 4), s_=src, m_=sm[ys]:
                             nc.sync.dma_start(out=d, in_=s_)
                             .then_inc(m_, 16))
                        x2load[tc, h] = (ys, ct[ys])
                        wait("v", *x2load[tc, h])
                        emit("v", lambda o=x1v(tc, h * 1024, (h + 1) * 1024),
                             i=xtile(k % 4), a=stc(tc, 6), b_=stc(tc, 7):
                             nc.vector.tensor_scalar(o, i, a, b_, ALU.mult,
                                                     ALU.subtract)
                             .then_inc(sm["v"], 1))
                        xh_led[k] = inc("v")

        if _DEBUG:
            wait("y", "v", ct["v"])
            wait("y", "s", ct["s"])
            wait("y", "t", ct["t"])
            for nm, src in (("d_x1", buf_a[:, :]), ("d_qt", buf_b[:, :]),
                            ("d_h2t", buf_u[:, :]), ("d_acc", buf_f[:, :]),
                            ("d_g", g_f[:, :]), ("d_acc2", a_cc[:, :]),
                            ("d_stats", st_f[:, :]), ("d_gt", gT_sb[:, :]),
                            ("d_eb", eb_sb[:, :]), ("d_lf", l_f[:, :])):
                inc("o", 16)
                emit("y", lambda d=dbg[nm], s_=src:
                     nc.sync.dma_start(out=d, in_=s_).then_inc(sm["o"], 16))
        emit("y", lambda v_=ct["o"]: nc.sync.wait_ge(sm["o"], v_))
        for _k, _v in wmax.items():
            assert _v <= ct[_k], f"wait on sem {_k} for {_v} but max count {ct[_k]}"


        @block.tensor
        def _(eng):
            for f in q["t"]:
                f()

        @block.vector
        def _(eng):
            for f in q["v"]:
                f()

        @block.scalar
        def _(eng):
            for f in q["s"]:
                f()

        @block.sync
        def _(eng):
            for f in q["y"]:
                f()

        @block.gpsimd
        def _(eng):
            for f in q["g"]:
                f()

    return nc


_PERM = np.concatenate([np.arange(0, 1024, 2), np.arange(1, 1024, 2),
                        np.arange(1024, 2048)])


def _host_stage(inputs):
    W_in = np.asarray(inputs["L0_attn_W_in"], dtype=np.float32)
    b_in = np.asarray(inputs["L0_attn_b_in"], dtype=np.float32)
    W_out = np.asarray(inputs["L0_attn_W_out"], dtype=np.float32)
    b_out = np.asarray(inputs["L0_attn_b_out"], dtype=np.float32)
    gate_w = np.asarray(inputs["L0_ffn_gate_w"], dtype=np.float32)
    ew = np.ascontiguousarray(np.asarray(inputs["L0_ffn_expert_w"],
                                         dtype=np.float32))
    eb = np.ascontiguousarray(np.asarray(inputs["L0_ffn_expert_b"],
                                         dtype=np.float32))

    Q_, K_, V_ = W_in[0:2048], W_in[2048:4096], W_in[4096:6144]
    bQ, bK, bV = b_in[0:2048], b_in[2048:4096], b_in[4096:6144]
    w1 = np.ascontiguousarray(
        np.concatenate([K_[_PERM], V_[_PERM], Q_[_PERM]], axis=0).T)
    b1 = np.ascontiguousarray(
        np.concatenate([bK[_PERM], bV[_PERM], bQ[_PERM]])[None, :])
    wo = np.ascontiguousarray(W_out.T[_PERM, :])
    bo = np.ascontiguousarray(b_out[None, :])
    gw = np.ascontiguousarray(gate_w.T)

    inv = 1.0 / (10000.0 ** (np.arange(0, 1024, 2, dtype=np.float64) / 1024.0))
    tabs = {}
    for half in range(2):
        pos = np.arange(half * 1024, (half + 1) * 1024, dtype=np.float64)
        ang = pos[:, None] * inv[None, :]
        tabs[half] = (np.cos(ang).astype(np.float32).astype(ml_dtypes.bfloat16),
                      np.sin(ang).astype(np.float32).astype(ml_dtypes.bfloat16))

    eye16 = np.eye(128, dtype=ml_dtypes.bfloat16)
    eye32 = np.eye(128, dtype=np.float32)
    ones16 = np.ones((1, 128), dtype=ml_dtypes.bfloat16)
    konst = np.tile(np.array([[EPS, 1.0]], dtype=np.float32), (128, 1))

    X = np.asarray(inputs["L0_attn_X"], dtype=np.float32)
    in_maps = []
    for c in range(8):
        b, half = c // 2, c % 2
        x = np.ascontiguousarray(X[b, half * 1024:(half + 1) * 1024, :])
        cosT, sinT = tabs[half]
        in_maps.append(dict(x=x, w1=w1, wo=wo, ew=ew, b1=b1, bo=bo, gw=gw,
                            eb=eb, cosT=cosT, sinT=sinT,
                            eye16=eye16, eye32=eye32, ones16=ones16,
                            konst=konst))
    return in_maps


def kernel(**inputs):
    in_maps = _host_stage(inputs)
    nc = build_nc()
    res = run_bass_kernel_spmd(nc, in_maps, core_ids=list(range(8)))
    out = np.empty((B, S, D), dtype=np.float32)
    for c in range(8):
        b, half = c // 2, c % 2
        out[b, half * 1024:(half + 1) * 1024, :] = res.results[c]["out"]
    return out


# version 12 (final)
# speedup vs baseline: 1.1381x; 1.0110x over previous
import numpy as np
import ml_dtypes
from contextlib import ExitStack

import concourse.bass as bass
import concourse.mybir as mybir
from concourse.bass_utils import run_bass_kernel_spmd
from concourse.hw_specs import get_activation_tables

F32 = mybir.dt.float32
BF16 = mybir.dt.bfloat16
AF = mybir.ActivationFunctionType
ALU = mybir.AluOpType
AX = mybir.AxisListType.X

B, S, D = 4, 2048, 2048
TPC = 1024
NTC = 8
NDC = 16
NE = 8
EPS = 1e-5
BIG = 1.0e30

ROUNDS = []
for r in range(16):
    ROUNDS.append(dict(kind=("K", "V", "Q", "WO")[r // 4], piece=r, n=r % 4))
for n in range(4):
    for e in range(NE):
        ROUNDS.append(dict(kind="EXP", piece=16 + n * 8 + e, n=n, e=e))
    ROUNDS.append(dict(kind="BIAS", piece=None, n=n))
NR = len(ROUNDS)
NPIECE = 48
_DEBUG = False
_OPT_PREPASS = True
_OPT_LADDER = True


def build_nc():
    nc = bass.Bass("TRN2", target_bir_lowering=False,
                   detect_race_conditions=False)

    x_h = nc.dram_tensor("x", [TPC, D], F32, kind="ExternalInput").ap()
    w1_h = nc.dram_tensor("w1", [D, 6144], F32, kind="ExternalInput").ap()
    wo_h = nc.dram_tensor("wo", [D, D], F32, kind="ExternalInput").ap()
    ew_h = nc.dram_tensor("ew", [NE, D, D], F32, kind="ExternalInput").ap()
    b1_h = nc.dram_tensor("b1", [1, 6144], F32, kind="ExternalInput").ap()
    bo_h = nc.dram_tensor("bo", [1, D], F32, kind="ExternalInput").ap()
    gw_h = nc.dram_tensor("gw", [D, 8], F32, kind="ExternalInput").ap()
    eb_h = nc.dram_tensor("eb", [NE, D], F32, kind="ExternalInput").ap()
    cos_h = nc.dram_tensor("cosT", [TPC, 512], BF16, kind="ExternalInput").ap()
    sin_h = nc.dram_tensor("sinT", [TPC, 512], BF16, kind="ExternalInput").ap()
    eye16_h = nc.dram_tensor("eye16", [128, 128], BF16, kind="ExternalInput").ap()
    eye32_h = nc.dram_tensor("eye32", [128, 128], F32, kind="ExternalInput").ap()
    ones_h = nc.dram_tensor("ones16", [1, 128], BF16, kind="ExternalInput").ap()
    konst_h = nc.dram_tensor("konst", [128, 2], F32, kind="ExternalInput").ap()
    out_h = nc.dram_tensor("out", [TPC, D], F32, kind="ExternalOutput").ap()

    cc_in = nc.dram_tensor("cc_in", [128, 16], F32)
    cc_out = nc.dram_tensor("cc_out", [128, 16], F32)
    dbg = {}
    if _DEBUG:
        for nm, shp, dt_ in (("d_x1", [128, 16384], BF16),
                             ("d_qt", [128, 16384], BF16),
                             ("d_h2t", [128, 16384], BF16),
                             ("d_acc", [128, 4096], F32),
                             ("d_g", [128, 64], F32),
                             ("d_acc2", [128, 16], F32),
                             ("d_stats", [128, 64], F32),
                             ("d_gt", [8, 1024], BF16),
                             ("d_eb", [8, 2048], BF16),
                             ("d_lf", [128, 64], F32)):
            dbg[nm] = nc.dram_tensor(nm, shp, dt_, kind="ExternalOutput").ap()

    def hbm_piece(p):
        if p < 12:
            src = w1_h[:, p * 512:(p + 1) * 512]
        elif p < 16:
            src = wo_h[:, (p - 12) * 512:(p - 11) * 512]
        else:
            k = p - 16
            n, e = k // 8, k % 8
            src = ew_h[e, :, n * 512:(n + 1) * 512]
        return src.rearrange("(c p) n -> p c n", p=128)

    with ExitStack() as ctx:
        al = ctx.enter_context
        buf_a = al(nc.sbuf_tensor("buf_a", [128, 16384], BF16))
        buf_b = al(nc.sbuf_tensor("buf_b", [128, 16384], BF16))
        buf_u = al(nc.sbuf_tensor("buf_u", [128, 16384], BF16))
        rs = [al(nc.sbuf_tensor(f"ring{i}", [128, 8192], BF16)) for i in range(3)]
        buf_f = al(nc.sbuf_tensor("buf_f", [128, 4096], F32))
        htmp = [al(nc.sbuf_tensor(f"htmp{i}", [128, 2048], BF16)) for i in range(2)]
        sqtr = al(nc.sbuf_tensor("sqtr", [128, 2048], BF16))
        st_f = al(nc.sbuf_tensor("st_f", [128, 64], F32))
        l_f = al(nc.sbuf_tensor("l_f", [128, 64], F32))
        e1_f = al(nc.sbuf_tensor("e1_f", [128, 64], F32))
        gs_f = al(nc.sbuf_tensor("gs_f", [128, 32], F32))
        g_f = al(nc.sbuf_tensor("g_f", [128, 64], F32))
        gbf = al(nc.sbuf_tensor("gbf", [128, 64], BF16))
        gT_sb = al(nc.sbuf_tensor("gT_sb", [8, 1024], BF16))
        l8_sb = al(nc.sbuf_tensor("l8_sb", [8, 128], F32))
        a_fm = al(nc.sbuf_tensor("a_fm", [128, 16], F32))
        a_cc = al(nc.sbuf_tensor("a_cc", [128, 16], F32))
        b1_sb = al(nc.sbuf_tensor("b1_sb", [1, 6144], BF16))
        bo_sb = al(nc.sbuf_tensor("bo_sb", [1, 2048], BF16))
        gw_sb = al(nc.sbuf_tensor("gw_sb", [128, 128], BF16))
        eb_sb = al(nc.sbuf_tensor("eb_sb", [8, 2048], BF16))
        eye16 = al(nc.sbuf_tensor("eye16s", [128, 128], BF16))
        eye32 = al(nc.sbuf_tensor("eye32s", [128, 128], F32))
        ones_sb = al(nc.sbuf_tensor("ones_sb", [1, 128], BF16))
        konst_sb = al(nc.sbuf_tensor("konst_sb", [128, 2], F32))
        ps_m = al(nc.psum_tensor("ps_m", [128, 2048], F32))
        ps_t = [al(nc.psum_tensor(f"ps_t{i}", [128, 1024], BF16)) for i in range(2)]
        ps_x = [al(nc.psum_tensor(f"ps_x{i}", [128, 512], F32)) for i in range(2)]
        sm = {k: al(nc.semaphore(f"sem_{k}"))
              for k in ("t", "v", "s", "o", "c", "yc", "y0", "y1", "y2",
                        "y3", "yt0", "yt1", "w0", "w1", "w2", "wa")}
        block = al(nc.Block())

        def h1T(dc, tc):
            return buf_a[:, dc * 1024 + tc * 128: dc * 1024 + (tc + 1) * 128]

        def x1v(tc, c0, c1):
            return buf_a[:, tc * 2048 + c0: tc * 2048 + c1]

        def kall(tc, c0, c1):
            return buf_b[:, tc * 2048 + c0: tc * 2048 + c1]

        def qT(dc, tc):
            return buf_b[:, dc * 1024 + tc * 128: dc * 1024 + (tc + 1) * 128]

        def qTblk(dc):
            return buf_b[:, dc * 1024:(dc + 1) * 1024]

        def h2T(dc, tc):
            return buf_u[:, dc * 1024 + tc * 128: dc * 1024 + (tc + 1) * 128]

        def tab_cos(j):
            return buf_u[:, j * 1024: j * 1024 + 512]

        def tab_sin(j):
            return buf_u[:, j * 1024 + 512:(j + 1) * 1024]

        def qtileA(tc):
            return buf_u[:, 2048 + tc * 512: 2048 + (tc + 1) * 512]

        def qslot(j):
            return buf_u[:, 6144 + j * 512: 6144 + (j + 1) * 512]

        def ropeout(j):
            return buf_u[:, 7168 + j * 1024: 7168 + (j + 1) * 1024]

        def vslot(j):
            return buf_u[:, 9216 + j * 512: 9216 + (j + 1) * 512]

        prodslot = buf_u[:, 11264:11776]

        def qtmp(j):
            return buf_u[:, 12288 + j * 512: 12288 + (j + 1) * 512]
        apart = buf_f[:, 0:2048]

        def xtile(j):
            base = 2048 + j * 1024 if j < 2 else (j - 2) * 1024
            return buf_f[:, base: base + 1024]

        def accv(tc):
            return buf_f[:, tc * 512:(tc + 1) * 512]

        def stc(tc, k):
            return st_f[:, tc * 8 + k: tc * 8 + k + 1]

        def bank(g):
            return ps_m[:, (g % 4) * 512:((g % 4) + 1) * 512]

        q = {k: [] for k in ("t", "v", "s", "y", "g")}
        ct = {k: 0 for k in ("t", "v", "s", "o", "c", "yc", "y0", "y1",
                             "y2", "y3", "yt0", "yt1", "w0", "w1", "w2",
                             "wa")}
        ENG = {"t": nc.tensor, "v": nc.vector, "s": nc.scalar,
               "y": nc.sync, "g": nc.gpsimd}

        def emit(e_, fn):
            q[e_].append(fn)

        wmax = {}

        def wait(e_, semk, val):
            wmax[semk] = max(wmax.get(semk, 0), val)
            emit(e_, lambda eng=ENG[e_], s_=sm[semk], v_=val: eng.wait_ge(s_, v_))

        def inc(semk, amt=1):
            ct[semk] += amt
            return ct[semk]

        Gseq = []
        mm_led = {}
        evac_led = {}
        tb_led = []
        tb_evac = []
        h1T_rdy = {}
        h1half = {}
        ln1 = {}
        ropeK_done = {}
        ropeQ_done = {}
        tabK = {}
        tabQ = {}
        rn_led = {}
        vprod_led = {}
        prod_cnt = [0]
        qslot_rel = {}
        q9tr = {}
        q9ev = {}
        qtr_ev = {}
        fold_led = {}
        red_done_led = {}
        x1ev = {}
        x2load = {}
        xh_led = {}
        a_tr_led = [0]
        h2b_led = {}
        h2T_rdy = {}
        logit_led = {}
        l8_led = {}
        ltr_led = {}
        lf_led = {}
        exp_led = {}
        gate_led = {}
        gT_led = {}
        out_led = {}
        w_led = {}
        a_dma_y = [None]
        acc_back_w = [None]
        piece_round = {}
        for r_, rd_ in enumerate(ROUNDS):
            if rd_["piece"] is not None:
                piece_round[rd_["piece"]] = r_

        def gk(r, tc):
            return r * 1000 + tc

        tabsets = get_activation_tables(nc.m.arch)
        set_id = list(tabsets).index("natural_log_exp_and_others")
        emit("s", lambda: nc.scalar.add_instruction(
            mybir.InstLoadActFuncSet(
                name=nc.get_next_instruction_name(),
                act_func_set_id=set_id, ins=[], outs=[])))

        nc.const_aps.register(F32, EPS, konst_sb[:, 0:1])
        nc.const_aps.register(F32, 1.0, konst_sb[:, 1:2])
        for dst, src in ((eye16[:, :], eye16_h), (eye32[:, :], eye32_h),
                         (ones_sb[:, :], ones_h), (konst_sb[:, :], konst_h)):
            inc("yc", 16)
            emit("y", lambda d=dst, s_=src:
                 nc.sync.dma_start(out=d, in_=s_).then_inc(sm["yc"], 16))
        const_y = ct["yc"]
        wait("s", "yc", const_y)
        for dst, src in (
                (b1_sb[:, :], b1_h), (bo_sb[:, :], bo_h),
                (gw_sb[:, :].rearrange("p (c n) -> p c n", n=8),
                 gw_h.rearrange("(c p) n -> p c n", p=128)),
                (eb_sb[:, :], eb_h)):
            inc("wa", 16)
            emit("g", lambda d=dst, s_=src:
                 nc.gpsimd.dma_start(out=d, in_=s_).then_inc(sm["wa"], 16))
        const_w = ct["wa"]

        def prepass_head(tc):
            for h in range(2):
                k = 2 * tc + h
                if k >= 4:
                    wait("y", "v", 2 * (tc - 2) + h + 1)
                    wait("y", "s", ln1[tc - 2])
                src = x_h[tc * 128:(tc + 1) * 128, h * 1024:(h + 1) * 1024]
                ys = f"y{k % 4}"
                inc(ys, 16)
                emit("y", lambda d=xtile(k % 4), s_=src, m_=sm[ys]:
                     nc.sync.dma_start(out=d, in_=s_).then_inc(m_, 16))
                wait("s", ys, ct[ys])
                xt = xtile(k % 4)
                emit("s", lambda o=sqtr[:, 0:1024], i=xt, a=stc(tc, 2 * h):
                     nc.scalar.activation(o, i, AF.Copy, accum_out=a))
                emit("s", lambda o=sqtr[:, 1024:2048], i=xt, a=stc(tc, 2 * h + 1):
                     nc.scalar.activation(o, i, AF.Square, accum_out=a))
            M, R, S1, S2 = stc(tc, 4), stc(tc, 5), stc(tc, 6), stc(tc, 7)
            t0, t1 = gs_f[:, 0:1], gs_f[:, 1:2]
            emit("s", lambda o=t0, i=stc(tc, 0), a=stc(tc, 2):
                 nc.scalar.add(o, i, a))
            emit("s", lambda o=M, i=t0: nc.scalar.mul(o, i, 1.0 / 2048.0))
            emit("s", lambda o=t1, i=M: nc.scalar.activation(o, i, AF.Square))
            emit("s", lambda o=t1, i=t1: nc.scalar.mul(o, i, -1.0))
            emit("s", lambda o=t0, i=stc(tc, 1), a=stc(tc, 3):
                 nc.scalar.add(o, i, a))
            emit("s", lambda o=t0, i=t0: nc.scalar.mul(o, i, 1.0 / 2048.0))
            emit("s", lambda o=t0, i=t0, a=t1: nc.scalar.add(o, i, a))
            emit("s", lambda o=t0, i=t0: nc.scalar.add(o, i, EPS))
            emit("s", lambda o=t0, i=t0: nc.scalar.activation(o, i, AF.Ln))
            emit("s", lambda o=R, i=t0:
                 nc.scalar.activation(o, i, AF.Exp, scale=-0.5))
            emit("s", lambda o=S1, i=R: nc.scalar.add(o, i, 1.0))
            emit("s", lambda o=S2, i=M, a=R:
                 nc.scalar.mul(o, i, a).then_inc(sm["s"], 1))
            ln1[tc] = inc("s")

        def prepass_tail(tc):
            M, R = stc(tc, 4), stc(tc, 5)
            if tc >= 2:
                wait("v", "t", tb_led[2 * (tc - 2) + 1])
            wait("v", "s", ln1[tc])
            for h in range(2):
                xt = xtile((2 * tc + h) % 4)
                dst = htmp[tc % 2][:, h * 1024:(h + 1) * 1024]
                emit("v", lambda o=dst, i=xt, m=M, r_=R:
                     nc.vector.tensor_scalar(o, i, m, r_, ALU.subtract, ALU.mult)
                     .then_inc(sm["v"], 1))
                h1half[tc, h] = inc("v")
                assert h1half[tc, h] == 2 * tc + h + 1, "x-load pacing broke"
            if tc == NTC - 1:
                emit("v", lambda: nc.vector.memset(apart, 0.0))

            for b_ in range(2):
                gb = len(tb_led)
                if gb == 0:
                    wait("t", "yc", const_y)
                wait("t", "v", h1half[tc, b_])
                if gb >= 2:
                    wait("t", "s", tb_evac[gb - 2])
                pt = ps_t[gb % 2]
                for j in range(8):
                    src = htmp[tc % 2][:, (b_ * 8 + j) * 128:(b_ * 8 + j + 1) * 128]
                    last = j == 7
                    emit("t", lambda o=pt[:, j * 128:(j + 1) * 128], i=src,
                         id_=eye16[:, :], l=last:
                         (nc.tensor.transpose(o, i, id_).then_inc(sm["t"], 1)
                          if l else nc.tensor.transpose(o, i, id_)))
                tb_led.append(inc("t"))
                wait("s", "t", tb_led[gb])
                dstv = buf_a[:, :].rearrange("p (c x) -> p c x", x=1024)[
                    :, b_ * 8:(b_ + 1) * 8, tc * 128:(tc + 1) * 128]
                srcv = pt[:, :].rearrange("p (c x) -> p c x", x=128)
                emit("s", lambda o=dstv, i=srcv:
                     nc.scalar.copy(o, i).then_inc(sm["s"], 1))
                tb_evac.append(inc("s"))
            h1T_rdy[tc] = tb_evac[-1]

        if _OPT_PREPASS:
            for _tc in range(4):
                prepass_head(_tc)
            for _tc in range(4, NTC):
                prepass_tail(_tc - 4)
                prepass_head(_tc)
        else:
            for _tc in range(NTC):
                prepass_head(_tc)
                prepass_tail(_tc)

        def emit_tab(tc, ledger, hold_sem=None):
            j = tc % 2
            if hold_sem is not None:
                wait("y", hold_sem[0], hold_sem[1])
            ys = f"yt{j}"
            for hsrc, dst in ((cos_h, tab_cos(j)), (sin_h, tab_sin(j))):
                src = hsrc[tc * 128:(tc + 1) * 128, :]
                inc(ys, 16)
                emit("y", lambda d=dst, s_=src, m_=sm[ys]:
                     nc.sync.dma_start(out=d, in_=s_).then_inc(m_, 16))
            ledger[tc] = (ys, ct[ys])

        emit_tab(0, tabK)
        emit_tab(1, tabK)

        gp_state = [0]

        def emit_collective_trigger():
            wait("g", "yc", a_dma_y[0])
            emit("g", lambda: nc.gpsimd.collective_compute(
                "AllReduce", ALU.add,
                replica_groups=[[0, 1], [2, 3], [4, 5], [6, 7]],
                ins=[cc_in.ap()], outs=[cc_out.ap()]).then_inc(sm["c"], 1))
            inc("c")

        def emit_collective_fetch():
            wait("g", "c", ct["c"])
            inc("wa", 16)
            emit("g", lambda: nc.gpsimd.dma_start(out=a_cc[:, :], in_=cc_out[:, :])
                 .then_inc(sm["wa"], 16))
            acc_back_w[0] = ct["wa"]

        def pump_pieces(upto):
            while gp_state[0] <= upto and gp_state[0] < NPIECE:
                p = gp_state[0]
                if p >= 3:
                    rprev = piece_round[p - 3]
                    wait("g", "t", mm_led[gk(rprev, NTC - 1)])
                slot = rs[p % 3]
                ws = f"w{p % 3}"
                dstv = slot[:, :].rearrange("p (c n) -> p c n", n=512)
                inc(ws, 16)
                emit("g", lambda d=dstv, s_=hbm_piece(p), m_=sm[ws]:
                     nc.gpsimd.dma_start(out=d, in_=s_).then_inc(m_, 16))
                w_led[p] = (ws, ct[ws])
                gp_state[0] += 1
                if gp_state[0] == 10:
                    emit_collective_trigger()
                if gp_state[0] == 13:
                    emit_collective_fetch()

        pump_pieces(2)

        def emit_group(r, tc):
            rd = ROUNDS[r]
            G = gk(r, tc)
            gidx = len(Gseq)
            Gseq.append(G)
            bk = bank(gidx)
            if r == 0 and tc == 0:
                wait("t", "wa", const_w)
                wait("t", "yc", const_y)
            if rd["piece"] is not None and tc == 0:
                wait("t", *w_led[rd["piece"]])
            if gidx >= 4:
                semk, cnt = evac_led[Gseq[gidx - 4]]
                wait("t", semk, cnt)
            if r == 0:
                wait("t", "s", h1T_rdy[tc])
            if rd["kind"] == "EXP" and rd["e"] == 0 and rd["n"] == 0:
                wait("t", "s", h2T_rdy[tc])
            if rd["kind"] == "BIAS":
                wait("t", "s", gT_led[tc])
                emit("t", lambda o=bk, l=gT_sb[0:8, tc * 128:(tc + 1) * 128],
                     rh=eb_sb[0:8, rd["n"] * 512:(rd["n"] + 1) * 512]:
                     nc.tensor.matmul(o, l, rh, start=True, stop=True)
                     .then_inc(sm["t"], 1))
                mm_led[G] = inc("t")
                return gidx
            slot = rs[rd["piece"] % 3]
            started = False
            if rd["kind"] in ("K", "V", "Q", "WO"):
                brow = (b1_sb[0:1, rd["piece"] * 512:(rd["piece"] + 1) * 512]
                        if rd["kind"] != "WO"
                        else bo_sb[0:1, rd["n"] * 512:(rd["n"] + 1) * 512])
                emit("t", lambda o=bk, l=ones_sb[0:1, :], rh=brow:
                     nc.tensor.matmul(o, l, rh, start=True, stop=False))
                started = True
            for dc in range(NDC):
                if rd["kind"] == "WO":
                    if r == 12 and tc == 0:
                        wait("t", "v", fold_led[dc])
                    lhs = qT(dc, tc)
                elif rd["kind"] == "EXP":
                    lhs = h2T(dc, tc)
                else:
                    lhs = h1T(dc, tc)
                rh = slot[:, dc * 512:(dc + 1) * 512]
                st_ = (not started) and dc == 0
                sp = dc == NDC - 1
                emit("t", lambda o=bk, l=lhs, rr=rh, a=st_, b_=sp:
                     (nc.tensor.matmul(o, l, rr, start=a, stop=b_)
                      .then_inc(sm["t"], 1) if b_ else
                      nc.tensor.matmul(o, l, rr, start=a, stop=b_)))
            mm_led[G] = inc("t")
            return gidx

        def evac_K(r, tc, gidx):
            G = gk(r, tc)
            bk = bank(gidx)
            wait("s", "t", mm_led[G])
            emit("s", lambda o=kall(tc, r * 512, (r + 1) * 512), i=bk:
                 nc.scalar.copy(o, i).then_inc(sm["s"], 1))
            evac_led[G] = ("s", inc("s"))
            if r == 1:
                wait("v", "s", evac_led[G][1])
                wait("v", *tabK[tc])
                j = tc % 2
                Ev, Ov = kall(tc, 0, 512), kall(tc, 512, 1024)
                tE, tO = qtmp(2 * j), qtmp(2 * j + 1)
                emit("v", lambda o=tE, a=Ev, b_=tab_sin(j): nc.vector.tensor_mul(o, a, b_))
                emit("v", lambda o=tO, a=Ov, b_=tab_sin(j): nc.vector.tensor_mul(o, a, b_))
                emit("v", lambda o=Ev, a=Ev, b_=tab_cos(j): nc.vector.tensor_mul(o, a, b_))
                emit("v", lambda o=Ov, a=Ov, b_=tab_cos(j): nc.vector.tensor_mul(o, a, b_))
                emit("v", lambda o=Ev, a=Ev, b_=tO: nc.vector.tensor_sub(o, a, b_))
                emit("v", lambda o=Ov, a=Ov, b_=tE:
                     nc.vector.tensor_add(o, a, b_).then_inc(sm["v"], 1))
                ropeK_done[tc] = inc("v")
                if tc + 2 < NTC:
                    emit_tab(tc + 2, tabK, ("v", ropeK_done[tc]))
            if r == 3:
                wait("s", "v", ropeK_done[tc])
                ssq, rn = stc(tc, 0), stc(tc, 1)
                emit("s", lambda o=sqtr[:, :], i=kall(tc, 0, 2048), a=ssq:
                     nc.scalar.activation(o, i, AF.Square, accum_out=a))
                emit("s", lambda o=ssq, i=ssq: nc.scalar.activation(o, i, AF.Ln))
                emit("s", lambda o=rn, i=ssq:
                     nc.scalar.activation(o, i, AF.Exp, scale=-0.5)
                     .then_inc(sm["s"], 1))
                rn_led[tc] = inc("s")

        def evac_V(r, tc, gidx):
            G = gk(r, tc)
            bk = bank(gidx)
            vj = vslot(gidx % 2)
            wait("s", "t", mm_led[G])
            if gidx >= 2 and Gseq[gidx - 2] in vprod_led:
                wait("s", "v", vprod_led[Gseq[gidx - 2]])
            emit("s", lambda o=vj, i=bk, sc=stc(tc, 1):
                 nc.scalar.mul(o, i, sc).then_inc(sm["s"], 1))
            evac_led[G] = ("s", inc("s"))
            c0 = (r - 4) * 512
            wait("v", "s", evac_led[G][1])
            emit("v", lambda o=prodslot, a=kall(tc, c0, c0 + 512), b_=vj:
                 nc.vector.tensor_mul(o, a, b_).then_inc(sm["v"], 1))
            pcnt = inc("v")
            wait("v", "v", pcnt)
            emit("v", lambda o=apart[:, c0:c0 + 512], a=apart[:, c0:c0 + 512],
                 b_=prodslot:
                 nc.vector.tensor_add(o, a, b_).then_inc(sm["v"], 1))
            vprod_led[G] = inc("v")
            prod_cnt[0] = vprod_led[G]

        def evac_Q(r, tc, gidx):
            G = gk(r, tc)
            bk = bank(gidx)
            wait("s", "t", mm_led[G])
            if r == 8 and tc == 0:
                wait("s", "v", prod_cnt[0])
            if r == 8:
                dst = qtileA(tc)
            else:
                if gidx >= 2:
                    prevG = Gseq[gidx - 2]
                    if prevG in qslot_rel:
                        semk, cnt = qslot_rel[prevG]
                        wait("s", semk, cnt)
                dst = qslot(gidx % 2)
            emit("s", lambda o=dst, i=bk: nc.scalar.copy(o, i).then_inc(sm["s"], 1))
            evac_led[G] = ("s", inc("s"))
            if r == 9:
                wait("v", "s", evac_led[G][1])
                wait("v", *tabQ[tc])
                if tc >= 2:
                    wait("v", "t", q9tr[tc - 2])
                j = tc % 2
                Ev, Ov = qtileA(tc), dst
                RO = ropeout(tc % 2)
                Eo, Oo = RO[:, 0:512], RO[:, 512:1024]
                sp1, sp2 = qtmp(2 * j), qtmp(2 * j + 1)
                emit("v", lambda o=sp1, a=Ev, b_=tab_sin(j): nc.vector.tensor_mul(o, a, b_))
                emit("v", lambda o=sp2, a=Ov, b_=tab_sin(j): nc.vector.tensor_mul(o, a, b_))
                emit("v", lambda o=Eo, a=Ev, b_=tab_cos(j): nc.vector.tensor_mul(o, a, b_))
                emit("v", lambda o=Oo, a=Ov, b_=tab_cos(j): nc.vector.tensor_mul(o, a, b_))
                emit("v", lambda o=Eo, a=Eo, b_=sp2: nc.vector.tensor_sub(o, a, b_))
                emit("v", lambda o=Oo, a=Oo, b_=sp1:
                     nc.vector.tensor_add(o, a, b_).then_inc(sm["v"], 1))
                ropeQ_done[tc] = inc("v")
                qslot_rel[G] = ("v", ropeQ_done[tc])
                if tc + 2 < NTC:
                    emit_tab(tc + 2, tabQ, ("v", ropeQ_done[tc]))

        def q_transposes(r, tc, gidx_of_group):
            gb = len(tb_led)
            pt = ps_t[gb % 2]
            if gb >= 2:
                wait("t", "s", tb_evac[gb - 2])
            if r == 9:
                wait("t", "v", ropeQ_done[tc])
                srcbase = ropeout(tc % 2)
                nchunks, dc0 = 8, 0
            else:
                G = gk(r, tc)
                wait("t", "s", evac_led[G][1])
                srcbase = qslot(gidx_of_group % 2)
                nchunks, dc0 = 4, 8 + (r - 10) * 4
            for j in range(nchunks):
                src = srcbase[:, j * 128:(j + 1) * 128]
                last = j == nchunks - 1
                emit("t", lambda o=pt[:, j * 128:(j + 1) * 128], i=src,
                     id_=eye16[:, :], l=last:
                     (nc.tensor.transpose(o, i, id_).then_inc(sm["t"], 1)
                      if l else nc.tensor.transpose(o, i, id_)))
            tb_led.append(inc("t"))
            if r == 9:
                q9tr[tc] = tb_led[-1]
            else:
                qslot_rel[gk(r, tc)] = ("t", tb_led[-1])
            wait("s", "t", tb_led[gb])
            dstv = buf_b[:, :].rearrange("p (c x) -> p c x", x=1024)[
                :, dc0:dc0 + nchunks, tc * 128:(tc + 1) * 128]
            srcv = pt[:, 0:nchunks * 128].rearrange("p (c x) -> p c x", x=128)
            emit("s", lambda o=dstv, i=srcv:
                 nc.scalar.copy(o, i).then_inc(sm["s"], 1))
            tb_evac.append(inc("s"))
            qtr_ev[r, tc] = tb_evac[-1]

        def emit_xload(k):
            tc, h = k // 2, k % 2
            if k >= 2:
                wait("y", "v", xh_led[k - 2])
            src = x_h[tc * 128:(tc + 1) * 128, h * 1024:(h + 1) * 1024]
            ys = f"y{k % 2}"
            inc(ys, 16)
            emit("y", lambda d=xtile(k % 2), s_=src, m_=sm[ys]:
                 nc.sync.dma_start(out=d, in_=s_).then_inc(m_, 16))
            x2load[tc, h] = (ys, ct[ys])

        def emit_xh(k):
            tc, h = k // 2, k % 2
            wait("v", *x2load[tc, h])
            emit("v", lambda o=x1v(tc, h * 1024, (h + 1) * 1024),
                 i=xtile(k % 2), a=stc(tc, 6), b_=stc(tc, 7):
                 nc.vector.tensor_scalar(o, i, a, b_, ALU.mult,
                                         ALU.subtract).then_inc(sm["v"], 1))
            xh_led[k] = inc("v")

        def evac_WO(r, tc, gidx):
            G = gk(r, tc)
            bk = bank(gidx)
            n = r - 12
            wait("v", "t", mm_led[G])
            xv = x1v(tc, n * 512, (n + 1) * 512)
            emit("v", lambda o=xv, a=bk, b_=xv:
                 nc.vector.tensor_add(o, a, b_).then_inc(sm["v"], 1))
            evac_led[G] = ("v", inc("v"))
            if r == 15:
                x1ev[tc] = evac_led[G][1]

        def evac_EXPBIAS(r, tc, gidx):
            G = gk(r, tc)
            rd = ROUNDS[r]
            bk = bank(gidx)
            n = rd["n"]
            acc = accv(tc)
            wait("v", "t", mm_led[G])
            if rd["kind"] == "BIAS":
                emit("v", lambda o=acc, a=bk, b_=acc:
                     nc.vector.tensor_add(o, a, b_).then_inc(sm["v"], 1))
                evac_led[G] = ("v", inc("v"))
                wait("y", "v", evac_led[G][1])
                dst = out_h[tc * 128:(tc + 1) * 128, n * 512:(n + 1) * 512]
                inc("o", 16)
                emit("y", lambda d=dst, s_=acc:
                     nc.sync.dma_start(out=d, in_=s_).then_inc(sm["o"], 16))
                out_led[n, tc] = ct["o"]
                return
            e = rd["e"]
            gsc = g_f[:, tc * 8 + e: tc * 8 + e + 1]
            if e == 0:
                wait("v", "v", gate_led[tc])
                if n > 0:
                    wait("v", "o", out_led[n - 1, tc])
                xv = x1v(tc, n * 512, (n + 1) * 512)
                emit("v", lambda o=acc, i=bk, sc=gsc, b_=xv:
                     nc.vector.scalar_tensor_tensor(o, i, sc, b_, ALU.mult,
                                                    ALU.add).then_inc(sm["v"], 1))
            else:
                emit("v", lambda o=acc, i=bk, sc=gsc, a=acc:
                     nc.vector.scalar_tensor_tensor(o, i, sc, a, ALU.mult,
                                                    ALU.add).then_inc(sm["v"], 1))
            evac_led[G] = ("v", inc("v"))

        def emit_a_reduce():
            for wv in range(4):
                px = ps_x[wv % 2]
                if wv >= 2:
                    wait("t", "v", red_done_led[wv - 2])
                for j in range(4):
                    c = wv * 4 + j
                    src = apart[:, c * 128:(c + 1) * 128]
                    last = j == 3
                    emit("t", lambda o=px[:, j * 128:(j + 1) * 128], i=src,
                         id_=eye32[:, :], l=last:
                         (nc.tensor.transpose(o, i, id_).then_inc(sm["t"], 1)
                          if l else nc.tensor.transpose(o, i, id_)))
                aw = inc("t")
                a_tr_led[0] = aw
                wait("v", "t", aw)
                for j in range(4):
                    c = wv * 4 + j
                    last = j == 3
                    emit("v", lambda o=a_fm[:, c:c + 1],
                         i=px[:, j * 128:(j + 1) * 128], l=last:
                         (nc.vector.reduce_sum(o, i, axis=AX).then_inc(sm["v"], 1)
                          if l else nc.vector.reduce_sum(o, i, axis=AX)))
                red_done_led[wv] = inc("v")
            wait("y", "v", red_done_led[3])
            inc("yc", 16)
            emit("y", lambda: nc.sync.dma_start(out=cc_in[:, :], in_=a_fm[:, :])
                 .then_inc(sm["yc"], 16))
            a_dma_y[0] = ct["yc"]

        def emit_p2a(tc):
            wait("s", "v", x1ev[tc])
            M2, R2 = stc(tc, 4), stc(tc, 5)
            sA, sB = stc(tc, 0), stc(tc, 1)
            t0, t1 = gs_f[:, 2:3], gs_f[:, 3:4]
            xv = x1v(tc, 0, 2048)
            emit("s", lambda o=sqtr[:, :], i=xv, a=sA:
                 nc.scalar.activation(o, i, AF.Copy, accum_out=a))
            emit("s", lambda o=sqtr[:, :], i=xv, a=sB:
                 nc.scalar.activation(o, i, AF.Square, accum_out=a))
            emit("s", lambda o=M2, i=sA: nc.scalar.mul(o, i, 1.0 / 2048.0))
            emit("s", lambda o=t1, i=M2: nc.scalar.activation(o, i, AF.Square))
            emit("s", lambda o=t1, i=t1: nc.scalar.mul(o, i, -1.0))
            emit("s", lambda o=t0, i=sB: nc.scalar.mul(o, i, 1.0 / 2048.0))
            emit("s", lambda o=t0, i=t0, a=t1: nc.scalar.add(o, i, a))
            emit("s", lambda o=t0, i=t0: nc.scalar.add(o, i, EPS))
            emit("s", lambda o=t0, i=t0: nc.scalar.activation(o, i, AF.Ln))
            emit("s", lambda o=R2, i=t0:
                 nc.scalar.activation(o, i, AF.Exp, scale=-0.5)
                 .then_inc(sm["s"], 1))
            ln2c = inc("s")
            if tc >= 2:
                wait("v", "t", h2b_led[tc - 2])
            else:
                wait("v", "t", tb_led[13 + 2 * tc])
            wait("v", "s", ln2c)
            emit("v", lambda o=htmp[tc % 2][:, :], i=x1v(tc, 0, 2048), m=M2, r_=R2:
                 nc.vector.tensor_scalar(o, i, m, r_, ALU.subtract, ALU.mult)
                 .then_inc(sm["v"], 1))
            h2ap = inc("v")
            for b_ in range(2):
                gb = len(tb_led)
                pt = ps_t[gb % 2]
                wait("t", "v", h2ap)
                if gb >= 2:
                    wait("t", "s", tb_evac[gb - 2])
                for j in range(8):
                    src = htmp[tc % 2][:, (b_ * 8 + j) * 128:(b_ * 8 + j + 1) * 128]
                    last = j == 7
                    emit("t", lambda o=pt[:, j * 128:(j + 1) * 128], i=src,
                         id_=eye16[:, :], l=last:
                         (nc.tensor.transpose(o, i, id_).then_inc(sm["t"], 1)
                          if l else nc.tensor.transpose(o, i, id_)))
                tb_led.append(inc("t"))
                wait("s", "t", tb_led[gb])
                dstv = buf_u[:, :].rearrange("p (c x) -> p c x", x=1024)[
                    :, b_ * 8:(b_ + 1) * 8, tc * 128:(tc + 1) * 128]
                srcv = pt[:, :].rearrange("p (c x) -> p c x", x=128)
                emit("s", lambda o=dstv, i=srcv:
                     nc.scalar.copy(o, i).then_inc(sm["s"], 1))
                tb_evac.append(inc("s"))
            h2b_led[tc] = tb_led[-1]
            h2T_rdy[tc] = tb_evac[-1]
            if tc == 0:
                wait("t", "v", red_done_led[3])
            else:
                wait("t", "s", l8_led[tc - 1])
            wait("t", "s", h2T_rdy[tc])
            for dc in range(NDC):
                st_, sp = dc == 0, dc == NDC - 1
                emit("t", lambda o=ps_x[0][0:8, 0:128],
                     l=gw_sb[:, dc * 8:(dc + 1) * 8],
                     rh=h2T(dc, tc), a=st_, b_=sp:
                     (nc.tensor.matmul(o, l, rh, start=a, stop=b_)
                      .then_inc(sm["t"], 1) if b_ else
                      nc.tensor.matmul(o, l, rh, start=a, stop=b_)))
            logit_led[tc] = inc("t")
            wait("s", "t", logit_led[tc])
            emit("s", lambda o=l8_sb[:, :], i=ps_x[0][0:8, 0:128]:
                 nc.scalar.copy(o, i).then_inc(sm["s"], 1))
            l8_led[tc] = inc("s")

        def emit_gates(tc):
            wait("t", "s", l8_led[tc])
            if tc == 0:
                wait("t", "v", red_done_led[3])
            else:
                wait("t", "s", lf_led[tc - 1])
            emit("t", lambda o=ps_x[1][:, 0:8], i=l8_sb[:, :], id_=eye32[0:8, 0:8]:
                 nc.tensor.transpose(o, i, id_).then_inc(sm["t"], 1))
            ltr_led[tc] = inc("t")
            wait("s", "t", ltr_led[tc])
            emit("s", lambda o=l_f[:, tc * 8:(tc + 1) * 8], i=ps_x[1][:, 0:8]:
                 nc.scalar.copy(o, i).then_inc(sm["s"], 1))
            lf_led[tc] = inc("s")
            lfv = l_f[:, tc * 8:(tc + 1) * 8]
            m1, m1n, m2 = gs_f[:, 4:5], gs_f[:, 5:6], gs_f[:, 6:7]
            eq = gs_f[:, 8:16]
            l2 = gs_f[:, 16:24]

            def vstep(fn):
                emit("v", lambda f=fn: f().then_inc(sm["v"], 1))
                c = inc("v")
                wait("v", "v", c)
                return c

            wait("v", "s", lf_led[tc])
            vstep(lambda o=m1, i=lfv: nc.vector.reduce_max(o, i, axis=AX))
            vstep(lambda o=m1n, i=lfv:
                  nc.vector.reduce_max(o, i, axis=AX, negate=True))
            vstep(lambda o=eq, i=lfv, sc=m1:
                  nc.vector.tensor_single_scalar(o, i, sc, ALU.is_ge))
            vstep(lambda o=l2, i=eq:
                  nc.vector.tensor_single_scalar(o, i, -BIG, ALU.mult))
            vstep(lambda o=l2, a=l2, b_=lfv: nc.vector.tensor_add(o, a, b_))
            p1 = vstep(lambda o=m2, i=l2: nc.vector.reduce_max(o, i, axis=AX))
            e1 = e1_f[:, tc * 8:(tc + 1) * 8]
            e2m = gs_f[:, 7:8]
            wait("s", "v", p1)
            emit("s", lambda o=e1, i=lfv, b_=m1n:
                 nc.scalar.activation(o, i, AF.Exp, bias=b_))
            emit("s", lambda o=e2m, i=m2, b_=m1n:
                 nc.scalar.activation(o, i, AF.Exp, bias=b_).then_inc(sm["s"], 1))
            exp_led[tc] = inc("s")
            ge2 = gs_f[:, 8:16]
            gu = gs_f[:, 16:24]
            den = gs_f[:, 4:5]
            rden = gs_f[:, 5:6]
            gv = g_f[:, tc * 8:(tc + 1) * 8]
            wait("v", "s", exp_led[tc])
            vstep(lambda o=ge2, i=lfv, sc=m2:
                  nc.vector.tensor_single_scalar(o, i, sc, ALU.is_ge))
            vstep(lambda o=gu, a=e1, b_=ge2: nc.vector.tensor_mul(o, a, b_))
            vstep(lambda o=den, i=e2m: nc.vector.tensor_scalar_add(o, i, 1.0))
            vstep(lambda o=rden, i=den: nc.vector.reciprocal(o, i))
            gate_led[tc] = vstep(lambda o=gv, i=gu, sc=rden:
                                 nc.vector.tensor_single_scalar(o, i, sc, ALU.mult))
            wait("t", "v", gate_led[tc])
            if tc >= 1:
                wait("t", "s", gT_led[tc - 1])
            emit("t", lambda o=ps_x[1][0:8, 128:256], i=g_f[:, tc * 8:(tc + 1) * 8],
                 id_=eye32[:, :]:
                 nc.tensor.transpose(o, i, id_).then_inc(sm["t"], 1))
            gttr = inc("t")
            wait("s", "t", gttr)
            emit("s", lambda o=gT_sb[0:8, tc * 128:(tc + 1) * 128],
                 i=ps_x[1][0:8, 128:256]:
                 nc.scalar.copy(o, i).then_inc(sm["s"], 1))
            gT_led[tc] = inc("s")

        for r, rd in enumerate(ROUNDS):
            if rd["piece"] is not None:
                pump_pieces(rd["piece"])
            if r == 16:
                if _OPT_LADDER:
                    emit_p2a(0)
                    emit_p2a(1)
                    emit_gates(0)
                else:
                    for _tc in range(NTC):
                        emit_p2a(_tc)
                        emit_gates(_tc)

            if r == 12:
                for dc in range(NDC):
                    if dc == 0:
                        wait("v", "wa", acc_back_w[0])
                        wait("v", "s", qtr_ev[11, NTC - 1])
                    emit("v", lambda o=qTblk(dc), i=qTblk(dc),
                         sc=a_cc[:, dc:dc + 1]:
                         nc.vector.tensor_single_scalar(o, i, sc, ALU.mult)
                         .then_inc(sm["v"], 1))
                    fold_led[dc] = inc("v")

            for tc in range(NTC):
                if _OPT_PREPASS and r == 0 and tc + 4 < NTC:
                    prepass_tail(tc + 4)
                if _OPT_LADDER and r == 16:
                    if tc + 2 < NTC:
                        emit_p2a(tc + 2)
                    if tc + 1 < NTC:
                        emit_gates(tc + 1)
                gidx = emit_group(r, tc)
                if rd["kind"] == "K":
                    evac_K(r, tc, gidx)
                elif rd["kind"] == "V":
                    evac_V(r, tc, gidx)
                elif rd["kind"] == "Q":
                    evac_Q(r, tc, gidx)
                    if r in (9, 10, 11) and tc >= 1:
                        q_transposes(r, tc - 1, gidx - 1)
                else:
                    if rd["kind"] == "WO":
                        if r == 12:
                            if tc == 0:
                                wait("v", "t", mm_led[gk(11, NTC - 1)])
                            emit_xh(2 * tc)
                            emit_xh(2 * tc + 1)
                            if tc + 1 < NTC:
                                emit_xload(2 * tc + 2)
                                emit_xload(2 * tc + 3)
                            wait("v", "v", xh_led[2 * tc + 1])
                        evac_WO(r, tc, gidx)
                    else:
                        evac_EXPBIAS(r, tc, gidx)
            if r in (9, 10, 11):
                q_transposes(r, NTC - 1, len(Gseq) - 1)

            if r == 7:
                emit_a_reduce()
            if r == 8:
                emit_tab(0, tabQ, ("v", ropeK_done[6]))
                emit_tab(1, tabQ, ("v", ropeK_done[7]))
            if r == 11:
                emit_xload(0)
                emit_xload(1)

        if _DEBUG:
            wait("y", "v", ct["v"])
            wait("y", "s", ct["s"])
            wait("y", "t", ct["t"])
            for nm, src in (("d_x1", buf_a[:, :]), ("d_qt", buf_b[:, :]),
                            ("d_h2t", buf_u[:, :]), ("d_acc", buf_f[:, :]),
                            ("d_g", g_f[:, :]), ("d_acc2", a_cc[:, :]),
                            ("d_stats", st_f[:, :]), ("d_gt", gT_sb[:, :]),
                            ("d_eb", eb_sb[:, :]), ("d_lf", l_f[:, :])):
                inc("o", 16)
                emit("y", lambda d=dbg[nm], s_=src:
                     nc.sync.dma_start(out=d, in_=s_).then_inc(sm["o"], 16))
        emit("y", lambda v_=ct["o"]: nc.sync.wait_ge(sm["o"], v_))
        for _k, _v in wmax.items():
            assert _v <= ct[_k], f"wait on sem {_k} for {_v} but max count {ct[_k]}"


        @block.tensor
        def _(eng):
            for f in q["t"]:
                f()

        @block.vector
        def _(eng):
            for f in q["v"]:
                f()

        @block.scalar
        def _(eng):
            for f in q["s"]:
                f()

        @block.sync
        def _(eng):
            for f in q["y"]:
                f()

        @block.gpsimd
        def _(eng):
            for f in q["g"]:
                f()

    return nc


_PERM = np.concatenate([np.arange(0, 1024, 2), np.arange(1, 1024, 2),
                        np.arange(1024, 2048)])


def _host_stage(inputs):
    W_in = np.asarray(inputs["L0_attn_W_in"], dtype=np.float32)
    b_in = np.asarray(inputs["L0_attn_b_in"], dtype=np.float32)
    W_out = np.asarray(inputs["L0_attn_W_out"], dtype=np.float32)
    b_out = np.asarray(inputs["L0_attn_b_out"], dtype=np.float32)
    gate_w = np.asarray(inputs["L0_ffn_gate_w"], dtype=np.float32)
    ew = np.ascontiguousarray(np.asarray(inputs["L0_ffn_expert_w"],
                                         dtype=np.float32))
    eb = np.ascontiguousarray(np.asarray(inputs["L0_ffn_expert_b"],
                                         dtype=np.float32))

    Q_, K_, V_ = W_in[0:2048], W_in[2048:4096], W_in[4096:6144]
    bQ, bK, bV = b_in[0:2048], b_in[2048:4096], b_in[4096:6144]
    w1 = np.ascontiguousarray(
        np.concatenate([K_[_PERM], V_[_PERM], Q_[_PERM]], axis=0).T)
    b1 = np.ascontiguousarray(
        np.concatenate([bK[_PERM], bV[_PERM], bQ[_PERM]])[None, :])
    wo = np.ascontiguousarray(W_out.T[_PERM, :])
    bo = np.ascontiguousarray(b_out[None, :])
    gw = np.ascontiguousarray(gate_w.T)

    inv = 1.0 / (10000.0 ** (np.arange(0, 1024, 2, dtype=np.float64) / 1024.0))
    tabs = {}
    for half in range(2):
        pos = np.arange(half * 1024, (half + 1) * 1024, dtype=np.float64)
        ang = pos[:, None] * inv[None, :]
        tabs[half] = (np.cos(ang).astype(np.float32).astype(ml_dtypes.bfloat16),
                      np.sin(ang).astype(np.float32).astype(ml_dtypes.bfloat16))

    eye16 = np.eye(128, dtype=ml_dtypes.bfloat16)
    eye32 = np.eye(128, dtype=np.float32)
    ones16 = np.ones((1, 128), dtype=ml_dtypes.bfloat16)
    konst = np.tile(np.array([[EPS, 1.0]], dtype=np.float32), (128, 1))

    X = np.asarray(inputs["L0_attn_X"], dtype=np.float32)
    in_maps = []
    for c in range(8):
        b, half = c // 2, c % 2
        x = np.ascontiguousarray(X[b, half * 1024:(half + 1) * 1024, :])
        cosT, sinT = tabs[half]
        in_maps.append(dict(x=x, w1=w1, wo=wo, ew=ew, b1=b1, bo=bo, gw=gw,
                            eb=eb, cosT=cosT, sinT=sinT,
                            eye16=eye16, eye32=eye32, ones16=ones16,
                            konst=konst))
    return in_maps


def kernel(**inputs):
    in_maps = _host_stage(inputs)
    nc = build_nc()
    res = run_bass_kernel_spmd(nc, in_maps, core_ids=list(range(8)))
    out = np.empty((B, S, D), dtype=np.float32)
    for c in range(8):
        b, half = c // 2, c % 2
        out[b, half * 1024:(half + 1) * 1024, :] = res.results[c]["out"]
    return out


# version 14
# speedup vs baseline: 1.1408x; 1.0024x over previous
import numpy as np
import ml_dtypes
from contextlib import ExitStack

import concourse.bass as bass
import concourse.mybir as mybir
from concourse.bass_utils import run_bass_kernel_spmd
from concourse.hw_specs import get_activation_tables

F32 = mybir.dt.float32
BF16 = mybir.dt.bfloat16
AF = mybir.ActivationFunctionType
ALU = mybir.AluOpType
AX = mybir.AxisListType.X

B, S, D = 4, 2048, 2048
TPC = 1024
NTC = 8
NDC = 16
NE = 8
EPS = 1e-5
BIG = 1.0e30

ROUNDS = []
for r in range(16):
    ROUNDS.append(dict(kind=("K", "V", "Q", "WO")[r // 4], piece=r, n=r % 4))
for n in range(4):
    for e in range(NE):
        ROUNDS.append(dict(kind="EXP", piece=16 + n * 8 + e, n=n, e=e))
    ROUNDS.append(dict(kind="BIAS", piece=None, n=n))
NR = len(ROUNDS)
NPIECE = 48
_DEBUG = False
_OPT_PREPASS = True
_OPT_LADDER = True


def build_nc():
    nc = bass.Bass("TRN2", target_bir_lowering=False,
                   detect_race_conditions=False)

    x_h = nc.dram_tensor("x", [TPC, D], F32, kind="ExternalInput").ap()
    w1_h = nc.dram_tensor("w1", [D, 6144], F32, kind="ExternalInput").ap()
    wo_h = nc.dram_tensor("wo", [D, D], F32, kind="ExternalInput").ap()
    ew_h = nc.dram_tensor("ew", [NE, D, D], F32, kind="ExternalInput").ap()
    b1_h = nc.dram_tensor("b1", [1, 6144], F32, kind="ExternalInput").ap()
    bo_h = nc.dram_tensor("bo", [1, D], F32, kind="ExternalInput").ap()
    gw_h = nc.dram_tensor("gw", [D, 8], F32, kind="ExternalInput").ap()
    eb_h = nc.dram_tensor("eb", [NE, D], F32, kind="ExternalInput").ap()
    cos_h = nc.dram_tensor("cosT", [TPC, 512], BF16, kind="ExternalInput").ap()
    sin_h = nc.dram_tensor("sinT", [TPC, 512], BF16, kind="ExternalInput").ap()
    eye16_h = nc.dram_tensor("eye16", [128, 128], BF16, kind="ExternalInput").ap()
    eye32_h = nc.dram_tensor("eye32", [128, 128], F32, kind="ExternalInput").ap()
    ones_h = nc.dram_tensor("ones16", [1, 128], BF16, kind="ExternalInput").ap()
    konst_h = nc.dram_tensor("konst", [128, 2], F32, kind="ExternalInput").ap()
    out_h = nc.dram_tensor("out", [TPC, D], F32, kind="ExternalOutput").ap()

    cc_in = nc.dram_tensor("cc_in", [128, 16], F32)
    cc_out = nc.dram_tensor("cc_out", [128, 16], F32)
    dbg = {}
    if _DEBUG:
        for nm, shp, dt_ in (("d_x1", [128, 16384], BF16),
                             ("d_qt", [128, 16384], BF16),
                             ("d_h2t", [128, 16384], BF16),
                             ("d_acc", [128, 4096], F32),
                             ("d_g", [128, 64], F32),
                             ("d_acc2", [128, 16], F32),
                             ("d_stats", [128, 64], F32),
                             ("d_gt", [8, 1024], BF16),
                             ("d_eb", [8, 2048], BF16),
                             ("d_lf", [128, 64], F32)):
            dbg[nm] = nc.dram_tensor(nm, shp, dt_, kind="ExternalOutput").ap()

    def hbm_piece(p):
        if p < 12:
            src = w1_h[:, p * 512:(p + 1) * 512]
        elif p < 16:
            src = wo_h[:, (p - 12) * 512:(p - 11) * 512]
        else:
            k = p - 16
            n, e = k // 8, k % 8
            src = ew_h[e, :, n * 512:(n + 1) * 512]
        return src.rearrange("(c p) n -> p c n", p=128)

    with ExitStack() as ctx:
        al = ctx.enter_context
        buf_a = al(nc.sbuf_tensor("buf_a", [128, 16384], BF16))
        buf_b = al(nc.sbuf_tensor("buf_b", [128, 16384], BF16))
        buf_u = al(nc.sbuf_tensor("buf_u", [128, 16384], BF16))
        rs = [al(nc.sbuf_tensor(f"ring{i}", [128, 8192], BF16)) for i in range(3)]
        buf_f = al(nc.sbuf_tensor("buf_f", [128, 4096], F32))
        htmp = [al(nc.sbuf_tensor(f"htmp{i}", [128, 2048], BF16)) for i in range(2)]
        sqtr = al(nc.sbuf_tensor("sqtr", [128, 2048], BF16))
        st_f = al(nc.sbuf_tensor("st_f", [128, 64], F32))
        l_f = al(nc.sbuf_tensor("l_f", [128, 64], F32))
        e1_f = al(nc.sbuf_tensor("e1_f", [128, 64], F32))
        gs_f = al(nc.sbuf_tensor("gs_f", [128, 32], F32))
        g_f = al(nc.sbuf_tensor("g_f", [128, 64], F32))
        gbf = al(nc.sbuf_tensor("gbf", [128, 64], BF16))
        gT_sb = al(nc.sbuf_tensor("gT_sb", [8, 1024], BF16))
        l8_sb = al(nc.sbuf_tensor("l8_sb", [8, 128], F32))
        a_fm = al(nc.sbuf_tensor("a_fm", [128, 16], F32))
        a_cc = al(nc.sbuf_tensor("a_cc", [128, 16], F32))
        b1_sb = al(nc.sbuf_tensor("b1_sb", [1, 6144], BF16))
        bo_sb = al(nc.sbuf_tensor("bo_sb", [1, 2048], BF16))
        gw_sb = al(nc.sbuf_tensor("gw_sb", [128, 128], BF16))
        eb_sb = al(nc.sbuf_tensor("eb_sb", [8, 2048], BF16))
        eye16 = al(nc.sbuf_tensor("eye16s", [128, 128], BF16))
        eye32 = al(nc.sbuf_tensor("eye32s", [128, 128], F32))
        ones_sb = al(nc.sbuf_tensor("ones_sb", [1, 128], BF16))
        konst_sb = al(nc.sbuf_tensor("konst_sb", [128, 2], F32))
        ps_m = al(nc.psum_tensor("ps_m", [128, 2048], F32))
        ps_t = [al(nc.psum_tensor(f"ps_t{i}", [128, 1024], BF16)) for i in range(2)]
        ps_x = [al(nc.psum_tensor(f"ps_x{i}", [128, 512], F32)) for i in range(2)]
        sm = {k: al(nc.semaphore(f"sem_{k}"))
              for k in ("t", "v", "s", "o", "c", "yc", "y0", "y1", "y2",
                        "y3", "yt0", "yt1", "w0", "w1", "w2", "wa")}
        block = al(nc.Block())

        def h1T(dc, tc):
            return buf_a[:, dc * 1024 + tc * 128: dc * 1024 + (tc + 1) * 128]

        def x1v(tc, c0, c1):
            return buf_a[:, tc * 2048 + c0: tc * 2048 + c1]

        def kall(tc, c0, c1):
            return buf_b[:, tc * 2048 + c0: tc * 2048 + c1]

        def qT(dc, tc):
            return buf_b[:, dc * 1024 + tc * 128: dc * 1024 + (tc + 1) * 128]

        def qTblk(dc):
            return buf_b[:, dc * 1024:(dc + 1) * 1024]

        def h2T(dc, tc):
            return buf_u[:, dc * 1024 + tc * 128: dc * 1024 + (tc + 1) * 128]

        def tab_cos(j):
            return buf_u[:, j * 1024: j * 1024 + 512]

        def tab_sin(j):
            return buf_u[:, j * 1024 + 512:(j + 1) * 1024]

        def qtileA(tc):
            return buf_u[:, 2048 + tc * 512: 2048 + (tc + 1) * 512]

        def qslot(j):
            return buf_u[:, 6144 + j * 512: 6144 + (j + 1) * 512]

        def ropeout(j):
            return buf_u[:, 7168 + j * 1024: 7168 + (j + 1) * 1024]

        def vslot(j):
            return buf_u[:, 9216 + j * 512: 9216 + (j + 1) * 512]

        prodslot = buf_u[:, 11264:11776]

        def qtmp(j):
            return buf_u[:, 12288 + j * 512: 12288 + (j + 1) * 512]
        apart = buf_f[:, 0:2048]

        def xtile(j):
            base = 2048 + j * 1024 if j < 2 else (j - 2) * 1024
            return buf_f[:, base: base + 1024]

        def accv(tc):
            return buf_f[:, tc * 512:(tc + 1) * 512]

        def stc(tc, k):
            return st_f[:, tc * 8 + k: tc * 8 + k + 1]

        def bank(g):
            return ps_m[:, (g % 4) * 512:((g % 4) + 1) * 512]

        q = {k: [] for k in ("t", "v", "s", "y", "g")}
        ct = {k: 0 for k in ("t", "v", "s", "o", "c", "yc", "y0", "y1",
                             "y2", "y3", "yt0", "yt1", "w0", "w1", "w2",
                             "wa")}
        ENG = {"t": nc.tensor, "v": nc.vector, "s": nc.scalar,
               "y": nc.sync, "g": nc.gpsimd}

        def emit(e_, fn):
            q[e_].append(fn)

        wmax = {}

        def wait(e_, semk, val):
            wmax[semk] = max(wmax.get(semk, 0), val)
            emit(e_, lambda eng=ENG[e_], s_=sm[semk], v_=val: eng.wait_ge(s_, v_))

        def inc(semk, amt=1):
            ct[semk] += amt
            return ct[semk]

        Gseq = []
        mm_led = {}
        evac_led = {}
        tb_led = []
        tb_evac = []
        h1T_rdy = {}
        h1half = {}
        ln1 = {}
        ropeK_done = {}
        ropeQ_done = {}
        tabK = {}
        tabQ = {}
        rn_led = {}
        vprod_led = {}
        prod_cnt = [0]
        qslot_rel = {}
        q9tr = {}
        q9ev = {}
        qtr_ev = {}
        fold_led = {}
        red_done_led = {}
        x1ev = {}
        x2load = {}
        xh_led = {}
        a_tr_led = [0]
        h2b_led = {}
        h2T_rdy = {}
        logit_led = {}
        l8_led = {}
        ltr_led = {}
        lf_led = {}
        exp_led = {}
        gate_led = {}
        gT_led = {}
        out_led = {}
        w_led = {}
        a_dma_y = [None]
        acc_back_w = [None]
        piece_round = {}
        for r_, rd_ in enumerate(ROUNDS):
            if rd_["piece"] is not None:
                piece_round[rd_["piece"]] = r_

        def gk(r, tc):
            return r * 1000 + tc

        tabsets = get_activation_tables(nc.m.arch)
        set_id = list(tabsets).index("natural_log_exp_and_others")
        emit("s", lambda: nc.scalar.add_instruction(
            mybir.InstLoadActFuncSet(
                name=nc.get_next_instruction_name(),
                act_func_set_id=set_id, ins=[], outs=[])))

        nc.const_aps.register(F32, EPS, konst_sb[:, 0:1])
        nc.const_aps.register(F32, 1.0, konst_sb[:, 1:2])
        for dst, src in ((eye16[:, :], eye16_h), (eye32[:, :], eye32_h),
                         (ones_sb[:, :], ones_h), (konst_sb[:, :], konst_h)):
            inc("yc", 16)
            emit("y", lambda d=dst, s_=src:
                 nc.sync.dma_start(out=d, in_=s_).then_inc(sm["yc"], 16))
        const_y = ct["yc"]
        wait("s", "yc", const_y)
        for dst, src in (
                (b1_sb[:, :], b1_h), (bo_sb[:, :], bo_h),
                (gw_sb[:, :].rearrange("p (c n) -> p c n", n=8),
                 gw_h.rearrange("(c p) n -> p c n", p=128)),
                (eb_sb[:, :], eb_h)):
            inc("wa", 16)
            emit("g", lambda d=dst, s_=src:
                 nc.gpsimd.dma_start(out=d, in_=s_).then_inc(sm["wa"], 16))
        const_w = ct["wa"]

        def prepass_head(tc):
            for h in range(2):
                k = 2 * tc + h
                if k >= 4:
                    wait("y", "v", 2 * (tc - 2) + h + 1)
                    wait("y", "s", ln1[tc - 2])
                src = x_h[tc * 128:(tc + 1) * 128, h * 1024:(h + 1) * 1024]
                ys = f"y{k % 4}"
                inc(ys, 16)
                emit("y", lambda d=xtile(k % 4), s_=src, m_=sm[ys]:
                     nc.sync.dma_start(out=d, in_=s_).then_inc(m_, 16))
                wait("s", ys, ct[ys])
                xt = xtile(k % 4)
                emit("s", lambda o=sqtr[:, 0:1024], i=xt, a=stc(tc, 2 * h):
                     nc.scalar.activation(o, i, AF.Copy, accum_out=a))
                emit("s", lambda o=sqtr[:, 1024:2048], i=xt, a=stc(tc, 2 * h + 1):
                     nc.scalar.activation(o, i, AF.Square, accum_out=a))
            M, R, S1, S2 = stc(tc, 4), stc(tc, 5), stc(tc, 6), stc(tc, 7)
            t0, t1 = gs_f[:, 0:1], gs_f[:, 1:2]
            emit("s", lambda o=t0, i=stc(tc, 0), a=stc(tc, 2):
                 nc.scalar.add(o, i, a))
            emit("s", lambda o=M, i=t0: nc.scalar.mul(o, i, 1.0 / 2048.0))
            emit("s", lambda o=t1, i=M: nc.scalar.activation(o, i, AF.Square))
            emit("s", lambda o=t1, i=t1: nc.scalar.mul(o, i, -1.0))
            emit("s", lambda o=t0, i=stc(tc, 1), a=stc(tc, 3):
                 nc.scalar.add(o, i, a))
            emit("s", lambda o=t0, i=t0: nc.scalar.mul(o, i, 1.0 / 2048.0))
            emit("s", lambda o=t0, i=t0, a=t1: nc.scalar.add(o, i, a))
            emit("s", lambda o=t0, i=t0: nc.scalar.add(o, i, EPS))
            emit("s", lambda o=t0, i=t0: nc.scalar.activation(o, i, AF.Ln))
            emit("s", lambda o=R, i=t0:
                 nc.scalar.activation(o, i, AF.Exp, scale=-0.5))
            emit("s", lambda o=S1, i=R: nc.scalar.add(o, i, 1.0))
            emit("s", lambda o=S2, i=M, a=R:
                 nc.scalar.mul(o, i, a).then_inc(sm["s"], 1))
            ln1[tc] = inc("s")

        def prepass_tail(tc):
            M, R = stc(tc, 4), stc(tc, 5)
            if tc >= 2:
                wait("v", "t", tb_led[2 * (tc - 2) + 1])
            wait("v", "s", ln1[tc])
            for h in range(2):
                xt = xtile((2 * tc + h) % 4)
                dst = htmp[tc % 2][:, h * 1024:(h + 1) * 1024]
                emit("v", lambda o=dst, i=xt, m=M, r_=R:
                     nc.vector.tensor_scalar(o, i, m, r_, ALU.subtract, ALU.mult)
                     .then_inc(sm["v"], 1))
                h1half[tc, h] = inc("v")
                assert h1half[tc, h] == 2 * tc + h + 1, "x-load pacing broke"
            if tc == NTC - 1:
                emit("v", lambda: nc.vector.memset(apart, 0.0))

            for b_ in range(2):
                gb = len(tb_led)
                if gb == 0:
                    wait("t", "yc", const_y)
                wait("t", "v", h1half[tc, b_])
                if gb >= 2:
                    wait("t", "s", tb_evac[gb - 2])
                pt = ps_t[gb % 2]
                for j in range(8):
                    src = htmp[tc % 2][:, (b_ * 8 + j) * 128:(b_ * 8 + j + 1) * 128]
                    last = j == 7
                    emit("t", lambda o=pt[:, j * 128:(j + 1) * 128], i=src,
                         id_=eye16[:, :], l=last:
                         (nc.tensor.transpose(o, i, id_).then_inc(sm["t"], 1)
                          if l else nc.tensor.transpose(o, i, id_)))
                tb_led.append(inc("t"))
                wait("s", "t", tb_led[gb])
                dstv = buf_a[:, :].rearrange("p (c x) -> p c x", x=1024)[
                    :, b_ * 8:(b_ + 1) * 8, tc * 128:(tc + 1) * 128]
                srcv = pt[:, :].rearrange("p (c x) -> p c x", x=128)
                emit("s", lambda o=dstv, i=srcv:
                     nc.scalar.copy(o, i).then_inc(sm["s"], 1))
                tb_evac.append(inc("s"))
            h1T_rdy[tc] = tb_evac[-1]

        if _OPT_PREPASS:
            for _tc in range(4):
                prepass_head(_tc)
            for _tc in range(4, NTC):
                prepass_tail(_tc - 4)
                prepass_head(_tc)
        else:
            for _tc in range(NTC):
                prepass_head(_tc)
                prepass_tail(_tc)

        def emit_tab(tc, ledger, hold_sem=None):
            j = tc % 2
            if hold_sem is not None:
                wait("y", hold_sem[0], hold_sem[1])
            ys = f"yt{j}"
            for hsrc, dst in ((cos_h, tab_cos(j)), (sin_h, tab_sin(j))):
                src = hsrc[tc * 128:(tc + 1) * 128, :]
                inc(ys, 16)
                emit("y", lambda d=dst, s_=src, m_=sm[ys]:
                     nc.sync.dma_start(out=d, in_=s_).then_inc(m_, 16))
            ledger[tc] = (ys, ct[ys])

        emit_tab(0, tabK)
        emit_tab(1, tabK)

        gp_state = [0]

        def emit_collective_trigger():
            wait("g", "yc", a_dma_y[0])
            emit("g", lambda: nc.gpsimd.collective_compute(
                "AllReduce", ALU.add,
                replica_groups=[[0, 1], [2, 3], [4, 5], [6, 7]],
                ins=[cc_in.ap()], outs=[cc_out.ap()]).then_inc(sm["c"], 1))
            inc("c")

        def emit_collective_fetch():
            wait("g", "c", ct["c"])
            inc("wa", 16)
            emit("g", lambda: nc.gpsimd.dma_start(out=a_cc[:, :], in_=cc_out[:, :])
                 .then_inc(sm["wa"], 16))
            acc_back_w[0] = ct["wa"]

        def pump_pieces(upto):
            while gp_state[0] <= upto and gp_state[0] < NPIECE:
                p = gp_state[0]
                if p >= 3:
                    rprev = piece_round[p - 3]
                    wait("g", "t", mm_led[gk(rprev, NTC - 1)])
                slot = rs[p % 3]
                ws = f"w{p % 3}"
                dstv = slot[:, :].rearrange("p (c n) -> p c n", n=512)
                inc(ws, 16)
                emit("g", lambda d=dstv, s_=hbm_piece(p), m_=sm[ws]:
                     nc.gpsimd.dma_start(out=d, in_=s_).then_inc(m_, 16))
                w_led[p] = (ws, ct[ws])
                gp_state[0] += 1
                if gp_state[0] == 10:
                    emit_collective_trigger()
                if gp_state[0] == 13:
                    emit_collective_fetch()

        pump_pieces(2)

        def emit_group(r, tc):
            rd = ROUNDS[r]
            G = gk(r, tc)
            gidx = len(Gseq)
            Gseq.append(G)
            bk = bank(gidx)
            if r == 0 and tc == 0:
                wait("t", "wa", const_w)
                wait("t", "yc", const_y)
            if rd["piece"] is not None and tc == 0:
                wait("t", *w_led[rd["piece"]])
            if gidx >= 4:
                semk, cnt = evac_led[Gseq[gidx - 4]]
                wait("t", semk, cnt)
            if r == 0:
                wait("t", "s", h1T_rdy[tc])
            if rd["kind"] == "EXP" and rd["e"] == 0 and rd["n"] == 0:
                wait("t", "s", h2T_rdy[tc])
            if rd["kind"] == "BIAS":
                wait("t", "s", gT_led[tc])
                emit("t", lambda o=bk, l=gT_sb[0:8, tc * 128:(tc + 1) * 128],
                     rh=eb_sb[0:8, rd["n"] * 512:(rd["n"] + 1) * 512]:
                     nc.tensor.matmul(o, l, rh, start=True, stop=True)
                     .then_inc(sm["t"], 1))
                mm_led[G] = inc("t")
                return gidx
            slot = rs[rd["piece"] % 3]
            started = False
            if rd["kind"] in ("K", "V", "Q", "WO"):
                brow = (b1_sb[0:1, rd["piece"] * 512:(rd["piece"] + 1) * 512]
                        if rd["kind"] != "WO"
                        else bo_sb[0:1, rd["n"] * 512:(rd["n"] + 1) * 512])
                emit("t", lambda o=bk, l=ones_sb[0:1, :], rh=brow:
                     nc.tensor.matmul(o, l, rh, start=True, stop=False))
                started = True
            for dc in range(NDC):
                if rd["kind"] == "WO":
                    if r == 12 and tc == 0:
                        wait("t", "v", fold_led[dc])
                    lhs = qT(dc, tc)
                elif rd["kind"] == "EXP":
                    lhs = h2T(dc, tc)
                else:
                    lhs = h1T(dc, tc)
                rh = slot[:, dc * 512:(dc + 1) * 512]
                st_ = (not started) and dc == 0
                sp = dc == NDC - 1
                emit("t", lambda o=bk, l=lhs, rr=rh, a=st_, b_=sp:
                     (nc.tensor.matmul(o, l, rr, start=a, stop=b_)
                      .then_inc(sm["t"], 1) if b_ else
                      nc.tensor.matmul(o, l, rr, start=a, stop=b_)))
            mm_led[G] = inc("t")
            return gidx

        def evac_K(r, tc, gidx):
            G = gk(r, tc)
            bk = bank(gidx)
            wait("s", "t", mm_led[G])
            emit("s", lambda o=kall(tc, r * 512, (r + 1) * 512), i=bk:
                 nc.scalar.copy(o, i).then_inc(sm["s"], 1))
            evac_led[G] = ("s", inc("s"))
            if r == 1:
                wait("v", "s", evac_led[G][1])
                wait("v", *tabK[tc])
                j = tc % 2
                Ev, Ov = kall(tc, 0, 512), kall(tc, 512, 1024)
                tE, tO = qtmp(2 * j), qtmp(2 * j + 1)
                emit("v", lambda o=tE, a=Ev, b_=tab_sin(j): nc.vector.tensor_mul(o, a, b_))
                emit("v", lambda o=tO, a=Ov, b_=tab_sin(j): nc.vector.tensor_mul(o, a, b_))
                emit("v", lambda o=Ev, a=Ev, b_=tab_cos(j): nc.vector.tensor_mul(o, a, b_))
                emit("v", lambda o=Ov, a=Ov, b_=tab_cos(j): nc.vector.tensor_mul(o, a, b_))
                emit("v", lambda o=Ev, a=Ev, b_=tO: nc.vector.tensor_sub(o, a, b_))
                emit("v", lambda o=Ov, a=Ov, b_=tE:
                     nc.vector.tensor_add(o, a, b_).then_inc(sm["v"], 1))
                ropeK_done[tc] = inc("v")
                if tc + 2 < NTC:
                    emit_tab(tc + 2, tabK, ("v", ropeK_done[tc]))
            if r == 3:
                wait("s", "v", ropeK_done[tc])
                ssq, rn = stc(tc, 0), stc(tc, 1)
                emit("s", lambda o=sqtr[:, :], i=kall(tc, 0, 2048), a=ssq:
                     nc.scalar.activation(o, i, AF.Square, accum_out=a))
                emit("s", lambda o=ssq, i=ssq: nc.scalar.activation(o, i, AF.Ln))
                emit("s", lambda o=rn, i=ssq:
                     nc.scalar.activation(o, i, AF.Exp, scale=-0.5)
                     .then_inc(sm["s"], 1))
                rn_led[tc] = inc("s")

        def evac_V(r, tc, gidx):
            G = gk(r, tc)
            bk = bank(gidx)
            vj = vslot(gidx % 2)
            wait("s", "t", mm_led[G])
            if gidx >= 2 and Gseq[gidx - 2] in vprod_led:
                wait("s", "v", vprod_led[Gseq[gidx - 2]])
            emit("s", lambda o=vj, i=bk, sc=stc(tc, 1):
                 nc.scalar.mul(o, i, sc).then_inc(sm["s"], 1))
            evac_led[G] = ("s", inc("s"))
            c0 = (r - 4) * 512
            wait("v", "s", evac_led[G][1])
            emit("v", lambda o=prodslot, a=kall(tc, c0, c0 + 512), b_=vj:
                 nc.vector.tensor_mul(o, a, b_).then_inc(sm["v"], 1))
            pcnt = inc("v")
            wait("v", "v", pcnt)
            emit("v", lambda o=apart[:, c0:c0 + 512], a=apart[:, c0:c0 + 512],
                 b_=prodslot:
                 nc.vector.tensor_add(o, a, b_).then_inc(sm["v"], 1))
            vprod_led[G] = inc("v")
            prod_cnt[0] = vprod_led[G]

        def evac_Q(r, tc, gidx):
            G = gk(r, tc)
            bk = bank(gidx)
            wait("s", "t", mm_led[G])
            if r == 8 and tc == 0:
                wait("s", "v", prod_cnt[0])
            if r == 8:
                dst = qtileA(tc)
            else:
                if gidx >= 2:
                    prevG = Gseq[gidx - 2]
                    if prevG in qslot_rel:
                        semk, cnt = qslot_rel[prevG]
                        wait("s", semk, cnt)
                dst = qslot(gidx % 2)
            emit("s", lambda o=dst, i=bk: nc.scalar.copy(o, i).then_inc(sm["s"], 1))
            evac_led[G] = ("s", inc("s"))
            if r == 9:
                wait("v", "s", evac_led[G][1])
                wait("v", *tabQ[tc])
                if tc >= 2:
                    wait("v", "t", q9tr[tc - 2])
                j = tc % 2
                Ev, Ov = qtileA(tc), dst
                RO = ropeout(tc % 2)
                Eo, Oo = RO[:, 0:512], RO[:, 512:1024]
                sp1, sp2 = qtmp(2 * j), qtmp(2 * j + 1)
                emit("v", lambda o=sp1, a=Ev, b_=tab_sin(j): nc.vector.tensor_mul(o, a, b_))
                emit("v", lambda o=sp2, a=Ov, b_=tab_sin(j): nc.vector.tensor_mul(o, a, b_))
                emit("v", lambda o=Eo, a=Ev, b_=tab_cos(j): nc.vector.tensor_mul(o, a, b_))
                emit("v", lambda o=Oo, a=Ov, b_=tab_cos(j): nc.vector.tensor_mul(o, a, b_))
                emit("v", lambda o=Eo, a=Eo, b_=sp2: nc.vector.tensor_sub(o, a, b_))
                emit("v", lambda o=Oo, a=Oo, b_=sp1:
                     nc.vector.tensor_add(o, a, b_).then_inc(sm["v"], 1))
                ropeQ_done[tc] = inc("v")
                qslot_rel[G] = ("v", ropeQ_done[tc])
                if tc + 2 < NTC:
                    emit_tab(tc + 2, tabQ, ("v", ropeQ_done[tc]))

        def q_transposes(r, tc, gidx_of_group):
            gb = len(tb_led)
            pt = ps_t[gb % 2]
            if gb >= 2:
                wait("t", "s", tb_evac[gb - 2])
            if r == 9:
                wait("t", "v", ropeQ_done[tc])
                srcbase = ropeout(tc % 2)
                nchunks, dc0 = 8, 0
            else:
                G = gk(r, tc)
                wait("t", "s", evac_led[G][1])
                srcbase = qslot(gidx_of_group % 2)
                nchunks, dc0 = 4, 8 + (r - 10) * 4
            for j in range(nchunks):
                src = srcbase[:, j * 128:(j + 1) * 128]
                last = j == nchunks - 1
                emit("t", lambda o=pt[:, j * 128:(j + 1) * 128], i=src,
                     id_=eye16[:, :], l=last:
                     (nc.tensor.transpose(o, i, id_).then_inc(sm["t"], 1)
                      if l else nc.tensor.transpose(o, i, id_)))
            tb_led.append(inc("t"))
            if r == 9:
                q9tr[tc] = tb_led[-1]
            else:
                qslot_rel[gk(r, tc)] = ("t", tb_led[-1])
            wait("s", "t", tb_led[gb])
            dstv = buf_b[:, :].rearrange("p (c x) -> p c x", x=1024)[
                :, dc0:dc0 + nchunks, tc * 128:(tc + 1) * 128]
            srcv = pt[:, 0:nchunks * 128].rearrange("p (c x) -> p c x", x=128)
            emit("s", lambda o=dstv, i=srcv:
                 nc.scalar.copy(o, i).then_inc(sm["s"], 1))
            tb_evac.append(inc("s"))
            qtr_ev[r, tc] = tb_evac[-1]

        def emit_xload(k):
            tc, h = k // 2, k % 2
            if k >= 2:
                wait("y", "v", xh_led[k - 2])
            src = x_h[tc * 128:(tc + 1) * 128, h * 1024:(h + 1) * 1024]
            ys = f"y{k % 2}"
            inc(ys, 16)
            emit("y", lambda d=xtile(k % 2), s_=src, m_=sm[ys]:
                 nc.sync.dma_start(out=d, in_=s_).then_inc(m_, 16))
            x2load[tc, h] = (ys, ct[ys])

        def emit_xh(k):
            tc, h = k // 2, k % 2
            wait("v", *x2load[tc, h])
            emit("v", lambda o=x1v(tc, h * 1024, (h + 1) * 1024),
                 i=xtile(k % 2), a=stc(tc, 6), b_=stc(tc, 7):
                 nc.vector.tensor_scalar(o, i, a, b_, ALU.mult,
                                         ALU.subtract).then_inc(sm["v"], 1))
            xh_led[k] = inc("v")

        def evac_WO(r, tc, gidx):
            G = gk(r, tc)
            bk = bank(gidx)
            n = r - 12
            wait("v", "t", mm_led[G])
            xv = x1v(tc, n * 512, (n + 1) * 512)
            emit("v", lambda o=xv, a=bk, b_=xv:
                 nc.vector.tensor_add(o, a, b_).then_inc(sm["v"], 1))
            evac_led[G] = ("v", inc("v"))
            if r == 15:
                x1ev[tc] = evac_led[G][1]

        def evac_EXPBIAS(r, tc, gidx):
            G = gk(r, tc)
            rd = ROUNDS[r]
            bk = bank(gidx)
            n = rd["n"]
            acc = accv(tc)
            wait("v", "t", mm_led[G])
            if rd["kind"] == "BIAS":
                emit("v", lambda o=acc, a=bk, b_=acc:
                     nc.vector.tensor_add(o, a, b_).then_inc(sm["v"], 1))
                evac_led[G] = ("v", inc("v"))
                wait("y", "v", evac_led[G][1])
                dst = out_h[tc * 128:(tc + 1) * 128, n * 512:(n + 1) * 512]
                inc("o", 16)
                emit("y", lambda d=dst, s_=acc:
                     nc.sync.dma_start(out=d, in_=s_).then_inc(sm["o"], 16))
                out_led[n, tc] = ct["o"]
                return
            e = rd["e"]
            gsc = g_f[:, tc * 8 + e: tc * 8 + e + 1]
            if e == 0:
                wait("v", "v", gate_led[tc])
                if n > 0:
                    wait("v", "o", out_led[n - 1, tc])
                xv = x1v(tc, n * 512, (n + 1) * 512)
                emit("v", lambda o=acc, i=bk, sc=gsc, b_=xv:
                     nc.vector.scalar_tensor_tensor(o, i, sc, b_, ALU.mult,
                                                    ALU.add).then_inc(sm["v"], 1))
            else:
                emit("v", lambda o=acc, i=bk, sc=gsc, a=acc:
                     nc.vector.scalar_tensor_tensor(o, i, sc, a, ALU.mult,
                                                    ALU.add).then_inc(sm["v"], 1))
            evac_led[G] = ("v", inc("v"))

        def emit_a_reduce():
            for wv in range(4):
                px = ps_x[wv % 2]
                if wv >= 2:
                    wait("t", "v", red_done_led[wv - 2])
                for j in range(4):
                    c = wv * 4 + j
                    src = apart[:, c * 128:(c + 1) * 128]
                    last = j == 3
                    emit("t", lambda o=px[:, j * 128:(j + 1) * 128], i=src,
                         id_=eye32[:, :], l=last:
                         (nc.tensor.transpose(o, i, id_).then_inc(sm["t"], 1)
                          if l else nc.tensor.transpose(o, i, id_)))
                aw = inc("t")
                a_tr_led[0] = aw
                wait("v", "t", aw)
                for j in range(4):
                    c = wv * 4 + j
                    last = j == 3
                    emit("v", lambda o=a_fm[:, c:c + 1],
                         i=px[:, j * 128:(j + 1) * 128], l=last:
                         (nc.vector.reduce_sum(o, i, axis=AX).then_inc(sm["v"], 1)
                          if l else nc.vector.reduce_sum(o, i, axis=AX)))
                red_done_led[wv] = inc("v")
            wait("y", "v", red_done_led[3])
            inc("yc", 16)
            emit("y", lambda: nc.sync.dma_start(out=cc_in[:, :], in_=a_fm[:, :])
                 .then_inc(sm["yc"], 16))
            a_dma_y[0] = ct["yc"]

        def emit_p2a(tc):
            wait("s", "v", x1ev[tc])
            M2, R2 = stc(tc, 4), stc(tc, 5)
            sA, sB = stc(tc, 0), stc(tc, 1)
            t0, t1 = gs_f[:, 2:3], gs_f[:, 3:4]
            xv = x1v(tc, 0, 2048)
            emit("s", lambda o=sqtr[:, :], i=xv, a=sA:
                 nc.scalar.activation(o, i, AF.Copy, accum_out=a))
            emit("s", lambda o=sqtr[:, :], i=xv, a=sB:
                 nc.scalar.activation(o, i, AF.Square, accum_out=a))
            emit("s", lambda o=M2, i=sA: nc.scalar.mul(o, i, 1.0 / 2048.0))
            emit("s", lambda o=t1, i=M2: nc.scalar.activation(o, i, AF.Square))
            emit("s", lambda o=t1, i=t1: nc.scalar.mul(o, i, -1.0))
            emit("s", lambda o=t0, i=sB: nc.scalar.mul(o, i, 1.0 / 2048.0))
            emit("s", lambda o=t0, i=t0, a=t1: nc.scalar.add(o, i, a))
            emit("s", lambda o=t0, i=t0: nc.scalar.add(o, i, EPS))
            emit("s", lambda o=t0, i=t0: nc.scalar.activation(o, i, AF.Ln))
            emit("s", lambda o=R2, i=t0:
                 nc.scalar.activation(o, i, AF.Exp, scale=-0.5)
                 .then_inc(sm["s"], 1))
            ln2c = inc("s")
            if tc >= 2:
                wait("v", "t", h2b_led[tc - 2])
            else:
                wait("v", "t", tb_led[13 + 2 * tc])
            wait("v", "s", ln2c)
            emit("v", lambda o=htmp[tc % 2][:, :], i=x1v(tc, 0, 2048), m=M2, r_=R2:
                 nc.vector.tensor_scalar(o, i, m, r_, ALU.subtract, ALU.mult)
                 .then_inc(sm["v"], 1))
            h2ap = inc("v")
            for b_ in range(2):
                gb = len(tb_led)
                pt = ps_t[gb % 2]
                wait("t", "v", h2ap)
                if gb >= 2:
                    wait("t", "s", tb_evac[gb - 2])
                for j in range(8):
                    src = htmp[tc % 2][:, (b_ * 8 + j) * 128:(b_ * 8 + j + 1) * 128]
                    last = j == 7
                    emit("t", lambda o=pt[:, j * 128:(j + 1) * 128], i=src,
                         id_=eye16[:, :], l=last:
                         (nc.tensor.transpose(o, i, id_).then_inc(sm["t"], 1)
                          if l else nc.tensor.transpose(o, i, id_)))
                tb_led.append(inc("t"))
                wait("s", "t", tb_led[gb])
                dstv = buf_u[:, :].rearrange("p (c x) -> p c x", x=1024)[
                    :, b_ * 8:(b_ + 1) * 8, tc * 128:(tc + 1) * 128]
                srcv = pt[:, :].rearrange("p (c x) -> p c x", x=128)
                emit("s", lambda o=dstv, i=srcv:
                     nc.scalar.copy(o, i).then_inc(sm["s"], 1))
                tb_evac.append(inc("s"))
            h2b_led[tc] = tb_led[-1]
            h2T_rdy[tc] = tb_evac[-1]
            if tc == 0:
                wait("t", "v", red_done_led[3])
            else:
                wait("t", "s", l8_led[tc - 1])
            wait("t", "s", h2T_rdy[tc])
            for dc in range(NDC):
                st_, sp = dc == 0, dc == NDC - 1
                emit("t", lambda o=ps_x[0][0:8, 0:128],
                     l=gw_sb[:, dc * 8:(dc + 1) * 8],
                     rh=h2T(dc, tc), a=st_, b_=sp:
                     (nc.tensor.matmul(o, l, rh, start=a, stop=b_)
                      .then_inc(sm["t"], 1) if b_ else
                      nc.tensor.matmul(o, l, rh, start=a, stop=b_)))
            logit_led[tc] = inc("t")
            wait("s", "t", logit_led[tc])
            emit("s", lambda o=l8_sb[:, :], i=ps_x[0][0:8, 0:128]:
                 nc.scalar.copy(o, i).then_inc(sm["s"], 1))
            l8_led[tc] = inc("s")

        def emit_gates(tc):
            wait("t", "s", l8_led[tc])
            if tc == 0:
                wait("t", "v", red_done_led[3])
            else:
                wait("t", "s", lf_led[tc - 1])
            emit("t", lambda o=ps_x[1][:, 0:8], i=l8_sb[:, :], id_=eye32[0:8, 0:8]:
                 nc.tensor.transpose(o, i, id_).then_inc(sm["t"], 1))
            ltr_led[tc] = inc("t")
            wait("s", "t", ltr_led[tc])
            emit("s", lambda o=l_f[:, tc * 8:(tc + 1) * 8], i=ps_x[1][:, 0:8]:
                 nc.scalar.copy(o, i).then_inc(sm["s"], 1))
            lf_led[tc] = inc("s")
            lfv = l_f[:, tc * 8:(tc + 1) * 8]
            m1, m1n, m2 = gs_f[:, 4:5], gs_f[:, 5:6], gs_f[:, 6:7]
            eq = gs_f[:, 8:16]
            l2 = gs_f[:, 16:24]

            def vstep(fn):
                emit("v", lambda f=fn: f().then_inc(sm["v"], 1))
                c = inc("v")
                wait("v", "v", c)
                return c

            wait("v", "s", lf_led[tc])
            vstep(lambda o=m1, i=lfv: nc.vector.reduce_max(o, i, axis=AX))
            vstep(lambda o=m1n, i=lfv:
                  nc.vector.reduce_max(o, i, axis=AX, negate=True))
            vstep(lambda o=eq, i=lfv, sc=m1:
                  nc.vector.tensor_single_scalar(o, i, sc, ALU.is_ge))
            vstep(lambda o=l2, i=eq:
                  nc.vector.tensor_single_scalar(o, i, -BIG, ALU.mult))
            vstep(lambda o=l2, a=l2, b_=lfv: nc.vector.tensor_add(o, a, b_))
            p1 = vstep(lambda o=m2, i=l2: nc.vector.reduce_max(o, i, axis=AX))
            e1 = e1_f[:, tc * 8:(tc + 1) * 8]
            e2m = gs_f[:, 7:8]
            wait("s", "v", p1)
            emit("s", lambda o=e1, i=lfv, b_=m1n:
                 nc.scalar.activation(o, i, AF.Exp, bias=b_))
            emit("s", lambda o=e2m, i=m2, b_=m1n:
                 nc.scalar.activation(o, i, AF.Exp, bias=b_).then_inc(sm["s"], 1))
            exp_led[tc] = inc("s")
            ge2 = gs_f[:, 8:16]
            gu = gs_f[:, 16:24]
            den = gs_f[:, 4:5]
            rden = gs_f[:, 5:6]
            gv = g_f[:, tc * 8:(tc + 1) * 8]
            wait("v", "s", exp_led[tc])
            vstep(lambda o=ge2, i=lfv, sc=m2:
                  nc.vector.tensor_single_scalar(o, i, sc, ALU.is_ge))
            vstep(lambda o=gu, a=e1, b_=ge2: nc.vector.tensor_mul(o, a, b_))
            vstep(lambda o=den, i=e2m: nc.vector.tensor_scalar_add(o, i, 1.0))
            vstep(lambda o=rden, i=den: nc.vector.reciprocal(o, i))
            gate_led[tc] = vstep(lambda o=gv, i=gu, sc=rden:
                                 nc.vector.tensor_single_scalar(o, i, sc, ALU.mult))
            wait("t", "v", gate_led[tc])
            if tc >= 1:
                wait("t", "s", gT_led[tc - 1])
            emit("t", lambda o=ps_x[1][0:8, 128:256], i=g_f[:, tc * 8:(tc + 1) * 8],
                 id_=eye32[:, :]:
                 nc.tensor.transpose(o, i, id_).then_inc(sm["t"], 1))
            gttr = inc("t")
            wait("s", "t", gttr)
            emit("s", lambda o=gT_sb[0:8, tc * 128:(tc + 1) * 128],
                 i=ps_x[1][0:8, 128:256]:
                 nc.scalar.copy(o, i).then_inc(sm["s"], 1))
            gT_led[tc] = inc("s")

        for r, rd in enumerate(ROUNDS):
            if rd["piece"] is not None:
                pump_pieces(rd["piece"])
            if r == 16:
                if _OPT_LADDER:
                    emit_p2a(0)
                    emit_p2a(1)
                    emit_gates(0)
                else:
                    for _tc in range(NTC):
                        emit_p2a(_tc)
                        emit_gates(_tc)

            if r == 12:
                for dc in range(NDC):
                    if dc == 0:
                        wait("v", "wa", acc_back_w[0])
                        wait("v", "s", qtr_ev[9, NTC - 1])
                    elif dc == 8:
                        wait("v", "s", qtr_ev[10, NTC - 1])
                    elif dc == 12:
                        wait("v", "s", qtr_ev[11, NTC - 1])
                    emit("v", lambda o=qTblk(dc), i=qTblk(dc),
                         sc=a_cc[:, dc:dc + 1]:
                         nc.vector.tensor_single_scalar(o, i, sc, ALU.mult)
                         .then_inc(sm["v"], 1))
                    fold_led[dc] = inc("v")

            for tc in range(NTC):
                if _OPT_PREPASS and r == 0 and tc + 4 < NTC:
                    prepass_tail(tc + 4)
                if _OPT_LADDER and r == 16:
                    if tc + 2 < NTC:
                        emit_p2a(tc + 2)
                    if tc + 1 < NTC:
                        emit_gates(tc + 1)
                gidx = emit_group(r, tc)
                if rd["kind"] == "K":
                    evac_K(r, tc, gidx)
                elif rd["kind"] == "V":
                    evac_V(r, tc, gidx)
                elif rd["kind"] == "Q":
                    evac_Q(r, tc, gidx)
                    if r in (9, 10, 11) and tc >= 1:
                        q_transposes(r, tc - 1, gidx - 1)
                else:
                    if rd["kind"] == "WO":
                        if r == 12:
                            if tc == 0:
                                wait("v", "t", mm_led[gk(11, NTC - 1)])
                            emit_xh(2 * tc)
                            emit_xh(2 * tc + 1)
                            if tc + 1 < NTC:
                                emit_xload(2 * tc + 2)
                                emit_xload(2 * tc + 3)
                            wait("v", "v", xh_led[2 * tc + 1])
                        evac_WO(r, tc, gidx)
                    else:
                        evac_EXPBIAS(r, tc, gidx)
            if r in (9, 10, 11):
                q_transposes(r, NTC - 1, len(Gseq) - 1)

            if r == 7:
                emit_a_reduce()
            if r == 8:
                emit_tab(0, tabQ, ("v", ropeK_done[6]))
                emit_tab(1, tabQ, ("v", ropeK_done[7]))
            if r == 11:
                emit_xload(0)
                emit_xload(1)

        if _DEBUG:
            wait("y", "v", ct["v"])
            wait("y", "s", ct["s"])
            wait("y", "t", ct["t"])
            for nm, src in (("d_x1", buf_a[:, :]), ("d_qt", buf_b[:, :]),
                            ("d_h2t", buf_u[:, :]), ("d_acc", buf_f[:, :]),
                            ("d_g", g_f[:, :]), ("d_acc2", a_cc[:, :]),
                            ("d_stats", st_f[:, :]), ("d_gt", gT_sb[:, :]),
                            ("d_eb", eb_sb[:, :]), ("d_lf", l_f[:, :])):
                inc("o", 16)
                emit("y", lambda d=dbg[nm], s_=src:
                     nc.sync.dma_start(out=d, in_=s_).then_inc(sm["o"], 16))
        emit("y", lambda v_=ct["o"]: nc.sync.wait_ge(sm["o"], v_))
        for _k, _v in wmax.items():
            assert _v <= ct[_k], f"wait on sem {_k} for {_v} but max count {ct[_k]}"


        @block.tensor
        def _(eng):
            for f in q["t"]:
                f()

        @block.vector
        def _(eng):
            for f in q["v"]:
                f()

        @block.scalar
        def _(eng):
            for f in q["s"]:
                f()

        @block.sync
        def _(eng):
            for f in q["y"]:
                f()

        @block.gpsimd
        def _(eng):
            for f in q["g"]:
                f()

    return nc


_PERM = np.concatenate([np.arange(0, 1024, 2), np.arange(1, 1024, 2),
                        np.arange(1024, 2048)])


def _host_stage(inputs):
    W_in = np.asarray(inputs["L0_attn_W_in"], dtype=np.float32)
    b_in = np.asarray(inputs["L0_attn_b_in"], dtype=np.float32)
    W_out = np.asarray(inputs["L0_attn_W_out"], dtype=np.float32)
    b_out = np.asarray(inputs["L0_attn_b_out"], dtype=np.float32)
    gate_w = np.asarray(inputs["L0_ffn_gate_w"], dtype=np.float32)
    ew = np.ascontiguousarray(np.asarray(inputs["L0_ffn_expert_w"],
                                         dtype=np.float32))
    eb = np.ascontiguousarray(np.asarray(inputs["L0_ffn_expert_b"],
                                         dtype=np.float32))

    Q_, K_, V_ = W_in[0:2048], W_in[2048:4096], W_in[4096:6144]
    bQ, bK, bV = b_in[0:2048], b_in[2048:4096], b_in[4096:6144]
    w1 = np.ascontiguousarray(
        np.concatenate([K_[_PERM], V_[_PERM], Q_[_PERM]], axis=0).T)
    b1 = np.ascontiguousarray(
        np.concatenate([bK[_PERM], bV[_PERM], bQ[_PERM]])[None, :])
    wo = np.ascontiguousarray(W_out.T[_PERM, :])
    bo = np.ascontiguousarray(b_out[None, :])
    gw = np.ascontiguousarray(gate_w.T)

    inv = 1.0 / (10000.0 ** (np.arange(0, 1024, 2, dtype=np.float64) / 1024.0))
    tabs = {}
    for half in range(2):
        pos = np.arange(half * 1024, (half + 1) * 1024, dtype=np.float64)
        ang = pos[:, None] * inv[None, :]
        tabs[half] = (np.cos(ang).astype(np.float32).astype(ml_dtypes.bfloat16),
                      np.sin(ang).astype(np.float32).astype(ml_dtypes.bfloat16))

    eye16 = np.eye(128, dtype=ml_dtypes.bfloat16)
    eye32 = np.eye(128, dtype=np.float32)
    ones16 = np.ones((1, 128), dtype=ml_dtypes.bfloat16)
    konst = np.tile(np.array([[EPS, 1.0]], dtype=np.float32), (128, 1))

    X = np.asarray(inputs["L0_attn_X"], dtype=np.float32)
    in_maps = []
    for c in range(8):
        b, half = c // 2, c % 2
        x = np.ascontiguousarray(X[b, half * 1024:(half + 1) * 1024, :])
        cosT, sinT = tabs[half]
        in_maps.append(dict(x=x, w1=w1, wo=wo, ew=ew, b1=b1, bo=bo, gw=gw,
                            eb=eb, cosT=cosT, sinT=sinT,
                            eye16=eye16, eye32=eye32, ones16=ones16,
                            konst=konst))
    return in_maps


def kernel(**inputs):
    in_maps = _host_stage(inputs)
    nc = build_nc()
    res = run_bass_kernel_spmd(nc, in_maps, core_ids=list(range(8)))
    out = np.empty((B, S, D), dtype=np.float32)
    for c in range(8):
        b, half = c // 2, c % 2
        out[b, half * 1024:(half + 1) * 1024, :] = res.results[c]["out"]
    return out
